# version 06
# baseline (speedup 1.0000x reference)
import contextlib
import numpy as np
import ml_dtypes
import concourse.bass as bass
import concourse.mybir as mybir

F32 = mybir.dt.float32
BF16 = mybir.dt.bfloat16
AF = mybir.ActivationFunctionType
ALU = mybir.AluOpType
AX = mybir.AxisListType

NEG_BIG = -1.0e30
MAGIC = 12582912.0
ENGS = ("sync", "scalar", "vector", "tensor", "gpsimd")


class Sem:
    __slots__ = ("h", "v", "name")

    def __init__(self, nc, name):
        self.h = nc.alloc_semaphore(name=name)
        self.v = 0
        self.name = name


class Ctx:
    def __init__(self, nc):
        self.nc = nc
        self.q = {e: [] for e in ENGS}
        self._n = 0
        self._pool = {}

    def sem(self, name=None):
        self._n += 1
        if name is None:
            return Sem(self.nc, f"s{self._n}")
        key = name[2:] if len(name) > 2 else name
        if key not in self._pool:
            self._pool[key] = Sem(self.nc, key)
        return self._pool[key]

    def on(self, eng, fn):
        self.q[eng].append(fn)

    def wait(self, eng, sem, val=None):
        v = sem.v if val is None else val
        self.on(eng, lambda e, h=sem.h, v=v: e.wait_ge(h, v))

    def waits(self, eng, pairs):
        for (s, v) in pairs:
            self.wait(eng, s, v)

    def run(self, eng, fn, inc=None, k=1):
        if inc is None:
            self.on(eng, fn)
            return None
        inc.v += k
        self.on(eng, lambda e, f=fn, h=inc.h, kk=k: f(e).then_inc(h, kk))
        return inc.v

    def dma(self, eng, out, in_, sem, **kw):
        sem.v += 16
        self.on(eng, lambda e, o=out, i=in_, h=sem.h, kw=kw: e.dma_start(out=o, in_=i, **kw).then_inc(h, 16))
        return sem.v

    def emit(self):
        nc = self.nc
        with nc.Block() as block:
            @block.sync
            def _(e):
                for f in self.q["sync"]:
                    f(e)

            @block.scalar
            def _(e):
                for f in self.q["scalar"]:
                    f(e)

            @block.vector
            def _(e):
                for f in self.q["vector"]:
                    f(e)

            @block.tensor
            def _(e):
                for f in self.q["tensor"]:
                    f(e)

            @block.gpsimd
            def _(e):
                for f in self.q["gpsimd"]:
                    f(e)




def ecopy(e, out, in_):
    if hasattr(e, "tensor_copy"):
        return e.tensor_copy(out=out, in_=in_)
    return e.copy(out=out, in_=in_)


class PBank:

    def __init__(self, ctx, ap, name):
        self.ctx = ctx
        self.ap = ap
        self.wr = ctx.sem(name + "_wr")
        self.ev = ctx.sem(name + "_ev")

    def group(self, mms, evacs, evac_eng="scalar", extra_waits=(), evac_extra_waits=()):
        ctx = self.ctx
        ctx.wait("tensor", self.ev, self.ev.v)
        for (s, v) in extra_waits:
            ctx.wait("tensor", s, v)
        for i, fn in enumerate(mms):
            ctx.run("tensor", fn, inc=self.wr if i == len(mms) - 1 else None)
        wr_t = self.wr.v
        evs = [f if isinstance(f, tuple) else (evac_eng, f) for f in evacs]
        seen = set()
        for j, (eng, fn) in enumerate(evs):
            if eng not in seen:
                ctx.wait(eng, self.wr, wr_t)
                for (s, v) in evac_extra_waits:
                    ctx.wait(eng, s, v)
                seen.add(eng)
            ctx.run(eng, fn, inc=self.ev if j == len(evs) - 1 else None)
        return self.ev.v



def make_consts():
    bf = ml_dtypes.bfloat16
    i = np.arange(128)
    same = (i[:, None] // 4) == (i[None, :] // 4)
    diag = i[:, None] == i[None, :]
    cons = {
        "c_ident": np.eye(128, dtype=bf),
        "c_ones1x128": np.ones((1, 128), dtype=bf),
        "c_ones128x1": np.ones((128, 1), dtype=bf),
        "c_onesf32": np.ones((1, 128), dtype=np.float32),
        "c_onescolf32": np.ones((128, 1), dtype=np.float32),
        "c_identf32": np.eye(128, dtype=np.float32),
        "c_epscols": np.tile(np.array([[1e-5, 1e-6, 1e-8, 1e-12]], dtype=np.float32), (128, 1)),
        "c_maskseg": np.where(same & ~diag, 0.0, NEG_BIG).astype(np.float32),
        "c_bo2": (np.arange(128)[:, None] // 64 == np.arange(2)[None, :]).astype(np.float32),
        "c_bc2": (np.arange(128)[None, :] // 64 == np.arange(2)[:, None]).astype(np.float32),
        "c_sel16": np.stack([(np.arange(16)[:, None] == n).astype(np.float32) * np.ones((1, 128), np.float32) for n in range(16)], 1).reshape(16, 2048),
        "c_sel8": np.stack([(np.arange(8)[:, None] == h).astype(np.float32) * np.ones((1, 128), np.float32) for h in range(8)], 1).reshape(8, 1024),
        "c_sel2": np.stack([(np.arange(16)[:, None] == 2 * ft + (np.arange(128)[None, :] >= 64)).astype(np.float32) for ft in range(8)], 1).reshape(16, 1024),
        "c_tril": np.where(i[:, None] >= i[None, :], 0.0, NEG_BIG).astype(np.float32),
    }
    return cons


def declare_consts(nc):
    return {
        "ident": nc.dram_tensor("c_ident", [128, 128], BF16, kind="ExternalInput").ap(),
        "ones1x128": nc.dram_tensor("c_ones1x128", [1, 128], BF16, kind="ExternalInput").ap(),
        "ones128x1": nc.dram_tensor("c_ones128x1", [128, 1], BF16, kind="ExternalInput").ap(),
        "onesf32": nc.dram_tensor("c_onesf32", [1, 128], F32, kind="ExternalInput").ap(),
        "onescolf32": nc.dram_tensor("c_onescolf32", [128, 1], F32, kind="ExternalInput").ap(),
        "identf32": nc.dram_tensor("c_identf32", [128, 128], F32, kind="ExternalInput").ap(),
        "epscols": nc.dram_tensor("c_epscols", [128, 4], F32, kind="ExternalInput").ap(),
        "maskseg": nc.dram_tensor("c_maskseg", [128, 128], F32, kind="ExternalInput").ap(),
        "bo2": nc.dram_tensor("c_bo2", [128, 2], F32, kind="ExternalInput").ap(),
        "bc2": nc.dram_tensor("c_bc2", [2, 128], F32, kind="ExternalInput").ap(),
        "sel2": nc.dram_tensor("c_sel2", [16, 1024], F32, kind="ExternalInput").ap(),
        "sel8": nc.dram_tensor("c_sel8", [8, 1024], F32, kind="ExternalInput").ap(),
        "sel16": nc.dram_tensor("c_sel16", [16, 2048], F32, kind="ExternalInput").ap(),
        "tril": nc.dram_tensor("c_tril", [128, 128], F32, kind="ExternalInput").ap(),
    }


class Shared:
    def __init__(self, ctx, es, caps):
        nc = ctx.nc
        self.pk = [PBank(ctx, es.enter_context(nc.psum_tensor(f"pk{i}", [128, 512], F32)), f"pk{i}")
                   for i in range(6)]
        self.pt = PBank(ctx, es.enter_context(nc.psum_tensor("pt", [128, 1024], BF16)), "pt")
        self.pb = PBank(ctx, es.enter_context(nc.psum_tensor("pb", [128, 512], F32)), "pb")
        self.ident = es.enter_context(nc.sbuf_tensor("ident", [128, 128], BF16, side="right"))
        self.ones1 = es.enter_context(nc.sbuf_tensor("ones1", [1, 128], BF16, side="right"))
        self.onescol = es.enter_context(nc.sbuf_tensor("onescol", [128, 1], BF16, side="right"))
        self.onesf32 = es.enter_context(nc.sbuf_tensor("onesf32", [1, 128], F32, side="right"))
        self.onescolf32 = es.enter_context(nc.sbuf_tensor("onescolf32", [128, 1], F32, side="right"))
        self.identf32 = es.enter_context(nc.sbuf_tensor("identf32", [128, 128], F32, side="right"))
        self.epscols = es.enter_context(nc.sbuf_tensor("epscols", [128, 4], F32, side="right"))
        self.maskseg = es.enter_context(nc.sbuf_tensor("maskseg", [128, 128], F32, side="right"))
        self.bo2 = es.enter_context(nc.sbuf_tensor("bo2", [128, 2], F32, side="right"))
        self.bc2 = es.enter_context(nc.sbuf_tensor("bc2", [2, 128], F32, side="right"))
        self.sel2 = es.enter_context(nc.sbuf_tensor("sel2", [16, 1024], F32, side="right"))
        self.sel8 = es.enter_context(nc.sbuf_tensor("sel8", [8, 1024], F32, side="right"))
        self.sel16 = es.enter_context(nc.sbuf_tensor("sel16", [16, 2048], F32, side="right"))
        self.tril = es.enter_context(nc.sbuf_tensor("tril", [128, 128], F32, side="right"))
        cld = ctx.sem("cld")
        ctx.dma("sync", self.ident[:, :], caps["ident"], cld)
        ctx.dma("sync", self.ones1[:, :], caps["ones1x128"], cld)
        ctx.dma("sync", self.onescol[:, :], caps["ones128x1"], cld)
        ctx.dma("sync", self.onesf32[:, :], caps["onesf32"], cld)
        ctx.dma("sync", self.onescolf32[:, :], caps["onescolf32"], cld)
        ctx.dma("sync", self.identf32[:, :], caps["identf32"], cld)
        ctx.dma("sync", self.epscols[:, :], caps["epscols"], cld)
        ctx.dma("sync", self.maskseg[:, :], caps["maskseg"], cld)
        ctx.dma("sync", self.bo2[:, :], caps["bo2"], cld)
        ctx.dma("sync", self.bc2[:, :], caps["bc2"], cld)
        ctx.dma("sync", self.sel2[:, :], caps["sel2"], cld)
        ctx.dma("sync", self.sel8[:, :], caps["sel8"], cld)
        ctx.dma("sync", self.sel16[:, :], caps["sel16"], cld)
        ctx.dma("sync", self.tril[:, :], caps["tril"], cld)
        for e in ("scalar", "vector", "tensor", "gpsimd"):
            ctx.wait(e, cld, cld.v)



def emit_ln_tile(ctx, xs_ap, out_ap, tmp_sq, stat, D, sem_done, waits=(), eps=1e-5,
                 sv=None, sh=None, gain=1.0):
    assert sh is not None
    epscol = {1e-5: 0, 1e-6: 1, 1e-8: 2, 1e-12: 3}[eps]
    sv = sv or ctx.sem()
    prev_done = sem_done.v
    ctx.waits("scalar", waits)
    ctx.wait("scalar", sem_done, prev_done)
    ctx.run("scalar", lambda e, a=xs_ap, o=tmp_sq[:, 0:D]: e.activation(
        out=o, in_=a, func=AF.Square), inc=sv)
    sv_s1 = sv.v
    ctx.waits("vector", waits)
    ctx.wait("vector", sem_done, prev_done)
    ctx.on("vector", lambda e, a=xs_ap, o=stat[:, 0:1]: e.tensor_reduce(
        out=o, in_=a, axis=AX.X, op=ALU.add, negate=True))
    ctx.wait("vector", sv, sv_s1)
    ctx.run("vector", lambda e, a=tmp_sq[:, 0:D], o=stat[:, 1:2]: e.tensor_reduce(
        out=o, in_=a, axis=AX.X, op=ALU.add), inc=sv)
    sv_v2 = sv.v
    ctx.wait("scalar", sv, sv_v2)
    ctx.on("scalar", lambda e, s=stat, d=D: e.activation(
        out=s[:, 2:3], in_=s[:, 0:1], func=AF.Square, scale=1.0 / d))
    ctx.run("scalar", lambda e, s=stat, d=D: e.activation(
        out=s[:, 3:4], in_=s[:, 1:2], func=AF.Identity, scale=1.0 / d), inc=sv)
    sv_s3 = sv.v
    ctx.wait("vector", sv, sv_s3)
    ctx.run("vector", lambda e, s=stat: e.scalar_tensor_tensor(
        out=s[:, 4:5], in0=s[:, 2:3], scalar=-1.0, in1=s[:, 3:4],
        op0=ALU.mult, op1=ALU.add), inc=sv)
    sv_v3 = sv.v
    ctx.wait("scalar", sv, sv_v3)
    ctx.on("scalar", lambda e, s=stat, ec=sh.epscols, j=epscol: e.activation(
        out=s[:, 5:6], in_=s[:, 4:5], func=AF.Ln, bias=ec[:, j:j + 1]))
    ctx.run("scalar", lambda e, s=stat: e.activation(
        out=s[:, 5:6], in_=s[:, 5:6], func=AF.Exp, scale=-0.5), inc=sv)
    sv_s5 = sv.v
    if gain != 1.0:
        ctx.wait("vector", sv, sv_s5)
        ctx.run("vector", lambda e, s=stat, g=gain: e.tensor_scalar(
            out=s[:, 5:6], in0=s[:, 5:6], scalar1=g, scalar2=None, op0=ALU.mult), inc=sv)
        sv_s5 = sv.v
    ctx.wait("vector", sv, sv_s5)
    ctx.run("vector", lambda e, s=stat, d=D: e.scalar_tensor_tensor(
        out=s[:, 6:7], in0=s[:, 0:1], scalar=1.0 / d, in1=s[:, 5:6],
        op0=ALU.mult, op1=ALU.mult), inc=sv)
    sv_v4 = sv.v
    ctx.wait("scalar", sv, sv_v4)
    ctx.run("scalar", lambda e, a=xs_ap, o=out_ap, s=stat: e.activation(
        out=o, in_=a, func=AF.Identity, bias=s[:, 6:7], scale=s[:, 5:6]), inc=sem_done)
    return sem_done.v


def emit_postln_resid(ctx, o_ap, xs_ap, tmp_sq, stat, D, sem_done, waits=(), eps=1e-5, sv=None, sln=None, sh=None):
    sln = sln or ctx.sem()
    emit_ln_tile(ctx, o_ap, tmp_sq[:, 0:D], tmp_sq, stat, D, sln, waits=waits, eps=eps, sv=sv, sh=sh)
    ctx.wait("vector", sln, sln.v)
    ctx.run("vector", lambda e, a=xs_ap, t=tmp_sq[:, 0:D]: e.tensor_add(out=a, in0=a, in1=t), inc=sem_done)
    return sem_done.v


def emit_ffn_quant(ctx, sh, es, x_in, x_out, w1_ap, b1_ap, w2_ap, b2_ap, name="f"):
    nc = ctx.nc
    T, D, FF = 4096, 1024, 4096
    CH, NTT = 256, 2
    NCH = T // CH

    w1qt = es.enter_context(nc.sbuf_tensor(name + "w1qt", [128, 8, FF], BF16))
    w2qt = es.enter_context(nc.sbuf_tensor(name + "w2qt", [128, 32, D], BF16))
    b1t = es.enter_context(nc.sbuf_tensor(name + "b1t", [128, 32], F32))
    b2b = es.enter_context(nc.sbuf_tensor(name + "b2b", [128, D], F32))
    wld = [es.enter_context(nc.sbuf_tensor(name + f"wld{i}", [128, 1024], F32)) for i in range(2)]
    wbf = [es.enter_context(nc.sbuf_tensor(name + f"wbf{i}", [128, 1024], BF16)) for i in range(2)]
    gpart = es.enter_context(nc.sbuf_tensor(name + "gpart", [128, 64], F32))
    gsc = es.enter_context(nc.sbuf_tensor(name + "gsc", [1, 4], F32))
    ginv = es.enter_context(nc.sbuf_tensor(name + "ginv", [128, 2], F32))
    browf = es.enter_context(nc.sbuf_tensor(name + "brow", [32, 128], F32))
    _xs1 = es.enter_context(nc.sbuf_tensor(name + "xs0", [128, NTT, 1024], F32))
    xs = [_xs1, _xs1]
    xn = es.enter_context(nc.sbuf_tensor(name + "xn", [128, NTT, 1024], BF16))
    xnt = es.enter_context(nc.sbuf_tensor(name + "xnt", [128, 8, CH], BF16))
    ht = es.enter_context(nc.sbuf_tensor(name + "ht", [128, 16, CH], BF16))
    osb = es.enter_context(nc.sbuf_tensor(name + "osb", [128, NTT, 1024], F32))
    sq = es.enter_context(nc.sbuf_tensor(name + "sq", [128, 1024], F32))
    stat = es.enter_context(nc.sbuf_tensor(name + "stat", [128, 8], F32))

    ld = ctx.sem(name + "ld")
    gred = ctx.sem(name + "gred")
    qcons = ctx.sem(name + "qcons")
    ldx = ctx.sem(name + "ldx")
    gred0, qcons0 = gred.v, qcons.v
    stx = ctx.sem(name + "stx")
    sLN = ctx.sem(name + "sln")
    sOut = ctx.sem(name + "sout")
    sRes = ctx.sem(name + "sres")
    svA = ctx.sem(name + "svA")
    svB = ctx.sem(name + "svB")

    w1v = w1_ap
    w2v = w2_ap
    for i in range(32):
        b = i % 2
        if i >= 2:
            ctx.wait("sync", gred, gred0 + i - 1)
        ctx.dma("sync", wld[b][:, :], w1v[i * 128:(i + 1) * 128, :], ld)
        ctx.wait("vector", ld, ld.v)
        ctx.run("vector", lambda e, s=wld[b], g=gpart, i=i: e.tensor_reduce(
            out=g[:, i:i + 1], in_=s[:, :], axis=AX.X, op=ALU.add,
            apply_absolute_value=True), inc=gred)
    for i in range(32):
        b = i % 2
        ctx.wait("sync", gred, gred0 + 31 + i)
        ot, fc = i // 4, i % 4
        ctx.dma("sync", wld[b][:, :], w2v[ot * 128:(ot + 1) * 128, fc * 1024:(fc + 1) * 1024], ld)
        ctx.wait("vector", ld, ld.v)
        ctx.run("vector", lambda e, s=wld[b], g=gpart, i=i: e.tensor_reduce(
            out=g[:, 32 + i:33 + i], in_=s[:, :], axis=AX.X, op=ALU.add,
            apply_absolute_value=True), inc=gred)
    ctx.wait("vector", gred, gred.v)
    ctx.on("vector", lambda e, g=gpart: e.tensor_reduce(
        out=g[:, 0:1], in_=g[:, 0:32], axis=AX.X, op=ALU.add))
    ctx.run("vector", lambda e, g=gpart: e.tensor_reduce(
        out=g[:, 1:2], in_=g[:, 32:64], axis=AX.X, op=ALU.add), inc=gred)
    ctx.wait("vector", gred, gred.v)
    gred_all = gred.v
    sh.pb.group(
        [lambda e, g=gpart, p=sh.pb.ap, oc=sh.onescolf32: e.matmul(
            p[0:1, 0:2], oc[:, 0:1], g[:, 0:2], start=True, stop=True)],
        [("vector", lambda e, p=sh.pb.ap, o=gsc: e.tensor_copy(out=o[0:1, 0:2], in_=p[0:1, 0:2]))],
        extra_waits=[(gred, gred_all)])
    ctx.wait("vector", sh.pb.ev, sh.pb.ev.v)
    ctx.run("vector", lambda e, o=gsc: e.tensor_scalar(
        out=o[0:1, 2:4], in0=o[0:1, 0:2], scalar1=1.0 / (4096.0 * 1024.0), scalar2=1e-5,
        op0=ALU.mult, op1=ALU.add), inc=gred)
    ctx.wait("vector", gred, gred.v)
    ctx.run("vector", lambda e, o=gsc: e.reciprocal(out=o[0:1, 2:4], in_=o[0:1, 2:4]), inc=gred)
    gv = gred.v
    sh.pb.group(
        [lambda e, o=gsc, p=sh.pb.ap, of=sh.onesf32: e.matmul(
            p[:, 0:2], of[0:1, :], o[0:1, 2:4], start=True, stop=True)],
        [("vector", lambda e, p=sh.pb.ap, o=ginv: e.tensor_copy(out=o[:, 0:2], in_=p[:, 0:2]))],
        extra_waits=[(gred, gv)])
    ginv_ready = sh.pb.ev.v

    trw = []

    def quant_piece(b, wi, ldv, piece_idx):
        ctx.wait("vector", ld, ldv)
        ctx.wait("vector", sh.pb.ev, ginv_ready)
        if piece_idx >= 2:
            ctx.wait("vector", sh.pt.wr, trw[piece_idx - 2])
        ctx.on("vector", lambda e, s=wld[b], g=ginv, wi=wi: e.tensor_scalar(
            out=s[:, :], in0=s[:, :], scalar1=g[:, wi:wi + 1], scalar2=None, op0=ALU.mult))
        ctx.on("vector", lambda e, s=wld[b]: e.tensor_scalar(
            out=s[:, :], in0=s[:, :], scalar1=MAGIC, scalar2=-MAGIC, op0=ALU.add, op1=ALU.add))
        ctx.run("vector", lambda e, s=wld[b], d=wbf[b]: e.tensor_scalar(
            out=d[:, :], in0=s[:, :], scalar1=-1.0, scalar2=1.0, op0=ALU.max, op1=ALU.min), inc=qcons)

    for i in range(32):
        b = i % 2
        if i < 2:
            ctx.wait("sync", gred, gred_all)
        else:
            ctx.wait("sync", qcons, qcons0 + i - 1)
        ctx.dma("sync", wld[b][:, :], w1v[i * 128:(i + 1) * 128, :], ld)
        quant_piece(b, 0, ld.v, i)
        qv = qcons.v
        sh.pt.group(
            [lambda e, s=wbf[b], dt=dt, p=sh.pt.ap, idn=sh.ident: e.transpose(
                p[:, dt * 128:(dt + 1) * 128], s[:, dt * 128:(dt + 1) * 128], idn[:, :])
             for dt in range(8)],
            [lambda e, p=sh.pt.ap, d=w1qt, i=i: ecopy(e,
                out=d[:, :, i * 128:(i + 1) * 128],
                in_=p[:, :].rearrange("p (a b) -> p a b", a=8))],
            evac_eng="scalar", extra_waits=[(qcons, qv)])
        trw.append(sh.pt.wr.v)
    for i in range(32):
        b = i % 2
        ctx.wait("sync", qcons, qcons0 + 31 + i)
        ot, fc = i // 4, i % 4
        ctx.dma("sync", wld[b][:, :], w2v[ot * 128:(ot + 1) * 128, fc * 1024:(fc + 1) * 1024], ld)
        quant_piece(b, 1, ld.v, 32 + i)
        qv = qcons.v
        sh.pt.group(
            [lambda e, s=wbf[b], fs=fs, p=sh.pt.ap, idn=sh.ident: e.transpose(
                p[:, fs * 128:(fs + 1) * 128], s[:, fs * 128:(fs + 1) * 128], idn[:, :])
             for fs in range(8)],
            [lambda e, p=sh.pt.ap, d=w2qt, ot=ot, fc=fc: ecopy(e,
                out=d[:, fc * 8:(fc + 1) * 8, ot * 128:(ot + 1) * 128],
                in_=p[:, :].rearrange("p (a b) -> p a b", a=8))],
            evac_eng="scalar", extra_waits=[(qcons, qv)])
        trw.append(sh.pt.wr.v)

    bld = ctx.sem(name + "bld")
    ctx.dma("sync", browf[:, :], b1_ap.rearrange("(a p) -> a p", p=128), bld)
    bv = bld.v
    sh.pb.group(
        [lambda e, s=browf, p=sh.pb.ap, idn=sh.identf32: e.transpose(
            p[:, 0:32], s[:, :], idn[0:32, 0:32])],
        [("vector", lambda e, p=sh.pb.ap, d=b1t: e.tensor_copy(out=d[:, :], in_=p[:, 0:32]))],
        extra_waits=[(bld, bv)])
    b2row = es.enter_context(nc.sbuf_tensor(name + "b2row", [1, 1024], F32))
    ctx.dma("sync", b2row[0:1, :], b2_ap.rearrange("(a d) -> a d", a=1), bld)
    bv = bld.v
    for h in range(2):
        sh.pb.group(
            [lambda e, r=b2row, p=sh.pb.ap, of=sh.onesf32, h=h: e.matmul(
                p[:, 0:512], of[0:1, :], r[0:1, h * 512:(h + 1) * 512], start=True, stop=True)],
            [("vector", lambda e, p=sh.pb.ap, d=b2b, h=h: e.tensor_copy(
                out=d[:, h * 512:(h + 1) * 512], in_=p[:, 0:512]))],
            extra_waits=[(bld, bv)])
    pre_done = sh.pb.ev.v

    osb_free = [0]
    stx0 = stx.v
    for c in range(NCH):
        b = c % 2
        if c >= 1:
            ctx.wait("sync", stx, stx0 + c * 16)
        ctx.dma("sync", xs[b][:, :, :],
                x_in[c * CH:(c + 1) * CH, :].rearrange("(a p) d -> p a d", p=128), ldx)
        lv = ldx.v
        ptw = sh.pt.wr.v
        for tt in range(NTT):
            emit_ln_tile(ctx, xs[b][:, tt, :], xn[:, tt, :], sq, stat, D, sLN,
                         waits=[(ldx, lv), (sh.pt.wr, ptw), (sRes, sRes.v)], sv=svA, sh=sh)
        lnv = sLN.v
        for tt in range(NTT):
            sh.pt.group(
                [lambda e, s=xn, tt=tt, dt=dt, p=sh.pt.ap, idn=sh.ident: e.transpose(
                    p[:, dt * 128:(dt + 1) * 128], s[:, tt, dt * 128:(dt + 1) * 128], idn[:, :])
                 for dt in range(8)],
                [lambda e, p=sh.pt.ap, d=xnt, tt=tt: ecopy(e,
                    out=d[:, :, tt * 128:(tt + 1) * 128],
                    in_=p[:, :].rearrange("p (a b) -> p a b", a=8))],
                evac_eng="scalar", extra_waits=[(sLN, lnv)])
        xnt_t = sh.pt.ev.v
        pk_prev_ev = [sh.pk[i].ev.v for i in range(4)]
        for half in range(2):
            hbanks = [sh.pk[4], sh.pk[5]]
            outw = sOut.v
            for ft in range(16):
                fg = half * 16 + ft
                pkb = hbanks[ft % 2]
                if half == 0:
                    ew = [(sh.pk[i].ev, pk_prev_ev[i]) for i in range(4)]
                else:
                    ew = [(sOut, outw)]
                pkb.group(
                    [lambda e, w=w1qt, x_=xnt, dt=dt, fg=fg, p=pkb.ap: e.matmul(
                        p[:, 0:CH], w[:, dt, fg * 128:(fg + 1) * 128], x_[:, dt, :],
                        start=(dt == 0), stop=(dt == 7))
                     for dt in range(8)],
                    [lambda e, p=pkb.ap, d=ht, ft=ft, b1=b1t, fg=fg: e.activation(
                        out=d[:, ft, :], in_=p[:, 0:CH], func=AF.Relu, bias=b1[:, fg:fg + 1])],
                    evac_eng="scalar",
                    extra_waits=[(sh.pt.ev, xnt_t)],
                    evac_extra_waits=ew + [(sh.pb.ev, pre_done)])
            htv = [sh.pk[4].ev.v, sh.pk[5].ev.v]
            for tt in range(NTT):
                for oh in range(2):
                    pkb = sh.pk[tt * 2 + oh]
                    ctx.wait("tensor", pkb.ev, pkb.ev.v)
                    ctx.wait("tensor", sh.pk[4].ev, htv[0])
                    ctx.wait("tensor", sh.pk[5].ev, htv[1])
                    for ft in range(16):
                        def mmf(e, w=w2qt, h_=ht, ft=ft, tt=tt, oh=oh, p=pkb.ap, half=half):
                            return e.matmul(p[:, 0:512], h_[:, ft, tt * 128:(tt + 1) * 128],
                                            w[:, half * 16 + ft, oh * 512:(oh + 1) * 512],
                                            start=(half == 0 and ft == 0), stop=(half == 1 and ft == 15))
                        if half == 0 and tt == NTT - 1 and oh == 1 and ft == 15:
                            ctx.run("tensor", mmf, inc=sOut)
                        elif half == 1 and ft == 15:
                            ctx.run("tensor", mmf, inc=pkb.wr)
                        else:
                            ctx.on("tensor", mmf)
        for tt in range(NTT):
            for oh in range(2):
                pkb = sh.pk[tt * 2 + oh]
                ctx.wait("vector", pkb.wr, pkb.wr.v)
                ctx.wait("vector", sh.pb.ev, pre_done)
                ctx.wait("vector", sLN, osb_free[0])
                ctx.run("vector", lambda e, p=pkb.ap, d=osb, tt=tt, oh=oh, bb=b2b: e.tensor_tensor(
                    out=d[:, tt, oh * 512:(oh + 1) * 512], in0=p[:, 0:512],
                    in1=bb[:, oh * 512:(oh + 1) * 512], op=ALU.add), inc=pkb.ev)
        for tt in range(NTT):
            w = [(sh.pk[tt * 2].ev, sh.pk[tt * 2].ev.v),
                 (sh.pk[tt * 2 + 1].ev, sh.pk[tt * 2 + 1].ev.v), (sRes, sRes.v)]
            emit_postln_resid(ctx, osb[:, tt, :], xs[b][:, tt, :], sq, stat, D, sRes,
                              waits=w, sv=svB, sln=sLN, sh=sh)
        osb_free[0] = sLN.v
        rv = sRes.v
        ctx.wait("gpsimd", sRes, rv)
        ctx.dma("gpsimd", x_out[c * CH:(c + 1) * CH, :].rearrange("(a p) d -> p a d", p=128),
                xs[b][:, :, :], stx)
    ctx.wait("gpsimd", stx, stx.v)
    return stx


def emit_loadT(ctx, sh, w_ap, F, D, dst, wld, wbf, ld, cons):
    nft, ndt = F // 128, D // 128
    meta = ctx.meta = getattr(ctx, "meta", {})
    for ft in range(nft):
        b = ft % 2
        u = meta.setdefault("loadT_uses", 0)
        if u >= 2:
            ps, pv = meta["loadT_cons"][u - 2]
            ctx.wait("sync", ps, pv)
        ctx.dma("sync", wld[b][:, 0:D], w_ap[ft * 128:(ft + 1) * 128, :], ld)
        ctx.wait("scalar", ld, ld.v)
        if u >= 2:
            ctx.wait("scalar", sh.pt.wr, meta["loadT_trw"][u - 2])
        ctx.run("scalar", lambda e, s=wld[b], d=wbf[b], D=D: e.activation(
            out=d[:, 0:D], in_=s[:, 0:D], func=AF.Copy), inc=cons)
        meta.setdefault("loadT_cons", []).append((cons, cons.v))
        qv = cons.v
        for g0 in range(0, ndt, 8):
            g1 = min(ndt, g0 + 8)
            sh.pt.group(
                [lambda e, s=wbf[b], dt=dt, g0=g0, p=sh.pt.ap, idn=sh.ident: e.transpose(
                    p[:, (dt - g0) * 128:(dt - g0 + 1) * 128], s[:, dt * 128:(dt + 1) * 128], idn[:, :])
                 for dt in range(g0, g1)],
                [lambda e, p=sh.pt.ap, d=dst, g0=g0, g1=g1, ft=ft: ecopy(
                    e, out=d[:, g0:g1, ft * 128:(ft + 1) * 128],
                    in_=p[:, 0:(g1 - g0) * 128].rearrange("p (a b) -> p a b", a=g1 - g0))],
                evac_eng="scalar", extra_waits=[(cons, qv)])
        meta.setdefault("loadT_trw", []).append(sh.pt.wr.v)
        meta["loadT_uses"] = u + 1


def emit_load_vecT(ctx, sh, vec_ap, N, dst, ld):
    nc = ctx.nc
    na = N // 128
    assert na <= 128
    tmp = ctx.meta.get("vecT_tmp")
    ctx.wait("sync", sh.pb.wr, sh.pb.wr.v)
    ctx.dma("sync", tmp[0:na, 0:128], vec_ap.rearrange("(a p) -> a p", p=128), ld)
    lv = ld.v
    sh.pb.group(
        [lambda e, s=tmp, p=sh.pb.ap, idn=sh.identf32, na=na: e.transpose(
            p[:, 0:na], s[0:na, 0:128], idn[0:na, 0:na])],
        [("vector", lambda e, p=sh.pb.ap, d=dst, na=na: e.tensor_copy(
            out=d[:, 0:na], in_=p[:, 0:na]))],
        extra_waits=[(ld, lv)])
    return sh.pb.ev.v


def emit_bcast_row(ctx, sh, vec_ap, N, dst, ld, tmprow):
    ctx.wait("sync", sh.pb.wr, sh.pb.wr.v)
    ctx.dma("sync", tmprow[0:1, 0:N], vec_ap.rearrange("(a d) -> a d", a=1), ld)
    lv = ld.v
    for c0 in range(0, N, 512):
        c1 = min(N, c0 + 512)
        sh.pb.group(
            [lambda e, r=tmprow, p=sh.pb.ap, of=sh.onesf32, c0=c0, c1=c1: e.matmul(
                p[:, 0:c1 - c0], of[0:1, :], r[0:1, c0:c1], start=True, stop=True)],
            [("vector", lambda e, p=sh.pb.ap, d=dst, c0=c0, c1=c1: e.tensor_copy(
                out=d[:, c0:c1], in_=p[:, 0:c1 - c0]))],
            extra_waits=[(ld, lv)])
    return sh.pb.ev.v



def emit_l0_attn(ctx, sh, es, x_in, x_out, W, name="a0"):
    nc = ctx.nc
    T, D = 4096, 1024
    CH, NTT = 256, 2
    NCH = T // CH
    H, HD = 8, 128
    SCALE = 1.0 / float(np.sqrt(np.float32(128.0)))

    wqkT = es.enter_context(nc.sbuf_tensor(name + "wqkT", [128, 8, D], BF16))
    wvT = es.enter_context(nc.sbuf_tensor(name + "wvT", [128, 8, D], BF16))
    wqT = es.enter_context(nc.sbuf_tensor(name + "wqT", [128, 8, D], BF16))
    wkT = es.enter_context(nc.sbuf_tensor(name + "wkT", [128, 8, D], BF16))
    wvT2 = es.enter_context(nc.sbuf_tensor(name + "wvT2", [128, 8, D], BF16))
    woT = es.enter_context(nc.sbuf_tensor(name + "woT", [128, 8, D], BF16))
    wld = [es.enter_context(nc.sbuf_tensor(name + f"wld{i}", [128, 1024], F32)) for i in range(2)]
    wbf = [es.enter_context(nc.sbuf_tensor(name + f"wbf{i}", [128, 1024], BF16)) for i in range(2)]
    vec8 = es.enter_context(nc.sbuf_tensor(name + "vec8", [128, 128], F32))
    bqk = es.enter_context(nc.sbuf_tensor(name + "bqk", [128, 8], F32))
    bvv = es.enter_context(nc.sbuf_tensor(name + "bvv", [128, 8], F32))
    bq = es.enter_context(nc.sbuf_tensor(name + "bq", [128, 8], F32))
    bk = es.enter_context(nc.sbuf_tensor(name + "bk", [128, 8], F32))
    bv2B = es.enter_context(nc.sbuf_tensor(name + "bv2B", [128, 1024], F32))
    boB = es.enter_context(nc.sbuf_tensor(name + "boB", [128, 1024], F32))
    tmprow = es.enter_context(nc.sbuf_tensor(name + "tmprow", [1, 1024], F32))
    xs = [es.enter_context(nc.sbuf_tensor(name + f"xs{i}", [128, NTT, 1024], F32)) for i in range(2)]
    xn = es.enter_context(nc.sbuf_tensor(name + "xn", [128, NTT, 1024], BF16))
    xnt = es.enter_context(nc.sbuf_tensor(name + "xnt", [128, 8, CH], BF16))
    qkT = es.enter_context(nc.sbuf_tensor(name + "qkT", [128, 8, CH], BF16))
    viT = es.enter_context(nc.sbuf_tensor(name + "viT", [128, 8, CH], BF16))
    qTb = es.enter_context(nc.sbuf_tensor(name + "qTb", [128, 8, CH], BF16))
    kTb = es.enter_context(nc.sbuf_tensor(name + "kTb", [128, 8, CH], BF16))
    v_sb = es.enter_context(nc.sbuf_tensor(name + "v_sb", [128, NTT, 1024], BF16))
    oT = es.enter_context(nc.sbuf_tensor(name + "oT", [128, 8, CH], BF16))
    osb = es.enter_context(nc.sbuf_tensor(name + "osb", [128, NTT, 1024], F32))
    sc_sb = es.enter_context(nc.sbuf_tensor(name + "sc", [128, 128], F32))
    e_sb = es.enter_context(nc.sbuf_tensor(name + "e", [128, 128], BF16))
    at_sb = es.enter_context(nc.sbuf_tensor(name + "at", [128, 128], BF16))
    mst = es.enter_context(nc.sbuf_tensor(name + "mst", [128, 4], F32))
    sq = es.enter_context(nc.sbuf_tensor(name + "sq", [128, 1024], F32))
    stat = es.enter_context(nc.sbuf_tensor(name + "stat", [128, 8], F32))

    ld = ctx.sem(name + "ld")
    cons = ctx.sem(name + "cons")
    ldx = ctx.sem(name + "ldx")
    stx = ctx.sem(name + "stx")
    sLN = ctx.sem(name + "sLN")
    sRes = ctx.sem(name + "sRes")
    svA = ctx.sem(name + "svA")
    svB = ctx.sem(name + "svB")
    sAt = ctx.sem(name + "sAt")

    ctx.meta = getattr(ctx, "meta", {})
    ctx.meta["vecT_tmp"] = vec8

    emit_loadT(ctx, sh, W["proj_qk_w"], D, D, wqkT, wld, wbf, ld, cons)
    emit_loadT(ctx, sh, W["proj_v_w"], D, D, wvT, wld, wbf, ld, cons)
    emit_loadT(ctx, sh, W["in_proj_w"][0:1024, :], D, D, wqT, wld, wbf, ld, cons)
    emit_loadT(ctx, sh, W["in_proj_w"][1024:2048, :], D, D, wkT, wld, wbf, ld, cons)
    emit_loadT(ctx, sh, W["in_proj_w"][2048:3072, :], D, D, wvT2, wld, wbf, ld, cons)
    emit_loadT(ctx, sh, W["out_proj_w"], D, D, woT, wld, wbf, ld, cons)
    emit_load_vecT(ctx, sh, W["proj_qk_b"], D, bqk, ld)
    emit_load_vecT(ctx, sh, W["proj_v_b"], D, bvv, ld)
    emit_load_vecT(ctx, sh, W["in_proj_b"][0:1024], D, bq, ld)
    emit_load_vecT(ctx, sh, W["in_proj_b"][1024:2048], D, bk, ld)
    bv2_t = emit_bcast_row(ctx, sh, W["in_proj_b"][2048:3072], D, bv2B, ld, tmprow)
    bo_t = emit_bcast_row(ctx, sh, W["out_proj_b"], D, boB, ld, tmprow)
    ctx.wait("vector", sh.pb.ev, sh.pb.ev.v)
    ctx.run("vector", lambda e, b=bq, s=SCALE: e.tensor_scalar(
        out=b[:, :], in0=b[:, :], scalar1=s, scalar2=None, op0=ALU.mult), inc=cons)
    pre_done_pb = sh.pb.ev.v
    pre_done_cons = cons.v
    wq_trw = sh.pt.ev.v

    def fproj(src, wT, nsrc_t, dst, bias_col, scale_imm, extra_ev=()):
        for ft in range(8):
            pkb = sh.pk[4 + ft % 2]
            pkb.group(
                [lambda e, w=wT, s=src, dt=dt, ft=ft, p=pkb.ap: e.matmul(
                    p[:, 0:CH], w[:, dt, ft * 128:(ft + 1) * 128], s[:, dt, :],
                    start=(dt == 0), stop=(dt == 7))
                 for dt in range(8)],
                [lambda e, p=pkb.ap, d=dst, ft=ft, bc=bias_col, si=scale_imm: e.activation(
                    out=d[:, ft, :], in_=p[:, 0:CH], func=AF.Identity,
                    bias=bc[:, ft:ft + 1], scale=si)],
                evac_eng="scalar",
                extra_waits=[(sh.pt.ev, sh.pt.ev.v)],
                evac_extra_waits=list(extra_ev) + [(cons, pre_done_cons), (sh.pb.ev, pre_done_pb)])

    stx0 = stx.v
    for c in range(NCH):
        b = c % 2
        if c >= 2:
            ctx.wait("sync", stx, stx0 + (c - 1) * 16)
        ctx.dma("sync", xs[b][:, :, :],
                x_in[c * CH:(c + 1) * CH, :].rearrange("(a p) d -> p a d", p=128), ldx)
        lv = ldx.v
        ptw = sh.pt.wr.v
        for tt in range(NTT):
            emit_ln_tile(ctx, xs[b][:, tt, :], xn[:, tt, :], sq, stat, D, sLN,
                         waits=[(ldx, lv), (sh.pt.wr, ptw), (sRes, sRes.v)], sv=svA, sh=sh)
        lnv = sLN.v
        for tt in range(NTT):
            sh.pt.group(
                [lambda e, s=xn, tt=tt, dt=dt, p=sh.pt.ap, idn=sh.ident: e.transpose(
                    p[:, dt * 128:(dt + 1) * 128], s[:, tt, dt * 128:(dt + 1) * 128], idn[:, :])
                 for dt in range(8)],
                [lambda e, p=sh.pt.ap, d=xnt, tt=tt: ecopy(
                    e, out=d[:, :, tt * 128:(tt + 1) * 128],
                    in_=p[:, :].rearrange("p (a b) -> p a b", a=8))],
                evac_eng="scalar", extra_waits=[(sLN, lnv)])
        fproj(xnt, wqkT, NTT, qkT, bqk, 1.0)
        fproj(xnt, wvT, NTT, viT, bvv, 1.0)
        fproj(qkT, wqT, NTT, qTb, bq, SCALE)
        fproj(qkT, wkT, NTT, kTb, bk, 1.0)
        for tt in range(NTT):
            for fh in range(2):
                pkb = sh.pk[fh]
                pkb.group(
                    [lambda e, s=viT, w=wvT2, ft=ft, tt=tt, fh=fh, p=pkb.ap: e.matmul(
                        p[:, 0:512], s[:, ft, tt * 128:(tt + 1) * 128], w[:, ft, fh * 512:(fh + 1) * 512],
                        start=(ft == 0), stop=(ft == 7))
                     for ft in range(8)],
                    [("vector", lambda e, p=pkb.ap, d=v_sb, tt=tt, fh=fh, bb=bv2B: e.tensor_tensor(
                        out=d[:, tt, fh * 512:(fh + 1) * 512], in0=p[:, 0:512],
                        in1=bb[:, fh * 512:(fh + 1) * 512], op=ALU.add))],
                    evac_extra_waits=[(sh.pb.ev, pre_done_pb)])
        for tt in range(NTT):
            for h in range(H):
                pkb = sh.pk[2 + (h % 2)]
                pkb.group(
                    [lambda e, q=qTb, k=kTb, h=h, tt=tt, p=pkb.ap: e.matmul(
                        p[:, 0:128], q[:, h, tt * 128:(tt + 1) * 128], k[:, h, tt * 128:(tt + 1) * 128],
                        start=True, stop=True)],
                    [("vector", lambda e, p=pkb.ap, m=sh.maskseg: e.tensor_tensor(
                        out=sc_sb[:, :], in0=p[:, 0:128], in1=m[:, :], op=ALU.add))],
                    evac_extra_waits=[(sAt, sAt.v)])
                ctx.wait("vector", pkb.ev, pkb.ev.v)
                ctx.run("vector", lambda e: e.tensor_reduce(
                    out=mst[:, 0:1], in_=sc_sb[:, :], axis=AX.X, op=ALU.max, negate=True), inc=sAt)
                ctx.wait("scalar", sAt, sAt.v)
                ctx.run("scalar", lambda e: e.activation(
                    out=e_sb[:, :], in_=sc_sb[:, :], func=AF.Exp, bias=mst[:, 0:1]), inc=sAt)
                ctx.wait("vector", sAt, sAt.v)
                ctx.run("vector", lambda e: e.tensor_reduce(
                    out=mst[:, 1:2], in_=e_sb[:, :], axis=AX.X, op=ALU.add), inc=sAt)
                ctx.wait("vector", sAt, sAt.v)
                ctx.run("vector", lambda e: e.reciprocal(out=mst[:, 2:3], in_=mst[:, 1:2]), inc=sAt)
                ctx.wait("scalar", sAt, sAt.v)
                ctx.wait("scalar", sh.pk[2].wr, sh.pk[2].wr.v)
                ctx.wait("scalar", sh.pk[3].wr, sh.pk[3].wr.v)
                ctx.run("scalar", lambda e: e.activation(
                    out=at_sb[:, :], in_=e_sb[:, :], func=AF.Identity, scale=mst[:, 2:3]), inc=sAt)
                atv = sAt.v
                sh.pt.group(
                    [lambda e, p=sh.pt.ap, idn=sh.ident: e.transpose(p[:, 0:128], at_sb[:, :], idn[:, :])],
                    [lambda e, p=sh.pt.ap: ecopy(e, out=at_sb[:, :], in_=p[:, 0:128])],
                    evac_eng="scalar", extra_waits=[(sAt, atv)])
                attv = sh.pt.ev.v
                pkc = sh.pk[2 + ((h + 1) % 2)]
                pkc.group(
                    [lambda e, v=v_sb, h=h, tt=tt, p=pkc.ap: e.matmul(
                        p[:, 0:128], v[:, tt, h * 128:(h + 1) * 128], at_sb[:, :],
                        start=True, stop=True)],
                    [lambda e, p=pkc.ap, d=oT, h=h, tt=tt: ecopy(
                        e, out=d[:, h, tt * 128:(tt + 1) * 128], in_=p[:, 0:128])],
                    evac_eng="scalar",
                    extra_waits=[(sh.pt.ev, attv)])
        for tt in range(NTT):
            for oh in range(2):
                pkb = sh.pk[oh]
                pkb.group(
                    [lambda e, s=oT, w=woT, ft=ft, tt=tt, oh=oh, p=pkb.ap: e.matmul(
                        p[:, 0:512], s[:, ft, tt * 128:(tt + 1) * 128], w[:, ft, oh * 512:(oh + 1) * 512],
                        start=(ft == 0), stop=(ft == 7))
                     for ft in range(8)],
                    [("vector", lambda e, p=pkb.ap, d=osb, tt=tt, oh=oh, bb=boB: e.tensor_tensor(
                        out=d[:, tt, oh * 512:(oh + 1) * 512], in0=p[:, 0:512],
                        in1=bb[:, oh * 512:(oh + 1) * 512], op=ALU.add))],
                    evac_extra_waits=[(sh.pb.ev, pre_done_pb), (sLN, sLN.v)])
        for tt in range(NTT):
            w = [(sh.pk[0].ev, sh.pk[0].ev.v), (sh.pk[1].ev, sh.pk[1].ev.v), (sRes, sRes.v)]
            emit_postln_resid(ctx, osb[:, tt, :], xs[b][:, tt, :], sq, stat, D, sRes,
                              waits=w, sv=svB, sln=sLN, sh=sh)
        rv = sRes.v
        ctx.wait("gpsimd", sRes, rv)
        ctx.dma("gpsimd", x_out[c * CH:(c + 1) * CH, :].rearrange("(a p) d -> p a d", p=128),
                xs[b][:, :, :], stx)
    ctx.wait("gpsimd", stx, stx.v)
    return stx


def emit_l3_ffn(ctx, sh, es, x_in, x_out, W, name="f3"):
    nc = ctx.nc
    T, D, E, R = 4096, 1024, 8, 16
    CH, NTT = 256, 2
    NCH = T // CH

    wT = es.enter_context(nc.sbuf_tensor(name + "wT", [128, 8, D], BF16))
    rT = es.enter_context(nc.sbuf_tensor(name + "rT", [128, 8, E], BF16))
    aT = es.enter_context(nc.sbuf_tensor(name + "aT", [128, 8, E, R], BF16))
    bT = es.enter_context(nc.sbuf_tensor(name + "bT", [16, E, D], BF16))
    bB = es.enter_context(nc.sbuf_tensor(name + "bB", [128, D], F32))
    wld = [es.enter_context(nc.sbuf_tensor(name + f"wld{i}", [128, 1024], F32)) for i in range(2)]
    wbf = [es.enter_context(nc.sbuf_tensor(name + f"wbf{i}", [128, 1024], BF16)) for i in range(2)]
    tmprow = es.enter_context(nc.sbuf_tensor(name + "tmprow", [1, 1024], F32))
    vrow = es.enter_context(nc.sbuf_tensor(name + "vrow", [16, 1024], F32))
    vbf = es.enter_context(nc.sbuf_tensor(name + "vbf", [16, 1024], BF16))
    xs = [es.enter_context(nc.sbuf_tensor(name + f"xs{i}", [128, NTT, 1024], F32)) for i in range(2)]
    xn = es.enter_context(nc.sbuf_tensor(name + "xn", [128, NTT, 1024], BF16))
    xnt = es.enter_context(nc.sbuf_tensor(name + "xnt", [128, 8, CH], BF16))
    lg = es.enter_context(nc.sbuf_tensor(name + "lg", [128, NTT, E], F32))
    ge = es.enter_context(nc.sbuf_tensor(name + "ge", [128, E], F32))
    gaten = es.enter_context(nc.sbuf_tensor(name + "gaten", [128, NTT, E], F32))
    gst = es.enter_context(nc.sbuf_tensor(name + "gst", [128, 8], F32))
    gh = es.enter_context(nc.sbuf_tensor(name + "gh", [128, R], BF16))
    ght = es.enter_context(nc.sbuf_tensor(name + "ght", [16, E, 128], BF16))
    osb = es.enter_context(nc.sbuf_tensor(name + "osb", [128, NTT, 1024], F32))
    sq = es.enter_context(nc.sbuf_tensor(name + "sq", [128, 1024], F32))
    stat = es.enter_context(nc.sbuf_tensor(name + "stat", [128, 8], F32))

    ld = ctx.sem(name + "ld")
    cons = ctx.sem(name + "cons")
    ldx = ctx.sem(name + "ldx")
    stx = ctx.sem(name + "stx")
    sLN = ctx.sem(name + "sLN")
    sRes = ctx.sem(name + "sRes")
    svA = ctx.sem(name + "svA")
    svB = ctx.sem(name + "svB")
    sG = ctx.sem(name + "sG")

    ctx.meta = getattr(ctx, "meta", {})

    emit_loadT(ctx, sh, W["W"], D, D, wT, wld, wbf, ld, cons)
    bB_t = emit_bcast_row(ctx, sh, W["b"], D, bB, ld, tmprow)

    def loadT_small(src_ap, nrows, dst_fn):
        ctx.wait("sync", sh.pt.wr, sh.pt.wr.v)
        ctx.wait("sync", cons, cons.v)
        ctx.dma("sync", vrow[0:nrows, :], src_ap, ld)
        ctx.wait("scalar", ld, ld.v)
        ctx.run("scalar", lambda e, n=nrows: e.activation(
            out=vbf[0:n, :], in_=vrow[0:n, :], func=AF.Copy), inc=cons)
        cv = cons.v
        for dt in range(8):
            sh.pt.group(
                [lambda e, dt=dt, n=nrows, p=sh.pt.ap, idn=sh.ident: e.transpose(
                    p[:, 0:n], vbf[0:n, dt * 128:(dt + 1) * 128], idn[0:n, 0:n])],
                [lambda e, p=sh.pt.ap, dt=dt, n=nrows, dfn=dst_fn: ecopy(
                    e, out=dfn(dt), in_=p[:, 0:n])],
                evac_eng="scalar", extra_waits=[(cons, cv)])

    loadT_small(W["R"], 8, lambda dt: rT[:, dt, 0:8])
    for e_ in range(E):
        loadT_small(W["A"][e_, :, :], 16, lambda dt, e_=e_: aT[:, dt, e_, :])
    for e_ in range(E):
        ctx.wait("sync", sh.pt.wr, sh.pt.wr.v)
        ctx.wait("sync", cons, cons.v)
        ctx.dma("sync", wld[0][:, 0:128].rearrange("p (a b) -> p a b", a=8),
                W["B"][e_, :, :].rearrange("(a p) r -> p a r", p=128), ld)
        ctx.wait("scalar", ld, ld.v)
        ctx.run("scalar", lambda e: e.activation(
            out=wbf[0][:, 0:128], in_=wld[0][:, 0:128], func=AF.Copy), inc=cons)
        cv = cons.v
        for ot in range(8):
            sh.pt.group(
                [lambda e, ot=ot, p=sh.pt.ap, idn=sh.ident: e.transpose(
                    p[0:16, 0:128], wbf[0][:, ot * 16:(ot + 1) * 16], idn[:, :])],
                [lambda e, p=sh.pt.ap, ot=ot, e_=e_: ecopy(
                    e, out=bT[0:16, e_, ot * 128:(ot + 1) * 128], in_=p[0:16, 0:128])],
                evac_eng="scalar", extra_waits=[(cons, cv)])
    pre_pb = sh.pb.ev.v
    pre_cons = cons.v

    stx0 = stx.v
    for c in range(NCH):
        b = c % 2
        if c >= 2:
            ctx.wait("sync", stx, stx0 + (c - 1) * 16)
        ctx.dma("sync", xs[b][:, :, :],
                x_in[c * CH:(c + 1) * CH, :].rearrange("(a p) d -> p a d", p=128), ldx)
        lv = ldx.v
        ptw = sh.pt.wr.v
        for tt in range(NTT):
            emit_ln_tile(ctx, xs[b][:, tt, :], xn[:, tt, :], sq, stat, D, sLN,
                         waits=[(ldx, lv), (sh.pt.wr, ptw), (sRes, sRes.v)], sv=svA, sh=sh)
        lnv = sLN.v
        for tt in range(NTT):
            sh.pt.group(
                [lambda e, s=xn, tt=tt, dt=dt, p=sh.pt.ap, idn=sh.ident: e.transpose(
                    p[:, dt * 128:(dt + 1) * 128], s[:, tt, dt * 128:(dt + 1) * 128], idn[:, :])
                 for dt in range(8)],
                [lambda e, p=sh.pt.ap, d=xnt, tt=tt: ecopy(
                    e, out=d[:, :, tt * 128:(tt + 1) * 128],
                    in_=p[:, :].rearrange("p (a b) -> p a b", a=8))],
                evac_eng="scalar", extra_waits=[(sLN, lnv)])
        xnt_t = sh.pt.ev.v
        for tt in range(NTT):
            sh.pb.group(
                [lambda e, dt=dt, tt=tt, p=sh.pb.ap: e.matmul(
                    p[:, 0:E], rT[:, dt, 0:8] if False else xnt[:, dt, tt * 128:(tt + 1) * 128],
                    rT[:, dt, 0:8], start=(dt == 0), stop=(dt == 7))
                 for dt in range(8)],
                [("vector", lambda e, p=sh.pb.ap, tt=tt: e.tensor_copy(
                    out=lg[:, tt, :], in_=p[:, 0:E]))],
                extra_waits=[(sh.pt.ev, xnt_t)],
                evac_extra_waits=[(sG, sG.v)])
            ctx.wait("vector", sh.pb.ev, sh.pb.ev.v)
            ctx.run("vector", lambda e, tt=tt: e.tensor_reduce(
                out=gst[:, 0:1], in_=lg[:, tt, :], axis=AX.X, op=ALU.max, negate=True), inc=sG)
            ctx.wait("scalar", sG, sG.v)
            ctx.run("scalar", lambda e, tt=tt: e.activation(
                out=ge[:, :], in_=lg[:, tt, :], func=AF.Exp, bias=gst[:, 0:1]), inc=sG)
            ctx.wait("vector", sG, sG.v)
            ctx.run("vector", lambda e: e.tensor_reduce(
                out=gst[:, 1:2], in_=ge[:, :], axis=AX.X, op=ALU.add), inc=sG)
            ctx.wait("vector", sG, sG.v)
            ctx.run("vector", lambda e: e.reciprocal(out=gst[:, 2:3], in_=gst[:, 1:2]), inc=sG)
            ctx.wait("vector", sG, sG.v)
            ctx.run("vector", lambda e: e.tensor_scalar(
                out=ge[:, :], in0=ge[:, :], scalar1=gst[:, 2:3], scalar2=None, op0=ALU.mult), inc=sG)
            ctx.wait("vector", sG, sG.v)
            ctx.run("vector", lambda e: e.tensor_reduce(
                out=gst[:, 3:4], in_=ge[:, :], axis=AX.X, op=ALU.max), inc=sG)
            ctx.wait("vector", sG, sG.v)
            ctx.run("vector", lambda e: e.tensor_single_scalar(
                out=gst[:, 4:5] if False else sq[:, 0:8], in_=ge[:, :], scalar=gst[:, 3:4], op=ALU.is_ge), inc=sG)
            ctx.wait("vector", sG, sG.v)
            ctx.run("vector", lambda e: e.scalar_tensor_tensor(
                out=sq[:, 8:16], in0=sq[:, 0:8], scalar=-1e30, in1=ge[:, :],
                op0=ALU.mult, op1=ALU.add), inc=sG)
            ctx.wait("vector", sG, sG.v)
            ctx.run("vector", lambda e: e.tensor_reduce(
                out=gst[:, 5:6], in_=sq[:, 8:16], axis=AX.X, op=ALU.max), inc=sG)
            ctx.wait("vector", sG, sG.v)
            ctx.run("vector", lambda e: e.tensor_single_scalar(
                out=sq[:, 0:8], in_=ge[:, :], scalar=gst[:, 5:6], op=ALU.is_ge), inc=sG)
            ctx.wait("vector", sG, sG.v)
            ctx.run("vector", lambda e: e.tensor_mul(
                out=ge[:, :], in0=ge[:, :], in1=sq[:, 0:8]), inc=sG)
            ctx.wait("vector", sG, sG.v)
            ctx.run("vector", lambda e: e.tensor_reduce(
                out=gst[:, 6:7], in_=ge[:, :], axis=AX.X, op=ALU.add), inc=sG)
            ctx.wait("vector", sG, sG.v)
            ctx.run("vector", lambda e: e.tensor_scalar(
                out=gst[:, 6:7], in0=gst[:, 6:7], scalar1=1e-6, scalar2=None, op0=ALU.add), inc=sG)
            ctx.wait("vector", sG, sG.v)
            ctx.run("vector", lambda e: e.reciprocal(out=gst[:, 7:8], in_=gst[:, 6:7]), inc=sG)
            ctx.wait("vector", sG, sG.v)
            ctx.run("vector", lambda e, tt=tt: e.tensor_scalar(
                out=gaten[:, tt, :], in0=ge[:, :], scalar1=gst[:, 7:8], scalar2=None, op0=ALU.mult), inc=sG)
            gv = sG.v
            for e_ in range(E):
                pkb = sh.pk[4 + e_ % 2]
                pkb.group(
                    [lambda e, dt=dt, tt=tt, e_=e_, p=pkb.ap: e.matmul(
                        p[:, 0:R], xnt[:, dt, tt * 128:(tt + 1) * 128], aT[:, dt, e_, :],
                        start=(dt == 0), stop=(dt == 7))
                     for dt in range(8)],
                    [lambda e, p=pkb.ap, tt=tt, e_=e_: e.activation(
                        out=gh[:, :], in_=p[:, 0:R], func=AF.Identity,
                        scale=gaten[:, tt, e_:e_ + 1])],
                    evac_eng="scalar",
                    extra_waits=[(sh.pt.ev, xnt_t)],
                    evac_extra_waits=[(sG, gv), (sh.pt.wr, sh.pt.wr.v)])
                ghv = pkb.ev.v
                sh.pt.group(
                    [lambda e, p=sh.pt.ap, idn=sh.ident: e.transpose(
                        p[0:16, 0:128], gh[:, :], idn[:, :])],
                    [lambda e, p=sh.pt.ap, e_=e_: ecopy(
                        e, out=ght[0:16, e_, :], in_=p[0:16, 0:128])],
                    evac_eng="scalar", extra_waits=[(pkb.ev, ghv)])
            ghtv = sh.pt.ev.v
            for oh in range(2):
                pkb = sh.pk[tt * 2 + oh]
                mms = [lambda e, dt=dt, tt=tt, oh=oh, p=pkb.ap: e.matmul(
                    p[:, 0:512], xnt[:, dt, tt * 128:(tt + 1) * 128], wT[:, dt, oh * 512:(oh + 1) * 512],
                    start=(dt == 0), stop=False) for dt in range(8)]
                mms += [lambda e, e_=e_, oh=oh, p=pkb.ap: e.matmul(
                    p[:, 0:512], ght[0:16, e_, :], bT[0:16, e_, oh * 512:(oh + 1) * 512],
                    start=False, stop=(e_ == E - 1)) for e_ in range(E)]
                pkb.group(
                    mms,
                    [("vector", lambda e, p=pkb.ap, tt=tt, oh=oh: e.tensor_tensor(
                        out=osb[:, tt, oh * 512:(oh + 1) * 512], in0=p[:, 0:512],
                        in1=bB[:, oh * 512:(oh + 1) * 512], op=ALU.add))],
                    extra_waits=[(sh.pt.ev, ghtv)],
                    evac_extra_waits=[(sh.pb.ev, pre_pb), (sLN, sLN.v)])
        for tt in range(NTT):
            w = [(sh.pk[tt * 2].ev, sh.pk[tt * 2].ev.v),
                 (sh.pk[tt * 2 + 1].ev, sh.pk[tt * 2 + 1].ev.v), (sRes, sRes.v)]
            emit_postln_resid(ctx, osb[:, tt, :], xs[b][:, tt, :], sq, stat, D, sRes,
                              waits=w, sv=svB, sln=sLN, sh=sh)
        rv = sRes.v
        ctx.wait("gpsimd", sRes, rv)
        ctx.dma("gpsimd", x_out[c * CH:(c + 1) * CH, :].rearrange("(a p) d -> p a d", p=128),
                xs[b][:, :, :], stx)
    ctx.wait("gpsimd", stx, stx.v)
    return stx


def emit_loadT_small(ctx, sh, src_ap, nrows, D, dst_fn, vrow, vbf, ld, cons):
    ctx.wait("sync", sh.pt.wr, sh.pt.wr.v)
    ctx.wait("sync", cons, cons.v)
    ctx.dma("sync", vrow[0:nrows, 0:D], src_ap, ld)
    ctx.wait("scalar", ld, ld.v)
    ctx.run("scalar", lambda e, n=nrows, D=D: e.activation(
        out=vbf[0:n, 0:D], in_=vrow[0:n, 0:D], func=AF.Copy), inc=cons)
    cv = cons.v
    for dt in range(D // 128):
        sh.pt.group(
            [lambda e, dt=dt, n=nrows, p=sh.pt.ap, idn=sh.ident: e.transpose(
                p[:, 0:n], vbf[0:n, dt * 128:(dt + 1) * 128], idn[0:n, 0:n])],
            [lambda e, p=sh.pt.ap, dt=dt, n=nrows, dfn=dst_fn: ecopy(
                e, out=dfn(dt), in_=p[:, 0:n])],
            evac_eng="scalar", extra_waits=[(cons, cv)])


def emit_l1_ffn(ctx, sh, es, x_in, x_out, W, name="m1", nexp=64, nchunks=4):
    nc = ctx.nc
    T, D, E, I = 4096, 1024, 64, 512
    CH2 = 1024
    NCH = nchunks
    NTT = CH2 // 128
    NQ = CH2 // 512

    egT_d = nc.dram_tensor(name + "egT", [E, 8, 128, I], BF16).ap()
    euT_d = nc.dram_tensor(name + "euT", [E, 8, 128, I], BF16).ap()
    edT_d = nc.dram_tensor(name + "edT", [E, 4, 128, D], BF16).ap()
    sgT_d = nc.dram_tensor(name + "sgT", [8, 128, D], BF16).ap()
    suT_d = nc.dram_tensor(name + "suT", [8, 128, D], BF16).ap()
    sdT_d = nc.dram_tensor(name + "sdT", [8, 128, D], BF16).ap()

    slab = es.enter_context(nc.sbuf_tensor(name + "slab", [128, 24576], BF16))

    def slabv(off, a, b):
        return slab[:, off:off + a * b].rearrange("p (a b) -> p a b", a=a)

    gwT = es.enter_context(nc.sbuf_tensor(name + "gwT", [128, 8, E], BF16))
    gbB = es.enter_context(nc.sbuf_tensor(name + "gbB", [128, E], F32))
    wld = [es.enter_context(nc.sbuf_tensor(name + f"wld{i}", [128, 1024], F32)) for i in range(2)]
    wbf = [es.enter_context(nc.sbuf_tensor(name + f"wbf{i}", [128, 1024], BF16)) for i in range(2)]
    vrow = es.enter_context(nc.sbuf_tensor(name + "vrow", [128, 1024], F32))
    vbf = es.enter_context(nc.sbuf_tensor(name + "vbf", [128, 1024], BF16))
    tmprow = es.enter_context(nc.sbuf_tensor(name + "tmprow", [1, 1024], F32))
    xs = es.enter_context(nc.sbuf_tensor(name + "xs", [128, 4, 1024], F32))
    xn = es.enter_context(nc.sbuf_tensor(name + "xn", [128, 4, 1024], BF16))
    xnt = es.enter_context(nc.sbuf_tensor(name + "xnt", [128, 8, CH2], BF16))
    hsT = es.enter_context(nc.sbuf_tensor(name + "hsT", [128, 8, CH2], BF16))
    hsg = es.enter_context(nc.sbuf_tensor(name + "hsg", [128, 512], BF16))
    acc = es.enter_context(nc.sbuf_tensor(name + "acc", [128, NTT, 1024], F32))
    lgs = es.enter_context(nc.sbuf_tensor(name + "lgs", [128, E], F32))
    msk = es.enter_context(nc.sbuf_tensor(name + "msk", [128, E], F32))
    ms2 = es.enter_context(nc.sbuf_tensor(name + "ms2", [128, E], F32))
    meq = es.enter_context(nc.sbuf_tensor(name + "meq", [128, E], F32))
    gsb = es.enter_context(nc.sbuf_tensor(name + "gsb", [128, 24], F32))
    comb = es.enter_context(nc.sbuf_tensor(name + "comb", [128, NTT, E], F32))
    sq = es.enter_context(nc.sbuf_tensor(name + "sq", [128, 1024], F32))
    stat = es.enter_context(nc.sbuf_tensor(name + "stat", [128, 8], F32))

    ld = ctx.sem(name + "ld")
    cons = ctx.sem(name + "cons")
    ldw = ctx.sem(name + "ldw")
    stp = ctx.sem(name + "stp")
    ldx = ctx.sem(name + "ldx")
    stx = ctx.sem(name + "stx")
    sLN = ctx.sem(name + "sLN")
    sRes = ctx.sem(name + "sRes")
    svA = ctx.sem(name + "svA")
    svB = ctx.sem(name + "svB")
    sG = ctx.sem(name + "sG")
    sAcc = ctx.sem(name + "sAcc")
    sXT = ctx.sem(name + "sXT")

    ctx.meta = getattr(ctx, "meta", {})

    def vstep(fn):
        ctx.wait("vector", sG, sG.v)
        ctx.run("vector", fn, inc=sG)

    emit_loadT_small(ctx, sh, W["gate_w"], 64, D, lambda dt: gwT[:, dt, :], vrow, vbf, ld, cons)
    emit_bcast_row(ctx, sh, W["gate_bias"], E, gbB, ld, tmprow)

    def prep_to_hbm(src_ap, F, Dd, dram_view, ndt):
        emit_loadT(ctx, sh, src_ap, F, Dd, slabv(0, ndt, F), wld, wbf, ld, cons)
        ctx.wait("gpsimd", sh.pt.ev, sh.pt.ev.v)
        ctx.dma("gpsimd", dram_view, slabv(0, ndt, F), stp)
        ctx.meta["l1_slab_guard"] = stp.v

    def slab_guarded_loadT(src_ap, F, Dd, ndt):
        g = ctx.meta.get("l1_slab_guard")
        if g is not None:
            ctx.wait("scalar", stp, g)
        emit_loadT(ctx, sh, src_ap, F, Dd, slabv(0, ndt, F), wld, wbf, ld, cons)

    def prep2(src_ap, F, Dd, dram_view, ndt):
        g = ctx.meta.get("l1_slab_guard")
        if g is not None:
            ctx.wait("scalar", stp, g)
        emit_loadT(ctx, sh, src_ap, F, Dd, slabv(0, ndt, F), wld, wbf, ld, cons)
        ctx.wait("gpsimd", sh.pt.ev, sh.pt.ev.v)
        ctx.dma("gpsimd", dram_view, slabv(0, ndt, F), stp)
        ctx.meta["l1_slab_guard"] = stp.v

    prep2(W["sg_w"], D, D, sgT_d.rearrange("a p d -> p a d"), 8)
    prep2(W["su_w"], D, D, suT_d.rearrange("a p d -> p a d"), 8)
    prep2(W["sd_w"], D, D, sdT_d.rearrange("a p d -> p a d"), 8)
    for e_ in range(nexp):
        prep2(W["eg_w"][e_], I, D, egT_d[e_].rearrange("a p d -> p a d"), 8)
        prep2(W["eu_w"][e_], I, D, euT_d[e_].rearrange("a p d -> p a d"), 8)
        prep2(W["ed_w"][e_], D, I, edT_d[e_].rearrange("a p d -> p a d"), 4)
    prep_done_stp = stp.v
    pre_cons = cons.v

    wsg = slabv(0, 8, 1024)
    wsu = slabv(8192, 8, 1024)
    wsd = slabv(16384, 8, 1024)

    def exview(slot, which):
        off = [0, 12288][slot] + {"eg": 0, "eu": 4096, "ed": 8192}[which]
        if which == "ed":
            return slabv(off, 4, 1024)
        return slabv(off, 8, 512)

    for c in range(NCH):
        for scc in range(2):
            ctx.wait("sync", sLN, sLN.v)
            ctx.wait("sync", sRes, sRes.v)
            ctx.wait("sync", stx, stx.v)
            ctx.dma("sync", xs[:, :, :],
                    x_in[c * CH2 + scc * 512: c * CH2 + (scc + 1) * 512, :]
                    .rearrange("(a p) d -> p a d", p=128), ldx)
            lv = ldx.v
            ptw = sh.pt.wr.v
            for tt in range(4):
                emit_ln_tile(ctx, xs[:, tt, :], xn[:, tt, :], sq, stat, D, sLN,
                             waits=[(ldx, lv), (sh.pt.wr, ptw), (sRes, sRes.v)], sv=svA, sh=sh)
            lnv = sLN.v
            for tt in range(4):
                sh.pt.group(
                    [lambda e, s=xn, tt=tt, dt=dt, p=sh.pt.ap, idn=sh.ident: e.transpose(
                        p[:, dt * 128:(dt + 1) * 128], s[:, tt, dt * 128:(dt + 1) * 128], idn[:, :])
                     for dt in range(8)],
                    [lambda e, p=sh.pt.ap, d=xnt, tt=tt, scc=scc: ecopy(
                        e, out=d[:, :, (scc * 4 + tt) * 128:(scc * 4 + tt + 1) * 128],
                        in_=p[:, :].rearrange("p (a b) -> p a b", a=8))],
                    evac_eng="scalar", extra_waits=[(sLN, lnv)],
                    evac_extra_waits=[(sXT, sXT.v)])
        xnt_t = sh.pt.ev.v
        for tt in range(NTT):
            sh.pb.group(
                [lambda e, dt=dt, tt=tt, p=sh.pb.ap: e.matmul(
                    p[:, 0:E], xnt[:, dt, tt * 128:(tt + 1) * 128], gwT[:, dt, :],
                    start=(dt == 0), stop=(dt == 7))
                 for dt in range(8)],
                [lambda e, p=sh.pb.ap: e.activation(
                    out=lgs[:, :], in_=p[:, 0:E], func=AF.Sigmoid)],
                evac_eng="scalar",
                extra_waits=[(sh.pt.ev, xnt_t)],
                evac_extra_waits=[(sG, sG.v), (cons, pre_cons)])
            ctx.wait("vector", sh.pb.ev, sh.pb.ev.v)
            ctx.run("vector", lambda e: e.tensor_add(out=lgs[:, :], in0=lgs[:, :], in1=gbB[:, :]), inc=sG)
            ctx.wait("vector", sG, sG.v)
            for g in range(8):
                last = g == 7
                ctx.run("vector", lambda e, g=g: e.tensor_reduce(
                    out=gsb[:, g:g + 1], in_=lgs[:, g * 8:(g + 1) * 8], axis=AX.X, op=ALU.max),
                    inc=sG if last else None)
            vstep(lambda e: e.tensor_copy(out=gsb[:, 8:16], in_=gsb[:, 0:8]))
            for it4 in range(4):
                vstep(lambda e: e.tensor_reduce(
                    out=gsb[:, 16:17], in_=gsb[:, 8:16], axis=AX.X, op=ALU.max))
                if it4 < 3:
                    vstep(lambda e: e.tensor_single_scalar(
                        out=gsb[:, 17:18] if False else meq[:, 0:8], in_=gsb[:, 8:16],
                        scalar=gsb[:, 16:17], op=ALU.is_ge))
                    vstep(lambda e: e.scalar_tensor_tensor(
                        out=gsb[:, 8:16], in0=meq[:, 0:8], scalar=-1e30, in1=gsb[:, 8:16],
                        op0=ALU.mult, op1=ALU.add))
            vstep(lambda e: e.tensor_single_scalar(
                out=gsb[:, 8:16], in_=gsb[:, 0:8], scalar=gsb[:, 16:17], op=ALU.is_ge))
            ctx.wait("vector", sG, sG.v)
            for g in range(8):
                last = g == 7
                ctx.run("vector", lambda e, g=g: e.tensor_scalar(
                    out=msk[:, g * 8:(g + 1) * 8], in0=lgs[:, g * 8:(g + 1) * 8],
                    scalar1=gsb[:, 8 + g:9 + g], scalar2=None, op0=ALU.mult),
                    inc=sG if last else None)
            vstep(lambda e: e.tensor_copy(out=ms2[:, :], in_=msk[:, :]))
            for it6 in range(6):
                vstep(lambda e: e.tensor_reduce(
                    out=gsb[:, 17:18], in_=msk[:, :], axis=AX.X, op=ALU.max))
                if it6 < 5:
                    vstep(lambda e: e.tensor_single_scalar(
                        out=meq[:, :], in_=msk[:, :], scalar=gsb[:, 17:18], op=ALU.is_ge))
                    vstep(lambda e: e.scalar_tensor_tensor(
                        out=msk[:, :], in0=meq[:, :], scalar=-1e30, in1=msk[:, :],
                        op0=ALU.mult, op1=ALU.add))
            vstep(lambda e: e.tensor_scalar(
                out=gsb[:, 18:19], in0=gsb[:, 17:18], scalar1=0.0, scalar2=None, op0=ALU.max))
            vstep(lambda e: e.tensor_single_scalar(
                out=meq[:, :], in_=ms2[:, :], scalar=gsb[:, 18:19], op=ALU.is_ge))
            vstep(lambda e: e.tensor_mul(out=ms2[:, :], in0=ms2[:, :], in1=meq[:, :]))
            vstep(lambda e: e.tensor_reduce(
                out=gsb[:, 19:20], in_=ms2[:, :], axis=AX.X, op=ALU.add))
            vstep(lambda e: e.tensor_scalar(
                out=gsb[:, 20:21], in0=gsb[:, 19:20], scalar1=1e-8, scalar2=None, op0=ALU.add))
            vstep(lambda e: e.reciprocal(out=gsb[:, 21:22], in_=gsb[:, 20:21]))
            vstep(lambda e, tt=tt: e.tensor_scalar(
                out=comb[:, tt, :], in0=ms2[:, :], scalar1=gsb[:, 21:22], scalar2=None, op0=ALU.mult))
        comb_t = sG.v
        ldw_guard = sXT.v
        ctx.wait("sync", sXT, ldw_guard)
        if c == 0:
            ctx.wait("sync", stp, prep_done_stp)
        ctx.dma("sync", wsg, sgT_d.rearrange("a p d -> p a d"), ldw)
        ctx.dma("sync", wsu, suT_d.rearrange("a p d -> p a d"), ldw)
        ldw_sh = ldw.v
        for ft in range(8):
            for nq in range(NQ):
                pk4, pk5 = sh.pk[4], sh.pk[5]
                pk4.group(
                    [lambda e, dt=dt, ft=ft, nq=nq, p=pk4.ap: e.matmul(
                        p[:, 0:512], wsg[:, dt, ft * 128:(ft + 1) * 128],
                        xnt[:, dt, nq * 512:(nq + 1) * 512], start=(dt == 0), stop=(dt == 7))
                     for dt in range(8)],
                    [lambda e, p=pk4.ap: e.activation(
                        out=hsg[:, :], in_=p[:, 0:512], func=AF.Silu)],
                    evac_eng="scalar",
                    extra_waits=[(ldw, ldw_sh), (sh.pt.ev, xnt_t)],
                    evac_extra_waits=[(pk5.ev, pk5.ev.v)])
                hv = pk4.ev.v
                pk5.group(
                    [lambda e, dt=dt, ft=ft, nq=nq, p=pk5.ap: e.matmul(
                        p[:, 0:512], wsu[:, dt, ft * 128:(ft + 1) * 128],
                        xnt[:, dt, nq * 512:(nq + 1) * 512], start=(dt == 0), stop=(dt == 7))
                     for dt in range(8)],
                    [("vector", lambda e, p=pk5.ap, ft=ft, nq=nq: e.tensor_tensor(
                        out=hsT[:, ft, nq * 512:(nq + 1) * 512], in0=p[:, 0:512],
                        in1=hsg[:, :], op=ALU.mult))],
                    extra_waits=[(ldw, ldw_sh)],
                    evac_extra_waits=[(pk4.ev, hv), (sXT, ldw_guard)])
        ctx.dma("sync", wsd, sdT_d.rearrange("a p d -> p a d"), ldw)
        ldw_sd = ldw.v
        hsT_t = sh.pk[5].ev.v
        for tt in range(NTT):
            for oh in range(2):
                pkb = sh.pk[(tt * 2 + oh) % 4]
                pkb.group(
                    [lambda e, ft=ft, tt=tt, oh=oh, p=pkb.ap: e.matmul(
                        p[:, 0:512], hsT[:, ft, tt * 128:(tt + 1) * 128],
                        wsd[:, ft, oh * 512:(oh + 1) * 512], start=(ft == 0), stop=(ft == 7))
                     for ft in range(8)],
                    [("vector", lambda e, p=pkb.ap, tt=tt, oh=oh: e.tensor_copy(
                        out=acc[:, tt, oh * 512:(oh + 1) * 512], in0=p[:, 0:512])
                      if False else e.tensor_copy(
                        out=acc[:, tt, oh * 512:(oh + 1) * 512], in_=p[:, 0:512]))],
                    extra_waits=[(ldw, ldw_sd), (sh.pk[5].ev, hsT_t)],
                    evac_extra_waits=[(sLN, sLN.v)])
        ctx.run("tensor", lambda e: e.nop(), inc=sXT)
        for e_ in range(nexp):
            slot = e_ % 2
            ctx.wait("sync", sXT, sXT.v if e_ == 0 else sXT.v - 1)
            ctx.dma("sync", exview(slot, "eg"), egT_d[e_].rearrange("a p d -> p a d"), ldw)
            ctx.dma("sync", exview(slot, "eu"), euT_d[e_].rearrange("a p d -> p a d"), ldw)
            ctx.dma("sync", exview(slot, "ed"), edT_d[e_].rearrange("a p d -> p a d"), ldw)
            ldw_e = ldw.v
            for it in range(4):
                for nq in range(NQ):
                    pk4, pk5 = sh.pk[4], sh.pk[5]
                    pk4.group(
                        [lambda e, dt=dt, it=it, nq=nq, slot=slot, p=pk4.ap: e.matmul(
                            p[:, 0:512], exview(slot, "eg")[:, dt, it * 128:(it + 1) * 128],
                            xnt[:, dt, nq * 512:(nq + 1) * 512], start=(dt == 0), stop=(dt == 7))
                         for dt in range(8)],
                        [lambda e, p=pk4.ap: e.activation(
                            out=hsg[:, :], in_=p[:, 0:512], func=AF.Silu)],
                        evac_eng="scalar",
                        extra_waits=[(ldw, ldw_e)],
                        evac_extra_waits=[(pk5.ev, pk5.ev.v)])
                    hv = pk4.ev.v
                    pk5.group(
                        [lambda e, dt=dt, it=it, nq=nq, slot=slot, p=pk5.ap: e.matmul(
                            p[:, 0:512], exview(slot, "eu")[:, dt, it * 128:(it + 1) * 128],
                            xnt[:, dt, nq * 512:(nq + 1) * 512], start=(dt == 0), stop=(dt == 7))
                         for dt in range(8)],
                        [("vector", lambda e, p=pk5.ap, it=it, nq=nq: e.tensor_tensor(
                            out=hsT[:, it, nq * 512:(nq + 1) * 512], in0=p[:, 0:512],
                            in1=hsg[:, :], op=ALU.mult))],
                        extra_waits=[(ldw, ldw_e)],
                        evac_extra_waits=[(pk4.ev, hv), (sXT, sXT.v)])
            hsTe_t = sh.pk[5].ev.v
            for tt in range(NTT):
                for oh in range(2):
                    pkb = sh.pk[(tt * 2 + oh) % 4]
                    pkb.group(
                        [lambda e, it=it, tt=tt, oh=oh, slot=slot, p=pkb.ap: e.matmul(
                            p[:, 0:512], hsT[:, it, tt * 128:(tt + 1) * 128],
                            exview(slot, "ed")[:, it, oh * 512:(oh + 1) * 512],
                            start=(it == 0), stop=(it == 3))
                         for it in range(4)],
                        [("vector", lambda e, p=pkb.ap, tt=tt, oh=oh, e_=e_: e.scalar_tensor_tensor(
                            out=acc[:, tt, oh * 512:(oh + 1) * 512], in0=p[:, 0:512],
                            scalar=comb[:, tt, e_:e_ + 1], in1=acc[:, tt, oh * 512:(oh + 1) * 512],
                            op0=ALU.mult, op1=ALU.add))],
                        extra_waits=[(sh.pk[5].ev, hsTe_t)],
                        evac_extra_waits=[(sG, comb_t)])
            ctx.run("tensor", lambda e: e.nop(), inc=sXT)
        ctx.run("vector", lambda e: e.engine_nop(), inc=sAcc)
        accv = sAcc.v
        for scc in range(2):
            ctx.wait("sync", sLN, sLN.v)
            ctx.wait("sync", sRes, sRes.v)
            ctx.wait("sync", stx, stx.v)
            ctx.dma("sync", xs[:, :, :],
                    x_in[c * CH2 + scc * 512: c * CH2 + (scc + 1) * 512, :]
                    .rearrange("(a p) d -> p a d", p=128), ldx)
            lv = ldx.v
            for tt in range(4):
                w = [(ldx, lv), (sAcc, accv), (sRes, sRes.v)]
                emit_postln_resid(ctx, acc[:, scc * 4 + tt, :], xs[:, tt, :], sq, stat, D, sRes,
                                  waits=w, sv=svB, sln=sLN, sh=sh)
            rv = sRes.v
            ctx.wait("gpsimd", sRes, rv)
            ctx.dma("gpsimd",
                    x_out[c * CH2 + scc * 512: c * CH2 + (scc + 1) * 512, :]
                    .rearrange("(a p) d -> p a d", p=128), xs[:, :, :], stx)
    ctx.wait("gpsimd", stx, stx.v)
    return stx


def emit_l2_attn(ctx, sh, es, x_in, x_out, W, name="a2"):
    nc = ctx.nc
    T, D, H, DH = 4096, 1024, 16, 64
    CH, NTT = 256, 2
    NCH = T // CH
    MAXLOG = float(np.log(1.0 / 0.01))

    qT_d = nc.dram_tensor(name + "qT", [1024, T], BF16).ap()
    kT_d = nc.dram_tensor(name + "kT", [1024, T], BF16).ap()
    v_d = nc.dram_tensor(name + "v", [T, 1024], BF16).ap()
    oT_d = nc.dram_tensor(name + "oT", [1024, T], BF16).ap()

    wqkv = es.enter_context(nc.sbuf_tensor(name + "wqkv", [128, 8, 3072], BF16))
    wout = es.enter_context(nc.sbuf_tensor(name + "wout", [128, 8, 1024], BF16))
    wld = [es.enter_context(nc.sbuf_tensor(name + f"wld{i}", [128, 1536], F32)) for i in range(2)]
    boB = es.enter_context(nc.sbuf_tensor(name + "boB", [128, 1024], F32))
    tmprow = es.enter_context(nc.sbuf_tensor(name + "tmprow", [1, 1024], F32))
    lsrow = es.enter_context(nc.sbuf_tensor(name + "lsr", [16, 2], F32))
    scB = es.enter_context(nc.sbuf_tensor(name + "scB", [128, 8], F32))
    negsc = es.enter_context(nc.sbuf_tensor(name + "negsc", [128, 16], F32))
    screp = es.enter_context(nc.sbuf_tensor(name + "screp", [1, 16], F32))
    xs = [es.enter_context(nc.sbuf_tensor(name + f"xs{i}", [128, NTT, 1024], F32)) for i in range(2)]
    xn = es.enter_context(nc.sbuf_tensor(name + "xn", [128, NTT, 1024], BF16))
    xnt = es.enter_context(nc.sbuf_tensor(name + "xnt", [128, 8, CH], BF16))
    rawT = es.enter_context(nc.sbuf_tensor(name + "rawT", [128, CH], BF16))
    sqf = es.enter_context(nc.sbuf_tensor(name + "sqf", [128, CH], F32))
    rn2 = es.enter_context(nc.sbuf_tensor(name + "rn2", [2, CH], F32))
    nrmT = es.enter_context(nc.sbuf_tensor(name + "nrmT", [128, CH], BF16))
    vtt = es.enter_context(nc.sbuf_tensor(name + "vtt", [128, NTT, 1024], BF16))
    sq = es.enter_context(nc.sbuf_tensor(name + "sq", [128, 1024], F32))
    stat = es.enter_context(nc.sbuf_tensor(name + "stat", [128, 8], F32))

    ld = ctx.sem(name + "ld")
    cons = ctx.sem(name + "cons")
    ldx = ctx.sem(name + "ldx")
    stx = ctx.sem(name + "stx")
    sta = ctx.sem(name + "sta")
    sLN = ctx.sem(name + "sLN")
    sRes = ctx.sem(name + "sRes")
    svA = ctx.sem(name + "svA")
    svB = ctx.sem(name + "svB")
    sN = ctx.sem(name + "sN")

    ctx.meta = getattr(ctx, "meta", {})

    for i in range(16):
        b = i % 2
        dt, half = i // 2, i % 2
        if i >= 2:
            ctx.wait("sync", cons, cons.v - 1)
        ctx.dma("sync", wld[b][:, :], W["w_qkv"][dt * 128:(dt + 1) * 128, half * 1536:(half + 1) * 1536], ld)
        ctx.wait("scalar", ld, ld.v)
        ctx.run("scalar", lambda e, s=wld[b], dt=dt, half=half: e.activation(
            out=wqkv[:, dt, half * 1536:(half + 1) * 1536], in_=s[:, :], func=AF.Copy), inc=cons)
    for dt in range(8):
        b = dt % 2
        ctx.wait("sync", cons, cons.v - 1)
        ctx.dma("sync", wld[b][:, 0:1024], W["w_out"][dt * 128:(dt + 1) * 128, :], ld)
        ctx.wait("scalar", ld, ld.v)
        ctx.run("scalar", lambda e, s=wld[b], dt=dt: e.activation(
            out=wout[:, dt, :], in_=s[:, 0:1024], func=AF.Copy), inc=cons)
    bo_t = emit_bcast_row(ctx, sh, W["b_out"], 1024, boB, ld, tmprow)
    ctx.dma("sync", lsrow[0:16, 0:1], W["logit_scale"].rearrange("a b c -> a (b c)"), ld)
    ctx.wait("vector", ld, ld.v)
    ctx.run("vector", lambda e: e.tensor_scalar(
        out=lsrow[0:16, 1:2], in0=lsrow[0:16, 0:1], scalar1=MAXLOG, scalar2=None, op0=ALU.min), inc=sN)
    ctx.wait("scalar", sN, sN.v)
    ctx.run("scalar", lambda e: e.activation(
        out=lsrow[0:16, 0:1], in_=lsrow[0:16, 1:2], func=AF.Exp), inc=sN)
    scv = sN.v
    for ft in range(8):
        sh.pb.group(
            [lambda e, ft=ft, p=sh.pb.ap: e.matmul(
                p[:, 0:1], sh.sel2[:, ft * 128:(ft + 1) * 128], lsrow[0:16, 0:1],
                start=True, stop=True)],
            [("vector", lambda e, p=sh.pb.ap, ft=ft: e.tensor_copy(
                out=scB[:, ft:ft + 1], in_=p[:, 0:1]))],
            extra_waits=[(sN, scv)])
    sh.pb.group(
        [lambda e, p=sh.pb.ap, idn=sh.identf32: e.transpose(
            p[0:1, 0:16], lsrow[0:16, 0:1], idn[0:16, 0:16])],
        [("vector", lambda e, p=sh.pb.ap: e.tensor_scalar(
            out=screp[0:1, :], in0=p[0:1, 0:16], scalar1=-1.0, scalar2=None, op0=ALU.mult))],
        extra_waits=[(sN, scv)])
    sh.pb.group(
        [lambda e, p=sh.pb.ap, of=sh.onesf32: e.matmul(
            p[:, 0:16], of[0:1, :], screp[0:1, :], start=True, stop=True)],
        [("vector", lambda e, p=sh.pb.ap: e.tensor_copy(out=negsc[:, :], in_=p[:, 0:16]))])
    pre_pb = sh.pb.ev.v
    pre_cons = cons.v

    for c in range(NCH):
        b = c % 2
        ctx.wait("sync", sLN, sLN.v)
        ctx.dma("sync", xs[b][:, :, :],
                x_in[c * CH:(c + 1) * CH, :].rearrange("(a p) d -> p a d", p=128), ldx)
        lv = ldx.v
        ptw = sh.pt.wr.v
        for tt in range(NTT):
            emit_ln_tile(ctx, xs[b][:, tt, :], xn[:, tt, :], sq, stat, D, sLN,
                         waits=[(ldx, lv), (sh.pt.wr, ptw), (sN, sN.v)], sv=svA, sh=sh)
        lnv = sLN.v
        for tt in range(NTT):
            sh.pt.group(
                [lambda e, s=xn, tt=tt, dt=dt, p=sh.pt.ap, idn=sh.ident: e.transpose(
                    p[:, dt * 128:(dt + 1) * 128], s[:, tt, dt * 128:(dt + 1) * 128], idn[:, :])
                 for dt in range(8)],
                [lambda e, p=sh.pt.ap, d=xnt, tt=tt: ecopy(
                    e, out=d[:, :, tt * 128:(tt + 1) * 128],
                    in_=p[:, :].rearrange("p (a b) -> p a b", a=8))],
                evac_eng="scalar", extra_waits=[(sLN, lnv)])
        xnt_t = sh.pt.ev.v
        for which, base in (("q", 0), ("k", 8)):
            for ft in range(8):
                pkb = sh.pk[4 + ft % 2]
                pkb.group(
                    [lambda e, dt=dt, ft=ft, base=base, p=pkb.ap: e.matmul(
                        p[:, 0:CH], wqkv[:, dt, (base + ft) * 128:(base + ft + 1) * 128],
                        xnt[:, dt, :], start=(dt == 0), stop=(dt == 7))
                     for dt in range(8)],
                    [("scalar", lambda e, p=pkb.ap: e.activation(
                        out=rawT[:, :], in_=p[:, 0:CH], func=AF.Copy)),
                     ("scalar", lambda e, p=pkb.ap: e.activation(
                        out=sqf[:, :], in_=p[:, 0:CH], func=AF.Square))],
                    extra_waits=[(sh.pt.ev, xnt_t), (cons, pre_cons)],
                    evac_extra_waits=[(sta, sta.v), (sN, sN.v), (sh.pb.ev, sh.pb.ev.v)])
                sqv = pkb.ev.v
                sh.pb.group(
                    [lambda e, p=sh.pb.ap: e.matmul(
                        p[0:2, 0:CH], sh.bo2[:, :], sqf[:, :], start=True, stop=True)],
                    [("scalar", lambda e, p=sh.pb.ap: e.activation(
                        out=rn2[0:2, :], in_=p[0:2, 0:CH], func=AF.Sqrt))],
                    extra_waits=[(pkb.ev, sqv)])
                ctx.wait("vector", sh.pb.ev, sh.pb.ev.v)
                ctx.run("vector", lambda e: e.tensor_scalar(
                    out=rn2[0:2, :], in0=rn2[0:2, :], scalar1=1e-12, scalar2=None, op0=ALU.max), inc=sN)
                ctx.wait("vector", sN, sN.v)
                ctx.run("vector", lambda e: e.reciprocal(out=rn2[0:2, :], in_=rn2[0:2, :]), inc=sN)
                rnv = sN.v
                sh.pb.group(
                    [lambda e, p=sh.pb.ap: e.matmul(
                        p[:, 0:CH], sh.bc2[:, :], rn2[0:2, :], start=True, stop=True)],
                    [("vector", (lambda e, p=sh.pb.ap, ft=ft: e.scalar_tensor_tensor(
                        out=nrmT[:, :], in0=rawT[:, :], scalar=scB[:, ft:ft + 1], in1=p[:, 0:CH],
                        op0=ALU.mult, op1=ALU.mult)) if which == "q" else
                     (lambda e, p=sh.pb.ap: e.tensor_tensor(
                        out=nrmT[:, :], in0=rawT[:, :], in1=p[:, 0:CH], op=ALU.mult)))],
                    extra_waits=[(sN, rnv)],
                    evac_extra_waits=[(sta, sta.v)])
                nv = sh.pb.ev.v
                ctx.wait("gpsimd", sh.pb.ev, nv)
                dst = qT_d if which == "q" else kT_d
                ctx.dma("gpsimd", dst[ft * 128:(ft + 1) * 128, c * CH:(c + 1) * CH], nrmT[:, :], sta)
        for ft in range(8):
            pkb = sh.pk[4 + ft % 2]
            pkb.group(
                [lambda e, dt=dt, ft=ft, p=pkb.ap: e.matmul(
                    p[:, 0:CH], wqkv[:, dt, (16 + ft) * 128:(16 + ft + 1) * 128],
                    xnt[:, dt, :], start=(dt == 0), stop=(dt == 7))
                 for dt in range(8)],
                [("scalar", lambda e, p=pkb.ap: e.activation(
                    out=rawT[:, :], in_=p[:, 0:CH], func=AF.Copy))],
                extra_waits=[(sh.pt.ev, xnt_t), (cons, pre_cons)],
                evac_extra_waits=[(sh.pt.wr, sh.pt.wr.v), (sh.pb.ev, sh.pb.ev.v)])
            rv_ = pkb.ev.v
            for tt in range(NTT):
                sh.pt.group(
                    [lambda e, tt=tt, p=sh.pt.ap, idn=sh.ident: e.transpose(
                        p[:, 0:128], rawT[:, tt * 128:(tt + 1) * 128], idn[:, :])],
                    [lambda e, p=sh.pt.ap, tt=tt, ft=ft: ecopy(
                        e, out=vtt[:, tt, ft * 128:(ft + 1) * 128], in_=p[:, 0:128])],
                    evac_eng="scalar", extra_waits=[(pkb.ev, rv_)],
                    evac_extra_waits=[(sta, sta.v)])
        vt_t = sh.pt.ev.v
        ctx.wait("gpsimd", sh.pt.ev, vt_t)
        ctx.dma("gpsimd", v_d[c * CH:(c + 1) * CH, :].rearrange("(a p) d -> p a d", p=128),
                vtt[:, :, :], sta)
    phaseA_sta = sta.v

    knh = es.enter_context(nc.sbuf_tensor(name + "knh", [64, T], BF16))
    qth = es.enter_context(nc.sbuf_tensor(name + "qth", [64, 512], BF16))
    vh = [es.enter_context(nc.sbuf_tensor(name + f"vh{i}", [128, 32, 65], BF16)) for i in range(2)]
    expT = es.enter_context(nc.sbuf_tensor(name + "expT", [128, 32, 512], BF16))
    lrow = es.enter_context(nc.sbuf_tensor(name + "lrow", [1, 512], F32))
    linvB = es.enter_context(nc.sbuf_tensor(name + "linvB", [64, 512], F32))
    oTn = es.enter_context(nc.sbuf_tensor(name + "oTn", [64, 512], BF16))
    ldh = ctx.sem(name + "ldh")
    ctx.wait("vector", sh.pb.ev, pre_pb)
    ctx.run("vector", lambda e: e.memset(vh[0][:, :, 64:65], 1.0), inc=sN)
    ctx.run("vector", lambda e: e.memset(vh[1][:, :, 64:65], 1.0), inc=sN)
    pk2 = sh.pk[2]
    for h in range(H):
        b = h % 2
        ctx.wait("sync", sh.pk[0].wr, sh.pk[0].wr.v)
        ctx.wait("sync", sh.pk[1].wr, sh.pk[1].wr.v)
        if h == 0:
            ctx.wait("sync", sta, phaseA_sta)
        ctx.dma("sync", knh[0:64, :], kT_d[h * 64:(h + 1) * 64, :], ldh)
        ctx.wait("sync", pk2.wr, pk2.wr.v)
        for quarter in range(4):
            ctx.dma("sync", vh[b][:, quarter * 8:(quarter + 1) * 8, 0:64],
                    v_d[quarter * 1024:(quarter + 1) * 1024, h * 64:(h + 1) * 64]
                    .rearrange("(a p) d -> p a d", p=128), ldh)
        ldh_h = ldh.v
        for qc in range(8):
            ctx.wait("sync", sh.pk[0].wr, sh.pk[0].wr.v)
            ctx.wait("sync", sh.pk[1].wr, sh.pk[1].wr.v)
            ctx.dma("sync", qth[0:64, :], qT_d[h * 64:(h + 1) * 64, qc * 512:(qc + 1) * 512], ldh)
            ldh_q = ldh.v
            for kt in range(32):
                pkb = sh.pk[kt % 2]
                pkb.group(
                    [lambda e, kt=kt, p=pkb.ap: e.matmul(
                        p[:, 0:512], knh[0:64, kt * 128:(kt + 1) * 128], qth[0:64, :],
                        start=True, stop=True)],
                    [("scalar", lambda e, p=pkb.ap, kt=kt, h=h: e.activation(
                        out=expT[:, kt, :], in_=p[:, 0:512], func=AF.Exp,
                        bias=negsc[:, h:h + 1]))],
                    extra_waits=[(ldh, ldh_q)],
                    evac_extra_waits=[(pk2.wr, pk2.wr.v)])
            ctx.wait("tensor", pk2.ev, pk2.ev.v)
            ctx.wait("tensor", sh.pk[0].ev, sh.pk[0].ev.v)
            ctx.wait("tensor", sh.pk[1].ev, sh.pk[1].ev.v)
            ctx.wait("tensor", ldh, ldh_h)
            for kt in range(32):
                ctx.run("tensor", lambda e, kt=kt, b=b: e.matmul(
                    pk2.ap[0:65, 0:512], vh[b][:, kt, :], expT[:, kt, :],
                    start=(kt == 0), stop=(kt == 31)),
                    inc=pk2.wr if kt == 31 else None)
            ctx.wait("vector", pk2.wr, pk2.wr.v)
            ctx.wait("vector", sta, sta.v)
            ctx.run("vector", lambda e: e.reciprocal(out=lrow[0:1, :], in_=pk2.ap[64:65, 0:512]), inc=sN)
            lr_v = sN.v
            sh.pb.group(
                [lambda e, p=sh.pb.ap, of=sh.onesf32: e.matmul(
                    p[0:64, 0:512], of[0:1, 0:64], lrow[0:1, :], start=True, stop=True)],
                [("scalar", lambda e, p=sh.pb.ap: e.activation(
                    out=linvB[0:64, :], in_=p[0:64, 0:512], func=AF.Copy))],
                extra_waits=[(sN, lr_v)])
            ctx.wait("vector", sh.pb.ev, sh.pb.ev.v)
            ctx.run("vector", lambda e: e.tensor_tensor(
                out=oTn[0:64, :], in0=pk2.ap[0:64, 0:512], in1=linvB[0:64, :],
                op=ALU.mult), inc=pk2.ev)
            ctx.wait("gpsimd", pk2.ev, pk2.ev.v)
            ctx.dma("gpsimd", oT_d[h * 64:(h + 1) * 64, qc * 512:(qc + 1) * 512],
                    oTn[0:64, :], sta)
    phaseB_sta = sta.v

    oth = es.enter_context(nc.sbuf_tensor(name + "oth", [128, 8, CH], BF16))
    osb = es.enter_context(nc.sbuf_tensor(name + "osb", [128, NTT, 1024], F32))
    for c in range(NCH):
        b = c % 2
        ctx.wait("sync", stx, stx.v)
        ctx.wait("sync", sLN, sLN.v)
        if c == 0:
            ctx.wait("sync", sta, phaseB_sta)
        ctx.dma("sync", xs[b][:, :, :],
                x_in[c * CH:(c + 1) * CH, :].rearrange("(a p) d -> p a d", p=128), ldx)
        ctx.wait("sync", sh.pk[0].wr, sh.pk[0].wr.v)
        ctx.wait("sync", sh.pk[1].wr, sh.pk[1].wr.v)
        ctx.wait("sync", sh.pk[2].wr, sh.pk[2].wr.v)
        ctx.wait("sync", sh.pk[3].wr, sh.pk[3].wr.v)
        ctx.dma("sync", oth[:, :, :],
                oT_d[:, c * CH:(c + 1) * CH].rearrange("(a p) t -> p a t", p=128), ldx)
        lv = ldx.v
        for tt in range(NTT):
            for oh in range(2):
                pkb = sh.pk[tt * 2 + oh]
                pkb.group(
                    [lambda e, ft=ft, tt=tt, oh=oh, p=pkb.ap: e.matmul(
                        p[:, 0:512], oth[:, ft, tt * 128:(tt + 1) * 128],
                        wout[:, ft, oh * 512:(oh + 1) * 512], start=(ft == 0), stop=(ft == 7))
                     for ft in range(8)],
                    [("vector", lambda e, p=pkb.ap, tt=tt, oh=oh: e.tensor_tensor(
                        out=osb[:, tt, oh * 512:(oh + 1) * 512], in0=p[:, 0:512],
                        in1=boB[:, oh * 512:(oh + 1) * 512], op=ALU.add))],
                    extra_waits=[(ldx, lv), (cons, pre_cons)],
                    evac_extra_waits=[(sh.pb.ev, pre_pb), (sLN, sLN.v)])
        for tt in range(NTT):
            w = [(sh.pk[tt * 2].ev, sh.pk[tt * 2].ev.v),
                 (sh.pk[tt * 2 + 1].ev, sh.pk[tt * 2 + 1].ev.v), (sRes, sRes.v), (ldx, lv)]
            emit_postln_resid(ctx, osb[:, tt, :], xs[b][:, tt, :], sq, stat, D, sRes,
                              waits=w, sv=svB, sln=sLN, sh=sh)
        rv = sRes.v
        ctx.wait("gpsimd", sRes, rv)
        ctx.dma("gpsimd", x_out[c * CH:(c + 1) * CH, :].rearrange("(a p) d -> p a d", p=128),
                xs[b][:, :, :], stx)
    ctx.wait("gpsimd", stx, stx.v)
    return stx


def emit_l3_attn(ctx, sh, es, x_in, x_out, W, name="a3"):
    nc = ctx.nc
    T, D, H, DH = 4096, 1024, 8, 128
    CH, NTT = 256, 2
    NCH = T // CH
    SC = 1.0 / float(np.sqrt(128.0))
    LAMI = 0.8

    qT_d = nc.dram_tensor(name + "qT", [2048, T], BF16).ap()
    kT_d = nc.dram_tensor(name + "kT", [2048, T], BF16).ap()
    v_d = nc.dram_tensor(name + "v", [T, 2048], BF16).ap()
    oT_d = nc.dram_tensor(name + "oT", [2048, T], BF16).ap()

    xs = [es.enter_context(nc.sbuf_tensor(name + f"xs{i}", [128, NTT, 1024], F32)) for i in range(2)]
    xn = es.enter_context(nc.sbuf_tensor(name + "xn", [128, NTT, 1024], BF16))
    xnt = es.enter_context(nc.sbuf_tensor(name + "xnt", [128, 8, CH], BF16))
    sq = es.enter_context(nc.sbuf_tensor(name + "sq", [128, 1024], F32))
    stat = es.enter_context(nc.sbuf_tensor(name + "stat", [128, 8], F32))
    lamt = es.enter_context(nc.sbuf_tensor(name + "lamt", [8, 640], F32))
    lamB = es.enter_context(nc.sbuf_tensor(name + "lamB", [128, 8], F32))
    rawT = es.enter_context(nc.sbuf_tensor(name + "rawT", [128, CH], BF16))

    ld = ctx.sem(name + "ld")
    cons = ctx.sem(name + "cons")
    ldx = ctx.sem(name + "ldx")
    stx = ctx.sem(name + "stx")
    sta = ctx.sem(name + "sta")
    sLN = ctx.sem(name + "sLN")
    sRes = ctx.sem(name + "sRes")
    svA = ctx.sem(name + "svA")
    svB = ctx.sem(name + "svB")
    sN = ctx.sem(name + "sN")

    ctx.meta = getattr(ctx, "meta", {})

    ctx.dma("sync", lamt[0:8, 0:128], W["lambda_q1"], ld)
    ctx.dma("sync", lamt[0:8, 128:256], W["lambda_k1"], ld)
    ctx.dma("sync", lamt[0:8, 256:384], W["lambda_q2"], ld)
    ctx.dma("sync", lamt[0:8, 384:512], W["lambda_k2"], ld)
    ctx.wait("vector", ld, ld.v)
    ctx.run("vector", lambda e: e.tensor_mul(
        out=lamt[0:8, 0:128], in0=lamt[0:8, 0:128], in1=lamt[0:8, 128:256]), inc=sN)
    ctx.run("vector", lambda e: e.tensor_mul(
        out=lamt[0:8, 256:384], in0=lamt[0:8, 256:384], in1=lamt[0:8, 384:512]), inc=sN)
    ctx.wait("vector", sN, sN.v)
    ctx.on("vector", lambda e: e.tensor_reduce(
        out=lamt[0:8, 512:513], in_=lamt[0:8, 0:128], axis=AX.X, op=ALU.add))
    ctx.run("vector", lambda e: e.tensor_reduce(
        out=lamt[0:8, 513:514], in_=lamt[0:8, 256:384], axis=AX.X, op=ALU.add), inc=sN)
    ctx.wait("scalar", sN, sN.v)
    ctx.run("scalar", lambda e: e.activation(
        out=lamt[0:8, 514:516], in_=lamt[0:8, 512:514], func=AF.Exp), inc=sN)
    ctx.wait("vector", sN, sN.v)
    ctx.run("vector", lambda e: e.scalar_tensor_tensor(
        out=lamt[0:8, 516:517], in0=lamt[0:8, 514:515], scalar=-1.0, in1=lamt[0:8, 515:516],
        op0=ALU.mult, op1=ALU.add), inc=sN)
    ctx.wait("vector", sN, sN.v)
    ctx.run("vector", lambda e: e.tensor_scalar(
        out=lamt[0:8, 516:517], in0=lamt[0:8, 516:517], scalar1=-LAMI, scalar2=None,
        op0=ALU.add), inc=sN)
    lam_t = sN.v
    for h in range(H):
        sh.pb.group(
            [lambda e, h=h, p=sh.pb.ap: e.matmul(
                p[:, 0:1], sh.sel8[:, h * 128:(h + 1) * 128], lamt[0:8, 516:517],
                start=True, stop=True)],
            [("vector", lambda e, p=sh.pb.ap, h=h: e.tensor_copy(
                out=lamB[:, h:h + 1], in_=p[:, 0:1]))],
            extra_waits=[(sN, lam_t)])
    pre_pb = sh.pb.ev.v

    with contextlib.ExitStack() as es2:
        wq = es2.enter_context(nc.sbuf_tensor(name + "wq", [128, 8, 2048], BF16))
        wk = es2.enter_context(nc.sbuf_tensor(name + "wk", [128, 8, 2048], BF16))
        wv = es2.enter_context(nc.sbuf_tensor(name + "wv", [128, 8, 2048], BF16))
        wld2 = [es2.enter_context(nc.sbuf_tensor(name + f"wld{i}", [128, 2048], F32)) for i in range(2)]
        vtt = es2.enter_context(nc.sbuf_tensor(name + "vtt", [128, NTT, 2048], BF16))
        for wi, (w_ap, dst) in enumerate([(W["W_q"], wq), (W["W_k"], wk), (W["W_v"], wv)]):
            for dt in range(8):
                b = dt % 2
                if wi + dt > 1:
                    ctx.wait("sync", cons, cons.v - 1)
                ctx.dma("sync", wld2[b][:, :], w_ap[dt * 128:(dt + 1) * 128, :], ld)
                ctx.wait("scalar", ld, ld.v)
                ctx.run("scalar", lambda e, s=wld2[b], d=dst, dt=dt: e.activation(
                    out=d[:, dt, :], in_=s[:, :], func=AF.Copy), inc=cons)
        pre_cons = cons.v
        for c in range(NCH):
            b = c % 2
            ctx.wait("sync", sLN, sLN.v)
            ctx.dma("sync", xs[b][:, :, :],
                    x_in[c * CH:(c + 1) * CH, :].rearrange("(a p) d -> p a d", p=128), ldx)
            lv = ldx.v
            ptw = sh.pt.wr.v
            for tt in range(NTT):
                emit_ln_tile(ctx, xs[b][:, tt, :], xn[:, tt, :], sq, stat, D, sLN,
                             waits=[(ldx, lv), (sh.pt.wr, ptw), (sN, sN.v)], sv=svA, sh=sh)
            lnv = sLN.v
            for tt in range(NTT):
                sh.pt.group(
                    [lambda e, s=xn, tt=tt, dt=dt, p=sh.pt.ap, idn=sh.ident: e.transpose(
                        p[:, dt * 128:(dt + 1) * 128], s[:, tt, dt * 128:(dt + 1) * 128], idn[:, :])
                     for dt in range(8)],
                    [lambda e, p=sh.pt.ap, d=xnt, tt=tt: ecopy(
                        e, out=d[:, :, tt * 128:(tt + 1) * 128],
                        in_=p[:, :].rearrange("p (a b) -> p a b", a=8))],
                    evac_eng="scalar", extra_waits=[(sLN, lnv)])
            xnt_t = sh.pt.ev.v
            for which, wsb, dst in (("q", wq, qT_d), ("k", wk, kT_d)):
                for ft in range(16):
                    pkb = sh.pk[4 + ft % 2]
                    pkb.group(
                        [lambda e, dt=dt, ft=ft, w=wsb, p=pkb.ap: e.matmul(
                            p[:, 0:CH], w[:, dt, ft * 128:(ft + 1) * 128],
                            xnt[:, dt, :], start=(dt == 0), stop=(dt == 7))
                         for dt in range(8)],
                        [("scalar", lambda e, p=pkb.ap, which=which: e.activation(
                            out=rawT[:, :], in_=p[:, 0:CH], func=AF.Copy,
                            scale=SC if which == "q" else 1.0))],
                        extra_waits=[(sh.pt.ev, xnt_t), (cons, pre_cons)],
                        evac_extra_waits=[(sta, sta.v)])
                    nv = pkb.ev.v
                    ctx.wait("gpsimd", pkb.ev, nv)
                    ctx.dma("gpsimd", dst[ft * 128:(ft + 1) * 128, c * CH:(c + 1) * CH],
                            rawT[:, :], sta)
            for ft in range(16):
                pkb = sh.pk[4 + ft % 2]
                pkb.group(
                    [lambda e, dt=dt, ft=ft, p=pkb.ap: e.matmul(
                        p[:, 0:CH], wv[:, dt, ft * 128:(ft + 1) * 128],
                        xnt[:, dt, :], start=(dt == 0), stop=(dt == 7))
                     for dt in range(8)],
                    [("scalar", lambda e, p=pkb.ap: e.activation(
                        out=rawT[:, :], in_=p[:, 0:CH], func=AF.Copy))],
                    extra_waits=[(sh.pt.ev, xnt_t), (cons, pre_cons)],
                    evac_extra_waits=[(sh.pt.wr, sh.pt.wr.v), (sta, sta.v)])
                rv_ = pkb.ev.v
                for tt in range(NTT):
                    sh.pt.group(
                        [lambda e, tt=tt, p=sh.pt.ap, idn=sh.ident: e.transpose(
                            p[:, 0:128], rawT[:, tt * 128:(tt + 1) * 128], idn[:, :])],
                        [lambda e, p=sh.pt.ap, tt=tt, ft=ft: ecopy(
                            e, out=vtt[:, tt, ft * 128:(ft + 1) * 128], in_=p[:, 0:128])],
                        evac_eng="scalar", extra_waits=[(pkb.ev, rv_)],
                        evac_extra_waits=[(sta, sta.v)])
            vt_t = sh.pt.ev.v
            ctx.wait("gpsimd", sh.pt.ev, vt_t)
            ctx.dma("gpsimd", v_d[c * CH:(c + 1) * CH, :].rearrange("(a p) d -> p a d", p=128),
                    vtt[:, :, :], sta)
    phaseA_sta = sta.v

    k1t = es.enter_context(nc.sbuf_tensor(name + "k1t", [128, 2, T], BF16))
    vh = es.enter_context(nc.sbuf_tensor(name + "vh", [128, 32, 256], BF16))
    qth = es.enter_context(nc.sbuf_tensor(name + "qth", [128, 2, 128], BF16))
    ssb = [es.enter_context(nc.sbuf_tensor(name + f"ssb{i}", [128, 4096], F32)) for i in range(2)]
    esb = [es.enter_context(nc.sbuf_tensor(name + f"esb{i}", [128, 4096], BF16)) for i in range(2)]
    atb = es.enter_context(nc.sbuf_tensor(name + "atb", [128, 32, 128], BF16))
    mst = es.enter_context(nc.sbuf_tensor(name + "mst", [128, 8], F32))
    ob = es.enter_context(nc.sbuf_tensor(name + "ob", [128, 256], BF16))
    ldh = ctx.sem(name + "ldh")
    sAt = ctx.sem(name + "sAt")
    pk2, pk3 = sh.pk[2], sh.pk[3]
    for h in range(H):
        ctx.wait("sync", sh.pk[0].wr, sh.pk[0].wr.v)
        ctx.wait("sync", sh.pk[1].wr, sh.pk[1].wr.v)
        if h == 0:
            ctx.wait("sync", sta, phaseA_sta)
        ctx.dma("sync", k1t[:, :, :],
                kT_d[h * 256:(h + 1) * 256, :].rearrange("(a p) t -> p a t", p=128), ldh)
        ctx.wait("sync", pk2.wr, pk2.wr.v)
        for quarter in range(4):
            ctx.dma("sync", vh[:, quarter * 8:(quarter + 1) * 8, :],
                    v_d[quarter * 1024:(quarter + 1) * 1024, h * 256:(h + 1) * 256]
                    .rearrange("(a p) d -> p a d", p=128), ldh)
        ldh_h = ldh.v
        for qt in range(32):
            ctx.wait("sync", sh.pk[0].wr, sh.pk[0].wr.v)
            ctx.wait("sync", sh.pk[1].wr, sh.pk[1].wr.v)
            ctx.wait("sync", sAt, sAt.v)
            ctx.dma("sync", qth[:, :, :],
                    qT_d[h * 256:(h + 1) * 256, qt * 128:(qt + 1) * 128]
                    .rearrange("(a p) t -> p a t", p=128), ldh)
            ldh_q = ldh.v
            W_k = (qt + 1) * 128
            nkc = (W_k + 511) // 512
            for half in range(2):
                pkb = sh.pk[half]
                for kc in range(nkc):
                    n = min(512, W_k - kc * 512)
                    pkb.group(
                        [lambda e, kc=kc, n=n, half=half, p=pkb.ap: e.matmul(
                            p[:, 0:n], qth[:, half, :], k1t[:, half, kc * 512:kc * 512 + n],
                            start=True, stop=True)],
                        [("vector", lambda e, p=pkb.ap, kc=kc, n=n, half=half: e.tensor_copy(
                            out=ssb[half][:, kc * 512:kc * 512 + n], in_=p[:, 0:n]))],
                        extra_waits=[(ldh, ldh_q)],
                        evac_extra_waits=[(sAt, sAt.v)])
                ctx.wait("vector", pkb.ev, pkb.ev.v)
                ctx.run("vector", lambda e, half=half, qt=qt: e.tensor_add(
                    out=ssb[half][:, qt * 128:(qt + 1) * 128],
                    in0=ssb[half][:, qt * 128:(qt + 1) * 128], in1=sh.tril[:, :]), inc=sAt)
                ctx.wait("vector", sAt, sAt.v)
                ctx.run("vector", lambda e, half=half, W_k=W_k: e.tensor_reduce(
                    out=mst[:, half:half + 1], in_=ssb[half][:, 0:W_k], axis=AX.X,
                    op=ALU.max, negate=True), inc=sAt)
                ctx.wait("scalar", sAt, sAt.v)
                ctx.run("scalar", lambda e, half=half, W_k=W_k: e.activation(
                    out=esb[half][:, 0:W_k], in_=ssb[half][:, 0:W_k], func=AF.Exp,
                    bias=mst[:, half:half + 1]), inc=sAt)
                ctx.wait("vector", sAt, sAt.v)
                ctx.run("vector", lambda e, half=half, W_k=W_k: e.tensor_reduce(
                    out=mst[:, 2 + half:3 + half], in_=esb[half][:, 0:W_k], axis=AX.X,
                    op=ALU.add), inc=sAt)
                ctx.wait("vector", sAt, sAt.v)
                ctx.run("vector", lambda e, half=half: e.reciprocal(
                    out=mst[:, 4 + half:5 + half], in_=mst[:, 2 + half:3 + half]), inc=sAt)
            ctx.wait("vector", sAt, sAt.v)
            ctx.run("vector", lambda e, h=h: e.tensor_mul(
                out=mst[:, 6:7], in0=mst[:, 5:6], in1=lamB[:, h:h + 1]), inc=sAt)
            ctx.wait("vector", sAt, sAt.v)
            ctx.run("vector", lambda e, W_k=W_k: e.tensor_scalar(
                out=ssb[0][:, 0:W_k], in0=esb[0][:, 0:W_k], scalar1=mst[:, 4:5],
                scalar2=None, op0=ALU.mult), inc=sAt)
            ctx.wait("vector", sAt, sAt.v)
            ctx.run("vector", lambda e, W_k=W_k: e.scalar_tensor_tensor(
                out=ssb[0][:, 0:W_k], in0=esb[1][:, 0:W_k], scalar=mst[:, 6:7],
                in1=ssb[0][:, 0:W_k], op0=ALU.mult, op1=ALU.add), inc=sAt)
            ctx.wait("scalar", sAt, sAt.v)
            ctx.run("scalar", lambda e, W_k=W_k: e.activation(
                out=esb[0][:, 0:W_k], in_=ssb[0][:, 0:W_k], func=AF.Copy), inc=sAt)
            attv = sAt.v
            for kt in range(qt + 1):
                sh.pt.group(
                    [lambda e, kt=kt, p=sh.pt.ap, idn=sh.ident: e.transpose(
                        p[:, 0:128], esb[0][:, kt * 128:(kt + 1) * 128], idn[:, :])],
                    [lambda e, p=sh.pt.ap, kt=kt: ecopy(
                        e, out=atb[:, kt, :], in_=p[:, 0:128])],
                    evac_eng="scalar", extra_waits=[(sAt, attv), (pk2.wr, pk2.wr.v)],
                    evac_extra_waits=[(sta, sta.v)])
            at_t = sh.pt.ev.v
            ctx.wait("tensor", pk2.ev, pk2.ev.v)
            ctx.wait("tensor", sh.pt.ev, at_t)
            ctx.wait("tensor", ldh, ldh_h)
            for kt in range(qt + 1):
                ctx.run("tensor", lambda e, kt=kt: e.matmul(
                    pk2.ap[:, 0:256], atb[:, kt, :], vh[:, kt, :],
                    start=(kt == 0), stop=(kt == qt)),
                    inc=pk2.wr if kt == qt else None)
            ctx.wait("scalar", pk2.wr, pk2.wr.v)
            ctx.wait("scalar", sAt, sAt.v)
            ctx.run("scalar", lambda e: e.activation(
                out=sq[:, 0:256], in_=pk2.ap[:, 0:256], func=AF.Square), inc=sAt)
            ctx.wait("vector", sAt, sAt.v)
            ctx.run("vector", lambda e: e.tensor_reduce(
                out=mst[:, 7:8], in_=sq[:, 0:256], axis=AX.X, op=ALU.add), inc=sAt)
            ctx.wait("scalar", sAt, sAt.v)
            ctx.on("scalar", lambda e: e.activation(
                out=mst[:, 7:8], in_=mst[:, 7:8], func=AF.Ln, scale=1.0 / 256.0,
                bias=sh.epscols[:, 0:1]))
            ctx.run("scalar", lambda e: e.activation(
                out=mst[:, 7:8], in_=mst[:, 7:8], func=AF.Exp, scale=-0.5), inc=sAt)
            rinv_t = sAt.v
            ctx.wait("vector", sAt, rinv_t)
            ctx.wait("vector", sh.pt.wr, sh.pt.wr.v)
            ctx.run("vector", lambda e: e.tensor_scalar(
                out=ob[:, :], in0=pk2.ap[:, 0:256], scalar1=mst[:, 7:8],
                scalar2=None, op0=ALU.mult), inc=pk2.ev)
            obv = pk2.ev.v
            for sub in range(2):
                sh.pt.group(
                    [lambda e, sub=sub, p=sh.pt.ap, idn=sh.ident: e.transpose(
                        p[:, 0:128], ob[:, sub * 128:(sub + 1) * 128], idn[:, :])],
                    [lambda e, p=sh.pt.ap, sub=sub: ecopy(
                        e, out=atb[:, sub, :], in_=p[:, 0:128])],
                    evac_eng="scalar", extra_waits=[(pk2.ev, obv)],
                    evac_extra_waits=[(sta, sta.v)])
            ctx.wait("gpsimd", sh.pt.ev, sh.pt.ev.v)
            ctx.dma("gpsimd", oT_d[h * 256:(h + 1) * 256, qt * 128:(qt + 1) * 128]
                    .rearrange("(a p) t -> p a t", p=128), atb[:, 0:2, :], sta)
    phaseB_sta = sta.v

    wo = es.enter_context(nc.sbuf_tensor(name + "wo", [128, 16, 1024], BF16))
    rmsT = es.enter_context(nc.sbuf_tensor(name + "rmsT", [128, 2], F32))
    wldc = [es.enter_context(nc.sbuf_tensor(name + f"wldc{i}", [128, 1024], F32)) for i in range(2)]
    oth = es.enter_context(nc.sbuf_tensor(name + "oth", [128, 16, CH], BF16))
    osb = es.enter_context(nc.sbuf_tensor(name + "osb", [128, NTT, 1024], F32))
    ctx.meta["vecT_tmp"] = lamt
    emit_load_vecT(ctx, sh, W["rms_scale"], 256, rmsT, ld)
    rms_t = sh.pb.ev.v
    for ft in range(16):
        b = ft % 2
        if ft >= 2:
            ctx.wait("sync", cons, cons.v - 1)
        ctx.dma("sync", wldc[b][:, :], W["W_o"][ft * 128:(ft + 1) * 128, :], ld)
        ctx.wait("vector", ld, ld.v)
        ctx.wait("vector", sh.pb.ev, rms_t)
        ctx.run("vector", lambda e, s=wldc[b], ft=ft: e.tensor_scalar(
            out=wo[:, ft, :], in0=s[:, :], scalar1=rmsT[:, ft % 2:ft % 2 + 1],
            scalar2=0.2, op0=ALU.mult, op1=ALU.mult), inc=cons)
    pre_cons2 = cons.v
    for c in range(NCH):
        b = c % 2
        ctx.wait("sync", stx, stx.v)
        ctx.wait("sync", sLN, sLN.v)
        if c == 0:
            ctx.wait("sync", sta, phaseB_sta)
        ctx.dma("sync", xs[b][:, :, :],
                x_in[c * CH:(c + 1) * CH, :].rearrange("(a p) d -> p a d", p=128), ldx)
        ctx.wait("sync", sh.pk[0].wr, sh.pk[0].wr.v)
        ctx.wait("sync", sh.pk[1].wr, sh.pk[1].wr.v)
        ctx.wait("sync", sh.pk[2].wr, sh.pk[2].wr.v)
        ctx.wait("sync", sh.pk[3].wr, sh.pk[3].wr.v)
        ctx.dma("sync", oth[:, :, :],
                oT_d[:, c * CH:(c + 1) * CH].rearrange("(a p) t -> p a t", p=128), ldx)
        lv = ldx.v
        for tt in range(NTT):
            for oh in range(2):
                pkb = sh.pk[tt * 2 + oh]
                pkb.group(
                    [lambda e, ft=ft, tt=tt, oh=oh, p=pkb.ap: e.matmul(
                        p[:, 0:512], oth[:, ft, tt * 128:(tt + 1) * 128],
                        wo[:, ft, oh * 512:(oh + 1) * 512], start=(ft == 0), stop=(ft == 15))
                     for ft in range(16)],
                    [("vector", lambda e, p=pkb.ap, tt=tt, oh=oh: e.tensor_copy(
                        out=osb[:, tt, oh * 512:(oh + 1) * 512], in_=p[:, 0:512]))],
                    extra_waits=[(ldx, lv), (cons, pre_cons2)],
                    evac_extra_waits=[(sLN, sLN.v)])
        for tt in range(NTT):
            w = [(sh.pk[tt * 2].ev, sh.pk[tt * 2].ev.v),
                 (sh.pk[tt * 2 + 1].ev, sh.pk[tt * 2 + 1].ev.v), (sRes, sRes.v), (ldx, lv)]
            emit_postln_resid(ctx, osb[:, tt, :], xs[b][:, tt, :], sq, stat, D, sRes,
                              waits=w, sv=svB, sln=sLN, sh=sh)
        rv = sRes.v
        ctx.wait("gpsimd", sRes, rv)
        ctx.dma("gpsimd", x_out[c * CH:(c + 1) * CH, :].rearrange("(a p) d -> p a d", p=128),
                xs[b][:, :, :], stx)
    ctx.wait("gpsimd", stx, stx.v)
    return stx


def emit_l1_attn(ctx, sh, es, x_in, x_out, W, name="s1"):
    nc = ctx.nc
    T, D, DI, NS, RK = 4096, 1024, 2048, 16, 64
    CL = 256
    NCH = T // CL
    NDT = DI // 128

    winT_d = nc.dram_tensor(name + "winT", [8, 128, 4096], BF16).ap()
    woutT_d = nc.dram_tensor(name + "woutT", [16, 128, 1024], BF16).ap()

    ld = ctx.sem(name + "ld")
    cons = ctx.sem(name + "cons")
    stp = ctx.sem(name + "stp")
    ctx.meta = getattr(ctx, "meta", {})
    wouth = es.enter_context(nc.sbuf_tensor(name + "wouth", [128, 16, 256], BF16))
    es2 = contextlib.ExitStack()
    wld = [es2.enter_context(nc.sbuf_tensor(name + f"wld{i}", [128, 2048], F32, side="right")) for i in range(2)]
    wbf = [es2.enter_context(nc.sbuf_tensor(name + f"wbf{i}", [128, 2048], BF16, side="right")) for i in range(2)]
    stage = es2.enter_context(nc.sbuf_tensor(name + "stage", [128, 8, 4096], BF16, side="right"))
    emit_loadT(ctx, sh, W["W_in"], 4096, 1024, stage, wld, wbf, ld, cons)
    ctx.wait("gpsimd", sh.pt.ev, sh.pt.ev.v)
    ctx.dma("gpsimd", winT_d.rearrange("a p d -> p a d"), stage[:, :, :], stp)
    ctx.wait("scalar", stp, stp.v)
    stage16 = stage[:, :, :].rearrange("p a (c d) -> p (a c) d", c=2)[:, :, 0:1024]
    emit_loadT(ctx, sh, W["W_out"], 1024, 2048, stage16, wld, wbf, ld, cons)
    ctx.wait("gpsimd", sh.pt.ev, sh.pt.ev.v)
    ctx.dma("gpsimd", woutT_d.rearrange("a p d -> p a d"), stage16, stp)
    ctx.wait("scalar", stp, stp.v)
    wxpT = es.enter_context(nc.sbuf_tensor(name + "wxpT", [128, 16, 96], F32))
    for dt in range(16):
        ctx.wait("sync", sh.pb.wr, sh.pb.wr.v)
        ctx.dma("sync", wld[0][0:96, 0:128], W["W_xproj"][:, dt * 128:(dt + 1) * 128], ld)
        lv = ld.v
        sh.pb.group(
            [lambda e, p=sh.pb.ap, idn=sh.identf32: e.transpose(
                p[:, 0:96], wld[0][0:96, 0:128], idn[0:96, 0:96])],
            [("vector", lambda e, p=sh.pb.ap, dt=dt: e.tensor_copy(
                out=wxpT[:, dt, :], in_=p[:, 0:96]))],
            extra_waits=[(ld, lv)])
    wdtT = es.enter_context(nc.sbuf_tensor(name + "wdtT", [64, 2048], F32))
    for dt in range(16):
        ctx.wait("sync", sh.pb.wr, sh.pb.wr.v)
        ctx.dma("sync", wld[0][:, 0:64], W["W_dt"][dt * 128:(dt + 1) * 128, :], ld)
        lv = ld.v
        sh.pb.group(
            [lambda e, p=sh.pb.ap, idn=sh.identf32: e.transpose(
                p[0:64, 0:128], wld[0][:, 0:64], idn[:, :])],
            [("vector", lambda e, p=sh.pb.ap, dt=dt: e.tensor_copy(
                out=wdtT[0:64, dt * 128:(dt + 1) * 128], in_=p[0:64, 0:128]))],
            extra_waits=[(ld, lv)])
    Abuf = es.enter_context(nc.sbuf_tensor(name + "A", [128, 16, 16], F32))
    for g in range(2):
        ctx.wait("sync", cons, cons.v)
        ctx.wait("sync", sh.pb.wr, sh.pb.wr.v)
        ctx.dma("sync", wld[0][:, 0:128].rearrange("p (a n) -> p a n", a=8),
                W["A_log"].rearrange("(a p) n -> p a n", p=128)[:, g * 8:(g + 1) * 8, :], ld)
        ctx.wait("scalar", ld, ld.v)
        ctx.run("scalar", lambda e, g=g: e.activation(
            out=Abuf[:, g * 8:(g + 1) * 8, :], in_=wld[0][:, 0:128].rearrange("p (a n) -> p a n", a=8),
            func=AF.Exp), inc=cons)
    ctx.wait("vector", cons, cons.v)
    ctx.run("vector", lambda e: e.tensor_scalar(
        out=Abuf[:, :, :], in0=Abuf[:, :, :], scalar1=-1.0, scalar2=None, op0=ALU.mult), inc=cons)
    vecT = es.enter_context(nc.sbuf_tensor(name + "vecT", [128, 64], F32))
    ctx.meta["vecT_tmp"] = wld[0]
    emit_load_vecT(ctx, sh, W["b_conv"], 2048, vecT[:, 0:16], ld)
    emit_load_vecT(ctx, sh, W["b_dt"], 2048, vecT[:, 16:32], ld)
    emit_load_vecT(ctx, sh, W["D_skip"], 2048, vecT[:, 32:48], ld)
    wcv = es.enter_context(nc.sbuf_tensor(name + "wcv", [128, 16, 4], F32))
    for g in range(2):
        ctx.wait("sync", cons, cons.v)
        ctx.wait("sync", sh.pb.wr, sh.pb.wr.v)
        ctx.dma("sync", wld[0][:, 0:32].rearrange("p (a k) -> p a k", a=8),
                W["W_conv"].rearrange("(a p) k -> p a k", p=128)[:, g * 8:(g + 1) * 8, :], ld)
        ctx.wait("vector", ld, ld.v)
        ctx.run("vector", lambda e, g=g: e.tensor_copy(
            out=wcv[:, g * 8:(g + 1) * 8, :],
            in_=wld[0][:, 0:32].rearrange("p (a k) -> p a k", a=8)), inc=cons)
    lnwT = es.enter_context(nc.sbuf_tensor(name + "lnwT", [128, 16], F32))
    emit_load_vecT(ctx, sh, W["ln_w"], 1024, lnwT[:, 0:8], ld)
    emit_load_vecT(ctx, sh, W["ln_b"], 1024, lnwT[:, 8:16], ld)
    pre_pb = sh.pb.ev.v
    pre_cons = cons.v
    pre_pt = sh.pt.ev.v
    es2.close()
    for _e in ENGS:
        ctx.wait(_e, cons, pre_cons)
        ctx.wait(_e, sh.pb.ev, pre_pb)
        ctx.wait(_e, sh.pt.ev, pre_pt)
        ctx.wait(_e, stp, stp.v)
        ctx.wait(_e, sh.pb.wr, sh.pb.wr.v)

    xs = es.enter_context(nc.sbuf_tensor(name + "xs", [128, 2, 1024], F32))
    tbuf = es.enter_context(nc.sbuf_tensor(name + "t", [128, 2, 1024], F32))
    xn = es.enter_context(nc.sbuf_tensor(name + "xn", [128, 2, 1024], BF16))
    xnt = es.enter_context(nc.sbuf_tensor(name + "xnt", [128, 8, CL], BF16))
    winp = es.enter_context(nc.sbuf_tensor(name + "winp", [128, 8, 128], BF16))
    bufA = es.enter_context(nc.sbuf_tensor(name + "bufA", [128, 16, 260], F32))
    zT = es.enter_context(nc.sbuf_tensor(name + "zT", [128, 16, CL], BF16))
    xcT = es.enter_context(nc.sbuf_tensor(name + "xcT", [128, 16, CL], F32))
    deltaT = es.enter_context(nc.sbuf_tensor(name + "deltaT", [128, 16, CL], F32))
    ubuf = es.enter_context(nc.sbuf_tensor(name + "u", [128, 16, CL], F32))
    ybuf = es.enter_context(nc.sbuf_tensor(name + "y", [128, 16, CL], F32))
    dAball = es.enter_context(nc.sbuf_tensor(name + "dAb", [128, 4096], F32))
    hball = es.enter_context(nc.sbuf_tensor(name + "hb", [128, 16, CL], F32))
    hcar = es.enter_context(nc.sbuf_tensor(name + "hcar", [128, 16, 16], F32, side="right"))
    dtlowT = es.enter_context(nc.sbuf_tensor(name + "dtlowT", [64, CL], F32, side="right"))
    BsS = es.enter_context(nc.sbuf_tensor(name + "BsS", [16, 2, CL], F32, side="right"))
    Bbc = es.enter_context(nc.sbuf_tensor(name + "Bbc", [128, 2, CL], F32, side="right"))
    zf = es.enter_context(nc.sbuf_tensor(name + "zf", [128, 3, CL], F32, side="right"))
    sq = es.enter_context(nc.sbuf_tensor(name + "sq", [128, 1024], F32, side="right"))
    stat = es.enter_context(nc.sbuf_tensor(name + "stat", [128, 8], F32, side="right"))
    dAv = dAball.rearrange("p (a b) -> p a b", a=16)
    osb = dAball.rearrange("p (a b) -> p a b", a=4)
    dBxall = bufA[:, :, 0:CL]

    ldx = ctx.sem(name + "ldx")
    ldw = ctx.sem(name + "ldw")
    stx = ctx.sem(name + "stx")
    sLN = ctx.sem(name + "sLN")
    sRes = ctx.sem(name + "sRes")
    svA = ctx.sem(name + "svA")
    svB = ctx.sem(name + "svB")
    sV = ctx.sem(name + "sV")
    sS = ctx.sem(name + "sS")
    sP = ctx.sem(name + "sP")

    ctx.wait("vector", cons, pre_cons)
    ctx.on("vector", lambda e: e.memset(bufA[:, :, 0:4], 0.0))
    ctx.run("vector", lambda e: e.memset(hcar[:, :, :], 0.0), inc=sV)

    for c in range(NCH):
        if c > 0:
            ctx.wait("vector", sS, sS.v)
            ctx.run("vector", lambda e: e.tensor_copy(
                out=bufA[:, :, 0:4], in_=bufA[:, :, 256:260]), inc=sV)
        halo_v = sV.v
        ctx.wait("sync", stx, stx.v)
        ctx.wait("sync", sLN, sLN.v)
        ctx.dma("sync", xs[:, :, :],
                x_in[c * CL:(c + 1) * CL, :].rearrange("(a p) d -> p a d", p=128), ldx)
        lv = ldx.v
        ptw = sh.pt.wr.v
        for tt in range(2):
            emit_ln_tile(ctx, xs[:, tt, :], tbuf[:, tt, :], sq, stat, D, sLN,
                         waits=[(ldx, lv), (sRes, sRes.v)], sv=svA, sh=sh)
            emit_ln_tile(ctx, tbuf[:, tt, :], xn[:, tt, :], sq, stat, D, sLN,
                         waits=[(sh.pt.wr, ptw)], sv=svA, sh=sh)
        lnv = sLN.v
        for tt in range(2):
            sh.pt.group(
                [lambda e, s=xn, tt=tt, dt=dt, p=sh.pt.ap, idn=sh.ident: e.transpose(
                    p[:, dt * 128:(dt + 1) * 128], s[:, tt, dt * 128:(dt + 1) * 128], idn[:, :])
                 for dt in range(8)],
                [lambda e, p=sh.pt.ap, d=xnt, tt=tt: ecopy(
                    e, out=d[:, :, tt * 128:(tt + 1) * 128],
                    in_=p[:, :].rearrange("p (a b) -> p a b", a=8))],
                evac_eng="scalar", extra_waits=[(sLN, lnv)])
        trv = sh.pt.ev.v
        ctx.wait("vector", sh.pt.ev, trv)
        for dt in range(8):
            ctx.run("vector", lambda e, dt=dt: e.tensor_scalar(
                out=xnt[:, dt, :], in0=xnt[:, dt, :], scalar1=lnwT[:, dt:dt + 1],
                scalar2=lnwT[:, 8 + dt:9 + dt], op0=ALU.mult, op1=ALU.add),
                inc=sV if dt == 7 else None)
        aff_v = sV.v
        for fp in range(32):
            ctx.wait("sync", sh.pk[4].wr, sh.pk[4].wr.v)
            ctx.wait("sync", sh.pk[5].wr, sh.pk[5].wr.v)
            ctx.dma("sync", winp[:, :, :],
                    winT_d[:, :, fp * 128:(fp + 1) * 128].rearrange("a p d -> p a d"), ldw)
            lwv = ldw.v
            for fs in range(1):
                fg = fp
                pkb = sh.pk[4 + fg % 2]
                if fg < 16:
                    evac = [("scalar", lambda e, p=pkb.ap, fg=fg: e.activation(
                        out=bufA[:, fg, 4:260], in_=p[:, 0:CL], func=AF.Copy))]
                else:
                    evac = [("scalar", lambda e, p=pkb.ap, fg=fg: e.activation(
                        out=zT[:, fg - 16, :], in_=p[:, 0:CL], func=AF.Copy))]
                pkb.group(
                    [lambda e, w=winp, x_=xnt, dt=dt, p=pkb.ap: e.matmul(
                        p[:, 0:CL], w[:, dt, :], x_[:, dt, :],
                        start=(dt == 0), stop=(dt == 7))
                     for dt in range(8)],
                    evac,
                    extra_waits=[(ldw, lwv), (sV, aff_v)],
                    evac_extra_waits=[(sV, sV.v), (sP, sP.v)])
        xz_v = [sh.pk[4].ev.v, sh.pk[5].ev.v]
        ctx.waits("vector", [(sh.pk[4].ev, xz_v[0]), (sh.pk[5].ev, xz_v[1]), (sV, halo_v),
                             (cons, pre_cons)])
        for k in range(4):
            for dt in range(16):
                if k == 0:
                    ctx.on("vector", lambda e, dt=dt, k=k: e.tensor_scalar(
                        out=xcT[:, dt, :], in0=bufA[:, dt, 1 + k:1 + k + CL],
                        scalar1=wcv[:, dt, k:k + 1], scalar2=None, op0=ALU.mult))
                else:
                    ctx.on("vector", lambda e, dt=dt, k=k: e.scalar_tensor_tensor(
                        out=xcT[:, dt, :], in0=bufA[:, dt, 1 + k:1 + k + CL],
                        scalar=wcv[:, dt, k:k + 1], in1=xcT[:, dt, :],
                        op0=ALU.mult, op1=ALU.add))
        ctx.run("vector", lambda e: e.engine_nop(), inc=sV)
        conv_v = sV.v
        ctx.wait("scalar", sV, conv_v)
        for dt in range(16):
            ctx.run("scalar", lambda e, dt=dt: e.activation(
                out=xcT[:, dt, :], in_=xcT[:, dt, :], func=AF.Silu,
                bias=vecT[:, dt:dt + 1]), inc=sS if dt == 15 else None)
        silu_v = sS.v
        ctx.wait("tensor", sS, silu_v)
        ctx.wait("tensor", cons, pre_cons)
        sh.pb.group(
            [lambda e, dt=dt, p=sh.pb.ap: e.matmul(
                p[0:64, 0:CL], wxpT[:, dt, 0:64], xcT[:, dt, :],
                start=(dt == 0), stop=(dt == 15)) for dt in range(16)],
            [("scalar", lambda e, p=sh.pb.ap: e.activation(
                out=dtlowT[0:64, :], in_=p[0:64, 0:CL], func=AF.Copy))],
            evac_extra_waits=[(sV, sV.v)])
        sh.pb.group(
            [lambda e, dt=dt, p=sh.pb.ap: e.matmul(
                p[0:16, 256:256 + CL], wxpT[:, dt, 64:80], xcT[:, dt, :],
                start=(dt == 0), stop=(dt == 15)) for dt in range(16)],
            [("scalar", lambda e, p=sh.pb.ap: e.activation(
                out=BsS[0:16, 0, :], in_=p[0:16, 256:256 + CL], func=AF.Copy))])
        sh.pb.group(
            [lambda e, dt=dt, p=sh.pb.ap: e.matmul(
                p[0:16, 0:CL], wxpT[:, dt, 80:96], xcT[:, dt, :],
                start=(dt == 0), stop=(dt == 15)) for dt in range(16)],
            [("scalar", lambda e, p=sh.pb.ap: e.activation(
                out=BsS[0:16, 1, :], in_=p[0:16, 0:CL], func=AF.Copy))])
        dbl_v = sh.pb.ev.v
        for dt in range(16):
            pkb = sh.pk[4 + dt % 2]
            def evacf(e, p=pkb.ap, dt=dt):
                return e.activation(out=zf[:, 0, :], in_=p[:, 0:CL], func=AF.Identity,
                                    bias=vecT[:, 16 + dt:17 + dt])
            pkb.group(
                [lambda e, dt=dt, p=pkb.ap: e.matmul(
                    p[:, 0:CL], wdtT[0:64, dt * 128:(dt + 1) * 128], dtlowT[0:64, :],
                    start=True, stop=True)],
                [("scalar", evacf)],
                extra_waits=[(sh.pb.ev, dbl_v)],
                evac_extra_waits=[(sV, sV.v)])
            zv = pkb.ev.v
            ctx.wait("scalar", pkb.ev, zv)
            ctx.on("scalar", lambda e: e.activation(out=zf[:, 1, :], in_=zf[:, 0, :], func=AF.Abs))
            ctx.on("scalar", lambda e: e.activation(out=zf[:, 1, :], in_=zf[:, 1, :], func=AF.Exp, scale=-1.0))
            ctx.on("scalar", lambda e: e.activation(out=zf[:, 1, :], in_=zf[:, 1, :], func=AF.Ln,
                                                    bias=sh.onescolf32[:, 0:1]))
            ctx.run("scalar", lambda e: e.activation(out=zf[:, 2, :], in_=zf[:, 0, :], func=AF.Relu), inc=sS)
            spv = sS.v
            ctx.wait("vector", sS, spv)
            ctx.run("vector", lambda e, dt=dt: e.tensor_add(
                out=deltaT[:, dt, :], in0=zf[:, 1, :], in1=zf[:, 2, :]), inc=sV)
        for dt in range(16):
            ctx.run("vector", lambda e, dt=dt: e.tensor_mul(
                out=ubuf[:, dt, :], in0=deltaT[:, dt, :], in1=xcT[:, dt, :]),
                inc=sV if dt == 15 else None)
        u_v = sV.v
        ctx.wait("vector", sV, u_v)
        ctx.run("vector", lambda e: e.memset(ybuf[:, :, :], 0.0), inc=sV)
        y0_v = sV.v
        for n in range(NS):
            sh.pb.group(
                [lambda e, n=n, p=sh.pb.ap: e.matmul(
                    p[:, 0:CL], sh.sel16[0:16, n * 128:(n + 1) * 128], BsS[0:16, 0, :],
                    start=True, stop=True)],
                [("scalar", lambda e, p=sh.pb.ap: e.activation(
                    out=Bbc[:, 0, :], in_=p[:, 0:CL], func=AF.Copy))],
                extra_waits=[(sh.pb.ev, dbl_v)],
                evac_extra_waits=[(sP, sP.v)])
            sh.pb.group(
                [lambda e, n=n, p=sh.pb.ap: e.matmul(
                    p[:, 256:256 + CL], sh.sel16[0:16, n * 128:(n + 1) * 128], BsS[0:16, 1, :],
                    start=True, stop=True)],
                [("scalar", lambda e, p=sh.pb.ap: e.activation(
                    out=Bbc[:, 1, :], in_=p[:, 256:256 + CL], func=AF.Copy))],
                evac_extra_waits=[(sP, sP.v)])
            bc_v = sh.pb.ev.v
            ctx.wait("scalar", sV, u_v)
            ctx.wait("scalar", sV, sV.v)
            ctx.wait("scalar", cons, pre_cons)
            for dt in range(16):
                ctx.run("scalar", lambda e, dt=dt, n=n: e.activation(
                    out=dAv[:, dt, :], in_=deltaT[:, dt, :], func=AF.Exp,
                    scale=Abuf[:, dt, n:n + 1]), inc=sS if dt == 15 else None)
            dA_v = sS.v
            ctx.wait("gpsimd", sh.pb.ev, bc_v)
            ctx.wait("gpsimd", sV, u_v)
            ctx.wait("gpsimd", sV, sV.v)
            ctx.wait("gpsimd", sS, sS.v)
            for dt in range(16):
                ctx.run("gpsimd", lambda e, dt=dt: e.tensor_tensor(
                    out=dBxall[:, dt, :], in0=ubuf[:, dt, :], in1=Bbc[:, 0, :],
                    op=ALU.mult), inc=sP if dt == 15 else None)
            dBx_v = sP.v
            ctx.wait("vector", sS, dA_v)
            ctx.wait("vector", sP, dBx_v)
            ctx.wait("vector", sV, y0_v)
            for dt in range(16):
                ctx.run("vector", lambda e, dt=dt, n=n: e.tensor_tensor_scan(
                    out=hball[:, dt, :], data0=dAv[:, dt, :], data1=dBxall[:, dt, :],
                    initial=hcar[:, dt, n:n + 1], op0=ALU.mult, op1=ALU.add),
                    inc=sV if dt == 15 else None)
            scan_v = sV.v
            for dt in range(16):
                ctx.run("vector", lambda e, dt=dt, n=n: e.tensor_copy(
                    out=hcar[:, dt, n:n + 1], in_=hball[:, dt, 255:256]),
                    inc=sV if dt == 15 else None)
            ctx.wait("gpsimd", sV, scan_v)
            for dt in range(16):
                ctx.run("gpsimd", lambda e, dt=dt: e.tensor_tensor(
                    out=dAv[:, dt, :], in0=hball[:, dt, :], in1=Bbc[:, 1, :],
                    op=ALU.mult), inc=sP if dt == 15 else None)
            ym_v = sP.v
            ctx.wait("vector", sP, ym_v)
            for dt in range(16):
                ctx.run("vector", lambda e, dt=dt: e.tensor_add(
                    out=ybuf[:, dt, :], in0=ybuf[:, dt, :], in1=dAv[:, dt, :]),
                    inc=sV if dt == 15 else None)
        yv = sV.v
        for dt in range(16):
            ctx.run("vector", lambda e, dt=dt: e.scalar_tensor_tensor(
                out=ybuf[:, dt, :], in0=xcT[:, dt, :], scalar=vecT[:, 32 + dt:33 + dt],
                in1=ybuf[:, dt, :], op0=ALU.mult, op1=ALU.add),
                inc=sV if dt == 15 else None)
        dsk_v = sV.v
        ctx.wait("scalar", sV, dsk_v)
        for dt in range(16):
            ctx.run("scalar", lambda e, dt=dt: e.activation(
                out=zT[:, dt, :], in_=zT[:, dt, :], func=AF.Silu), inc=sS if dt == 15 else None)
        sz_v = sS.v
        ctx.wait("vector", sS, sz_v)
        for dt in range(16):
            ctx.run("vector", lambda e, dt=dt: e.tensor_tensor(
                out=zT[:, dt, :], in0=ybuf[:, dt, :], in1=zT[:, dt, :],
                op=ALU.mult), inc=sV if dt == 15 else None)
        ybf_v = sV.v
        for oq in range(4):
            ctx.wait("sync", sh.pk[0].wr, sh.pk[0].wr.v)
            ctx.wait("sync", sh.pk[1].wr, sh.pk[1].wr.v)
            ctx.wait("sync", sh.pk[2].wr, sh.pk[2].wr.v)
            ctx.wait("sync", sh.pk[3].wr, sh.pk[3].wr.v)
            ctx.dma("sync", wouth[:, :, :],
                    woutT_d[:, :, oq * 256:(oq + 1) * 256].rearrange("a p d -> p a d"), ldw)
            lwo = ldw.v
            for tt in range(2):
                pkb = sh.pk[tt * 2 + oq % 2]
                pkb.group(
                    [lambda e, et=et, tt=tt, p=pkb.ap: e.matmul(
                        p[:, 0:256], zT[:, et, tt * 128:(tt + 1) * 128],
                        wouth[:, et, :], start=(et == 0), stop=(et == 15))
                     for et in range(16)],
                    [("vector", lambda e, p=pkb.ap, tt=tt, oq=oq: e.tensor_tensor(
                        out=osb[:, tt, oq * 256:(oq + 1) * 256], in0=p[:, 0:256],
                        in1=tbuf[:, tt, oq * 256:(oq + 1) * 256], op=ALU.add))],
                    extra_waits=[(sV, ybf_v), (cons, pre_cons), (ldw, lwo)],
                    evac_extra_waits=[(sS, sS.v), (sLN, sLN.v), (sP, sP.v)])
        for tt in range(2):
            w = [(sh.pk[tt * 2].ev, sh.pk[tt * 2].ev.v),
                 (sh.pk[tt * 2 + 1].ev, sh.pk[tt * 2 + 1].ev.v), (sRes, sRes.v)]
            emit_postln_resid(ctx, osb[:, tt, :], xs[:, tt, :], sq, stat, D, sRes,
                              waits=w, sv=svB, sln=sLN, sh=sh)
        rv = sRes.v
        ctx.wait("gpsimd", sRes, rv)
        ctx.dma("gpsimd", x_out[c * CL:(c + 1) * CL, :].rearrange("(a p) d -> p a d", p=128),
                xs[:, :, :], stx)
    ctx.wait("gpsimd", stx, stx.v)
    return stx


from concourse.bass_utils import run_bass_kernel_spmd

_SHAPES = {
    "L0_attn": [("proj_qk_w", (1024, 1024)), ("proj_qk_b", (1024,)),
                ("proj_v_w", (1024, 1024)), ("proj_v_b", (1024,)),
                ("in_proj_w", (3072, 1024)), ("in_proj_b", (3072,)),
                ("out_proj_w", (1024, 1024)), ("out_proj_b", (1024,))],
    "L0_ffn": [("W1", (4096, 1024)), ("b1", (4096,)), ("W2", (1024, 4096)), ("b2", (1024,))],
    "L1_attn": [("ln_w", (1024,)), ("ln_b", (1024,)), ("W_in", (4096, 1024)),
                ("W_conv", (2048, 4)), ("b_conv", (2048,)), ("W_xproj", (96, 2048)),
                ("W_dt", (2048, 64)), ("b_dt", (2048,)), ("A_log", (2048, 16)),
                ("D_skip", (2048,)), ("W_out", (1024, 2048))],
    "L1_ffn": [("gate_w", (64, 1024)), ("gate_bias", (64,)),
               ("eg_w", (64, 512, 1024)), ("eu_w", (64, 512, 1024)),
               ("ed_w", (64, 1024, 512)), ("sg_w", (1024, 1024)),
               ("su_w", (1024, 1024)), ("sd_w", (1024, 1024))],
    "L2_attn": [("w_qkv", (1024, 3072)), ("w_out", (1024, 1024)),
                ("b_out", (1024,)), ("logit_scale", (16, 1, 1))],
    "L2_ffn": [("W1", (4096, 1024)), ("b1", (4096,)), ("W2", (1024, 4096)), ("b2", (1024,))],
    "L3_attn": [("W_q", (1024, 2048)), ("W_k", (1024, 2048)), ("W_v", (1024, 2048)),
                ("W_o", (2048, 1024)), ("lambda_q1", (8, 128)), ("lambda_k1", (8, 128)),
                ("lambda_q2", (8, 128)), ("lambda_k2", (8, 128)), ("rms_scale", (256,))],
    "L3_ffn": [("W", (1024, 1024)), ("b", (1024,)), ("R", (8, 1024)),
               ("A", (8, 16, 1024)), ("B", (8, 1024, 16))],
}
_LAYERS = ["L0_attn", "L0_ffn", "L1_attn", "L1_ffn",
           "L2_attn", "L2_ffn", "L3_attn", "L3_ffn"]
_EMITTERS = {}


def _fill_emitters():
    _EMITTERS.update({
        "L0_attn": lambda ctx, sh, es, xi, xo, W: emit_l0_attn(ctx, sh, es, xi, xo, W, name="a0"),
        "L0_ffn": lambda ctx, sh, es, xi, xo, W: emit_ffn_quant(
            ctx, sh, es, xi, xo, W["W1"], W["b1"], W["W2"], W["b2"], name="f0"),
        "L1_attn": lambda ctx, sh, es, xi, xo, W: emit_l1_attn(ctx, sh, es, xi, xo, W, name="s1"),
        "L1_ffn": lambda ctx, sh, es, xi, xo, W: emit_l1_ffn(ctx, sh, es, xi, xo, W, name="m1"),
        "L2_attn": lambda ctx, sh, es, xi, xo, W: emit_l2_attn(ctx, sh, es, xi, xo, W, name="a2"),
        "L2_ffn": lambda ctx, sh, es, xi, xo, W: emit_ffn_quant(
            ctx, sh, es, xi, xo, W["W1"], W["b1"], W["W2"], W["b2"], name="f2"),
        "L3_attn": lambda ctx, sh, es, xi, xo, W: emit_l3_attn(ctx, sh, es, xi, xo, W, name="a3"),
        "L3_ffn": lambda ctx, sh, es, xi, xo, W: emit_l3_ffn(ctx, sh, es, xi, xo, W, name="f3"),
    })


def build_fused():
    _fill_emitters()
    nc = bass.Bass("TRN2", target_bir_lowering=False)
    caps = declare_consts(nc)
    x0 = nc.dram_tensor("L0_attn_x", [4096, 1024], F32, kind="ExternalInput").ap()
    xout = nc.dram_tensor("x_out", [4096, 1024], F32, kind="ExternalOutput").ap()
    xa = nc.dram_tensor("xping", [4096, 1024], F32).ap()
    xb = nc.dram_tensor("xpong", [4096, 1024], F32).ap()
    waps = {}
    for layer in _LAYERS:
        for k, s in _SHAPES[layer]:
            waps[(layer, k)] = nc.dram_tensor(layer + "_" + k, list(s), F32,
                                              kind="ExternalInput").ap()
    ctx = Ctx(nc)
    with contextlib.ExitStack() as es_top:
        sh = Shared(ctx, es_top, caps)
        cur = x0
        scratch = [xa, xb]
        for li, layer in enumerate(_LAYERS):
            nxt = xout if li == len(_LAYERS) - 1 else scratch[li % 2]
            W = {k: waps[(layer, k)] for k, _ in _SHAPES[layer]}
            with contextlib.ExitStack() as es:
                stx = _EMITTERS[layer](ctx, sh, es, cur, nxt, W)
                for _e in ENGS:
                    ctx.wait(_e, stx, stx.v)
                    for bk in sh.pk + [sh.pt, sh.pb]:
                        ctx.wait(_e, bk.ev, bk.ev.v)
            cur = nxt
        ctx.emit()
    return nc


_FUSED_NC = None


def kernel(**inputs):
    global _FUSED_NC
    if _FUSED_NC is None:
        _FUSED_NC = build_fused()
    nc = _FUSED_NC
    cons = make_consts()
    x = np.ascontiguousarray(np.asarray(inputs["L0_attn_x"], dtype=np.float32))
    base = dict(cons)
    for layer in _LAYERS:
        for k, _ in _SHAPES[layer]:
            nm = layer + "_" + k
            base[nm] = np.ascontiguousarray(np.asarray(inputs[nm], dtype=np.float32))
    in_maps = []
    for c in range(8):
        m = dict(base)
        m["L0_attn_x"] = np.ascontiguousarray(x[c])
        in_maps.append(m)
    res = run_bass_kernel_spmd(nc, in_maps, core_ids=list(range(8)))
    return np.stack([res.results[c]["x_out"] for c in range(8)]).astype(np.float32)


# version 07
# speedup vs baseline: 1.0138x; 1.0138x over previous
import contextlib
import numpy as np
import ml_dtypes
import concourse.bass as bass
import concourse.mybir as mybir

F32 = mybir.dt.float32
BF16 = mybir.dt.bfloat16
AF = mybir.ActivationFunctionType
ALU = mybir.AluOpType
AX = mybir.AxisListType

NEG_BIG = -1.0e30
MAGIC = 12582912.0
ENGS = ("sync", "scalar", "vector", "tensor", "gpsimd")


class Sem:
    __slots__ = ("h", "v", "name")

    def __init__(self, nc, name):
        self.h = nc.alloc_semaphore(name=name)
        self.v = 0
        self.name = name


class Ctx:
    def __init__(self, nc):
        self.nc = nc
        self.q = {e: [] for e in ENGS}
        self._n = 0
        self._pool = {}

    def sem(self, name=None):
        self._n += 1
        if name is None:
            return Sem(self.nc, f"s{self._n}")
        key = name[2:] if len(name) > 2 else name
        if key not in self._pool:
            self._pool[key] = Sem(self.nc, key)
        return self._pool[key]

    def on(self, eng, fn):
        self.q[eng].append(fn)

    def wait(self, eng, sem, val=None):
        v = sem.v if val is None else val
        self.on(eng, lambda e, h=sem.h, v=v: e.wait_ge(h, v))

    def waits(self, eng, pairs):
        for (s, v) in pairs:
            self.wait(eng, s, v)

    def run(self, eng, fn, inc=None, k=1):
        if inc is None:
            self.on(eng, fn)
            return None
        inc.v += k
        self.on(eng, lambda e, f=fn, h=inc.h, kk=k: f(e).then_inc(h, kk))
        return inc.v

    def dma(self, eng, out, in_, sem, **kw):
        sem.v += 16
        self.on(eng, lambda e, o=out, i=in_, h=sem.h, kw=kw: e.dma_start(out=o, in_=i, **kw).then_inc(h, 16))
        return sem.v

    def emit(self):
        nc = self.nc
        with nc.Block() as block:
            @block.sync
            def _(e):
                for f in self.q["sync"]:
                    f(e)

            @block.scalar
            def _(e):
                for f in self.q["scalar"]:
                    f(e)

            @block.vector
            def _(e):
                for f in self.q["vector"]:
                    f(e)

            @block.tensor
            def _(e):
                for f in self.q["tensor"]:
                    f(e)

            @block.gpsimd
            def _(e):
                for f in self.q["gpsimd"]:
                    f(e)




def ecopy(e, out, in_):
    if hasattr(e, "tensor_copy"):
        return e.tensor_copy(out=out, in_=in_)
    return e.copy(out=out, in_=in_)


class PBank:

    def __init__(self, ctx, ap, name):
        self.ctx = ctx
        self.ap = ap
        self.wr = ctx.sem(name + "_wr")
        self.ev = ctx.sem(name + "_ev")

    def group(self, mms, evacs, evac_eng="scalar", extra_waits=(), evac_extra_waits=()):
        ctx = self.ctx
        ctx.wait("tensor", self.ev, self.ev.v)
        for (s, v) in extra_waits:
            ctx.wait("tensor", s, v)
        for i, fn in enumerate(mms):
            ctx.run("tensor", fn, inc=self.wr if i == len(mms) - 1 else None)
        wr_t = self.wr.v
        evs = [f if isinstance(f, tuple) else (evac_eng, f) for f in evacs]
        seen = set()
        for j, (eng, fn) in enumerate(evs):
            if eng not in seen:
                ctx.wait(eng, self.wr, wr_t)
                for (s, v) in evac_extra_waits:
                    ctx.wait(eng, s, v)
                seen.add(eng)
            ctx.run(eng, fn, inc=self.ev if j == len(evs) - 1 else None)
        return self.ev.v



def make_consts():
    bf = ml_dtypes.bfloat16
    i = np.arange(128)
    same = (i[:, None] // 4) == (i[None, :] // 4)
    diag = i[:, None] == i[None, :]
    cons = {
        "c_ident": np.eye(128, dtype=bf),
        "c_ones1x128": np.ones((1, 128), dtype=bf),
        "c_ones128x1": np.ones((128, 1), dtype=bf),
        "c_onesf32": np.ones((1, 128), dtype=np.float32),
        "c_onescolf32": np.ones((128, 1), dtype=np.float32),
        "c_identf32": np.eye(128, dtype=np.float32),
        "c_epscols": np.tile(np.array([[1e-5, 1e-6, 1e-8, 1e-12]], dtype=np.float32), (128, 1)),
        "c_maskseg": np.where(same & ~diag, 0.0, NEG_BIG).astype(np.float32),
        "c_bo2": (np.arange(128)[:, None] // 64 == np.arange(2)[None, :]).astype(np.float32),
        "c_bc2": (np.arange(128)[None, :] // 64 == np.arange(2)[:, None]).astype(np.float32),
        "c_sel16": np.stack([(np.arange(16)[:, None] == n).astype(np.float32) * np.ones((1, 128), np.float32) for n in range(16)], 1).reshape(16, 2048),
        "c_sel8": np.stack([(np.arange(8)[:, None] == h).astype(np.float32) * np.ones((1, 128), np.float32) for h in range(8)], 1).reshape(8, 1024),
        "c_sel2": np.stack([(np.arange(16)[:, None] == 2 * ft + (np.arange(128)[None, :] >= 64)).astype(np.float32) for ft in range(8)], 1).reshape(16, 1024),
        "c_tril": np.where(i[:, None] >= i[None, :], 0.0, NEG_BIG).astype(np.float32),
    }
    return cons


def declare_consts(nc):
    return {
        "ident": nc.dram_tensor("c_ident", [128, 128], BF16, kind="ExternalInput").ap(),
        "ones1x128": nc.dram_tensor("c_ones1x128", [1, 128], BF16, kind="ExternalInput").ap(),
        "ones128x1": nc.dram_tensor("c_ones128x1", [128, 1], BF16, kind="ExternalInput").ap(),
        "onesf32": nc.dram_tensor("c_onesf32", [1, 128], F32, kind="ExternalInput").ap(),
        "onescolf32": nc.dram_tensor("c_onescolf32", [128, 1], F32, kind="ExternalInput").ap(),
        "identf32": nc.dram_tensor("c_identf32", [128, 128], F32, kind="ExternalInput").ap(),
        "epscols": nc.dram_tensor("c_epscols", [128, 4], F32, kind="ExternalInput").ap(),
        "maskseg": nc.dram_tensor("c_maskseg", [128, 128], F32, kind="ExternalInput").ap(),
        "bo2": nc.dram_tensor("c_bo2", [128, 2], F32, kind="ExternalInput").ap(),
        "bc2": nc.dram_tensor("c_bc2", [2, 128], F32, kind="ExternalInput").ap(),
        "sel2": nc.dram_tensor("c_sel2", [16, 1024], F32, kind="ExternalInput").ap(),
        "sel8": nc.dram_tensor("c_sel8", [8, 1024], F32, kind="ExternalInput").ap(),
        "sel16": nc.dram_tensor("c_sel16", [16, 2048], F32, kind="ExternalInput").ap(),
        "tril": nc.dram_tensor("c_tril", [128, 128], F32, kind="ExternalInput").ap(),
    }


class Shared:
    def __init__(self, ctx, es, caps):
        nc = ctx.nc
        self.pk = [PBank(ctx, es.enter_context(nc.psum_tensor(f"pk{i}", [128, 512], F32)), f"pk{i}")
                   for i in range(6)]
        self.pt = PBank(ctx, es.enter_context(nc.psum_tensor("pt", [128, 1024], BF16)), "pt")
        self.pb = PBank(ctx, es.enter_context(nc.psum_tensor("pb", [128, 512], F32)), "pb")
        self.ident = es.enter_context(nc.sbuf_tensor("ident", [128, 128], BF16, side="right"))
        self.ones1 = es.enter_context(nc.sbuf_tensor("ones1", [1, 128], BF16, side="right"))
        self.onescol = es.enter_context(nc.sbuf_tensor("onescol", [128, 1], BF16, side="right"))
        self.onesf32 = es.enter_context(nc.sbuf_tensor("onesf32", [1, 128], F32, side="right"))
        self.onescolf32 = es.enter_context(nc.sbuf_tensor("onescolf32", [128, 1], F32, side="right"))
        self.identf32 = es.enter_context(nc.sbuf_tensor("identf32", [128, 128], F32, side="right"))
        self.epscols = es.enter_context(nc.sbuf_tensor("epscols", [128, 4], F32, side="right"))
        self.maskseg = es.enter_context(nc.sbuf_tensor("maskseg", [128, 128], F32, side="right"))
        self.bo2 = es.enter_context(nc.sbuf_tensor("bo2", [128, 2], F32, side="right"))
        self.bc2 = es.enter_context(nc.sbuf_tensor("bc2", [2, 128], F32, side="right"))
        self.sel2 = es.enter_context(nc.sbuf_tensor("sel2", [16, 1024], F32, side="right"))
        self.sel8 = es.enter_context(nc.sbuf_tensor("sel8", [8, 1024], F32, side="right"))
        self.sel16 = es.enter_context(nc.sbuf_tensor("sel16", [16, 2048], F32, side="right"))
        self.tril = es.enter_context(nc.sbuf_tensor("tril", [128, 128], F32, side="right"))
        cld = ctx.sem("cld")
        ctx.dma("sync", self.ident[:, :], caps["ident"], cld)
        ctx.dma("sync", self.ones1[:, :], caps["ones1x128"], cld)
        ctx.dma("sync", self.onescol[:, :], caps["ones128x1"], cld)
        ctx.dma("sync", self.onesf32[:, :], caps["onesf32"], cld)
        ctx.dma("sync", self.onescolf32[:, :], caps["onescolf32"], cld)
        ctx.dma("sync", self.identf32[:, :], caps["identf32"], cld)
        ctx.dma("sync", self.epscols[:, :], caps["epscols"], cld)
        ctx.dma("sync", self.maskseg[:, :], caps["maskseg"], cld)
        ctx.dma("sync", self.bo2[:, :], caps["bo2"], cld)
        ctx.dma("sync", self.bc2[:, :], caps["bc2"], cld)
        ctx.dma("sync", self.sel2[:, :], caps["sel2"], cld)
        ctx.dma("sync", self.sel8[:, :], caps["sel8"], cld)
        ctx.dma("sync", self.sel16[:, :], caps["sel16"], cld)
        ctx.dma("sync", self.tril[:, :], caps["tril"], cld)
        for e in ("scalar", "vector", "tensor", "gpsimd"):
            ctx.wait(e, cld, cld.v)



def emit_ln_tile(ctx, xs_ap, out_ap, tmp_sq, stat, D, sem_done, waits=(), eps=1e-5,
                 sv=None, sh=None, gain=1.0):
    assert sh is not None
    epscol = {1e-5: 0, 1e-6: 1, 1e-8: 2, 1e-12: 3}[eps]
    sv = sv or ctx.sem()
    prev_done = sem_done.v
    ctx.waits("scalar", waits)
    ctx.wait("scalar", sem_done, prev_done)
    ctx.run("scalar", lambda e, a=xs_ap, o=tmp_sq[:, 0:D]: e.activation(
        out=o, in_=a, func=AF.Square), inc=sv)
    sv_s1 = sv.v
    ctx.waits("vector", waits)
    ctx.wait("vector", sem_done, prev_done)
    ctx.on("vector", lambda e, a=xs_ap, o=stat[:, 0:1]: e.tensor_reduce(
        out=o, in_=a, axis=AX.X, op=ALU.add, negate=True))
    ctx.wait("vector", sv, sv_s1)
    ctx.run("vector", lambda e, a=tmp_sq[:, 0:D], o=stat[:, 1:2]: e.tensor_reduce(
        out=o, in_=a, axis=AX.X, op=ALU.add), inc=sv)
    sv_v2 = sv.v
    ctx.wait("scalar", sv, sv_v2)
    ctx.on("scalar", lambda e, s=stat, d=D: e.activation(
        out=s[:, 2:3], in_=s[:, 0:1], func=AF.Square, scale=1.0 / d))
    ctx.run("scalar", lambda e, s=stat, d=D: e.activation(
        out=s[:, 3:4], in_=s[:, 1:2], func=AF.Identity, scale=1.0 / d), inc=sv)
    sv_s3 = sv.v
    ctx.wait("vector", sv, sv_s3)
    ctx.run("vector", lambda e, s=stat: e.scalar_tensor_tensor(
        out=s[:, 4:5], in0=s[:, 2:3], scalar=-1.0, in1=s[:, 3:4],
        op0=ALU.mult, op1=ALU.add), inc=sv)
    sv_v3 = sv.v
    ctx.wait("scalar", sv, sv_v3)
    ctx.on("scalar", lambda e, s=stat, ec=sh.epscols, j=epscol: e.activation(
        out=s[:, 5:6], in_=s[:, 4:5], func=AF.Ln, bias=ec[:, j:j + 1]))
    ctx.run("scalar", lambda e, s=stat: e.activation(
        out=s[:, 5:6], in_=s[:, 5:6], func=AF.Exp, scale=-0.5), inc=sv)
    sv_s5 = sv.v
    if gain != 1.0:
        ctx.wait("vector", sv, sv_s5)
        ctx.run("vector", lambda e, s=stat, g=gain: e.tensor_scalar(
            out=s[:, 5:6], in0=s[:, 5:6], scalar1=g, scalar2=None, op0=ALU.mult), inc=sv)
        sv_s5 = sv.v
    ctx.wait("vector", sv, sv_s5)
    ctx.run("vector", lambda e, s=stat, d=D: e.scalar_tensor_tensor(
        out=s[:, 6:7], in0=s[:, 0:1], scalar=1.0 / d, in1=s[:, 5:6],
        op0=ALU.mult, op1=ALU.mult), inc=sv)
    sv_v4 = sv.v
    ctx.wait("scalar", sv, sv_v4)
    ctx.run("scalar", lambda e, a=xs_ap, o=out_ap, s=stat: e.activation(
        out=o, in_=a, func=AF.Identity, bias=s[:, 6:7], scale=s[:, 5:6]), inc=sem_done)
    return sem_done.v


def emit_postln_resid(ctx, o_ap, xs_ap, tmp_sq, stat, D, sem_done, waits=(), eps=1e-5, sv=None, sln=None, sh=None):
    sln = sln or ctx.sem()
    emit_ln_tile(ctx, o_ap, tmp_sq[:, 0:D], tmp_sq, stat, D, sln, waits=waits, eps=eps, sv=sv, sh=sh)
    ctx.wait("vector", sln, sln.v)
    ctx.run("vector", lambda e, a=xs_ap, t=tmp_sq[:, 0:D]: e.tensor_add(out=a, in0=a, in1=t), inc=sem_done)
    return sem_done.v


def emit_ffn_quant(ctx, sh, es, x_in, x_out, w1_ap, b1_ap, w2_ap, b2_ap, name="f"):
    nc = ctx.nc
    T, D, FF = 4096, 1024, 4096
    CH, NTT = 256, 2
    NCH = T // CH

    w1qt = es.enter_context(nc.sbuf_tensor(name + "w1qt", [128, 8, FF], BF16))
    w2qt = es.enter_context(nc.sbuf_tensor(name + "w2qt", [128, 32, D], BF16))
    b1t = es.enter_context(nc.sbuf_tensor(name + "b1t", [128, 32], F32))
    b2b = es.enter_context(nc.sbuf_tensor(name + "b2b", [128, D], F32))
    wld = [es.enter_context(nc.sbuf_tensor(name + f"wld{i}", [128, 1024], F32)) for i in range(2)]
    wbf = [es.enter_context(nc.sbuf_tensor(name + f"wbf{i}", [128, 1024], BF16)) for i in range(2)]
    gpart = es.enter_context(nc.sbuf_tensor(name + "gpart", [128, 64], F32))
    gsc = es.enter_context(nc.sbuf_tensor(name + "gsc", [1, 4], F32))
    ginv = es.enter_context(nc.sbuf_tensor(name + "ginv", [128, 2], F32))
    browf = es.enter_context(nc.sbuf_tensor(name + "brow", [32, 128], F32))
    _xs1 = es.enter_context(nc.sbuf_tensor(name + "xs0", [128, NTT, 1024], F32))
    xs = [_xs1, _xs1]
    xn = es.enter_context(nc.sbuf_tensor(name + "xn", [128, NTT, 1024], BF16))
    xnt = es.enter_context(nc.sbuf_tensor(name + "xnt", [128, 8, CH], BF16))
    ht = es.enter_context(nc.sbuf_tensor(name + "ht", [128, 16, CH], BF16))
    osb = es.enter_context(nc.sbuf_tensor(name + "osb", [128, NTT, 1024], F32))
    sq = es.enter_context(nc.sbuf_tensor(name + "sq", [128, 1024], F32))
    stat = es.enter_context(nc.sbuf_tensor(name + "stat", [128, 8], F32))

    ld = ctx.sem(name + "ld")
    gred = ctx.sem(name + "gred")
    qcons = ctx.sem(name + "qcons")
    ldx = ctx.sem(name + "ldx")
    gred0, qcons0 = gred.v, qcons.v
    stx = ctx.sem(name + "stx")
    sLN = ctx.sem(name + "sln")
    sOut = ctx.sem(name + "sout")
    sRes = ctx.sem(name + "sres")
    svA = ctx.sem(name + "svA")
    svB = ctx.sem(name + "svB")

    w1v = w1_ap
    w2v = w2_ap
    for i in range(32):
        b = i % 2
        if i >= 2:
            ctx.wait("sync", gred, gred0 + i - 1)
        ctx.dma("sync", wld[b][:, :], w1v[i * 128:(i + 1) * 128, :], ld)
        ctx.wait("vector", ld, ld.v)
        ctx.run("vector", lambda e, s=wld[b], g=gpart, i=i: e.tensor_reduce(
            out=g[:, i:i + 1], in_=s[:, :], axis=AX.X, op=ALU.add,
            apply_absolute_value=True), inc=gred)
    for i in range(32):
        b = i % 2
        ctx.wait("sync", gred, gred0 + 31 + i)
        ot, fc = i // 4, i % 4
        ctx.dma("sync", wld[b][:, :], w2v[ot * 128:(ot + 1) * 128, fc * 1024:(fc + 1) * 1024], ld)
        ctx.wait("vector", ld, ld.v)
        ctx.run("vector", lambda e, s=wld[b], g=gpart, i=i: e.tensor_reduce(
            out=g[:, 32 + i:33 + i], in_=s[:, :], axis=AX.X, op=ALU.add,
            apply_absolute_value=True), inc=gred)
    ctx.wait("vector", gred, gred.v)
    ctx.on("vector", lambda e, g=gpart: e.tensor_reduce(
        out=g[:, 0:1], in_=g[:, 0:32], axis=AX.X, op=ALU.add))
    ctx.run("vector", lambda e, g=gpart: e.tensor_reduce(
        out=g[:, 1:2], in_=g[:, 32:64], axis=AX.X, op=ALU.add), inc=gred)
    ctx.wait("vector", gred, gred.v)
    gred_all = gred.v
    sh.pb.group(
        [lambda e, g=gpart, p=sh.pb.ap, oc=sh.onescolf32: e.matmul(
            p[0:1, 0:2], oc[:, 0:1], g[:, 0:2], start=True, stop=True)],
        [("vector", lambda e, p=sh.pb.ap, o=gsc: e.tensor_copy(out=o[0:1, 0:2], in_=p[0:1, 0:2]))],
        extra_waits=[(gred, gred_all)])
    ctx.wait("vector", sh.pb.ev, sh.pb.ev.v)
    ctx.run("vector", lambda e, o=gsc: e.tensor_scalar(
        out=o[0:1, 2:4], in0=o[0:1, 0:2], scalar1=1.0 / (4096.0 * 1024.0), scalar2=1e-5,
        op0=ALU.mult, op1=ALU.add), inc=gred)
    ctx.wait("vector", gred, gred.v)
    ctx.run("vector", lambda e, o=gsc: e.reciprocal(out=o[0:1, 2:4], in_=o[0:1, 2:4]), inc=gred)
    gv = gred.v
    sh.pb.group(
        [lambda e, o=gsc, p=sh.pb.ap, of=sh.onesf32: e.matmul(
            p[:, 0:2], of[0:1, :], o[0:1, 2:4], start=True, stop=True)],
        [("vector", lambda e, p=sh.pb.ap, o=ginv: e.tensor_copy(out=o[:, 0:2], in_=p[:, 0:2]))],
        extra_waits=[(gred, gv)])
    ginv_ready = sh.pb.ev.v

    trw = []

    def quant_piece(b, wi, ldv, piece_idx):
        ctx.wait("vector", ld, ldv)
        ctx.wait("vector", sh.pb.ev, ginv_ready)
        if piece_idx >= 2:
            ctx.wait("vector", sh.pt.wr, trw[piece_idx - 2])
        ctx.on("vector", lambda e, s=wld[b], g=ginv, wi=wi: e.tensor_scalar(
            out=s[:, :], in0=s[:, :], scalar1=g[:, wi:wi + 1], scalar2=None, op0=ALU.mult))
        ctx.on("vector", lambda e, s=wld[b]: e.tensor_scalar(
            out=s[:, :], in0=s[:, :], scalar1=MAGIC, scalar2=-MAGIC, op0=ALU.add, op1=ALU.add))
        ctx.run("vector", lambda e, s=wld[b], d=wbf[b]: e.tensor_scalar(
            out=d[:, :], in0=s[:, :], scalar1=-1.0, scalar2=1.0, op0=ALU.max, op1=ALU.min), inc=qcons)

    for i in range(32):
        b = i % 2
        if i < 2:
            ctx.wait("sync", gred, gred_all)
        else:
            ctx.wait("sync", qcons, qcons0 + i - 1)
        ctx.dma("sync", wld[b][:, :], w1v[i * 128:(i + 1) * 128, :], ld)
        quant_piece(b, 0, ld.v, i)
        qv = qcons.v
        sh.pt.group(
            [lambda e, s=wbf[b], dt=dt, p=sh.pt.ap, idn=sh.ident: e.transpose(
                p[:, dt * 128:(dt + 1) * 128], s[:, dt * 128:(dt + 1) * 128], idn[:, :])
             for dt in range(8)],
            [lambda e, p=sh.pt.ap, d=w1qt, i=i: ecopy(e,
                out=d[:, :, i * 128:(i + 1) * 128],
                in_=p[:, :].rearrange("p (a b) -> p a b", a=8))],
            evac_eng="scalar", extra_waits=[(qcons, qv)])
        trw.append(sh.pt.wr.v)
    for i in range(32):
        b = i % 2
        ctx.wait("sync", qcons, qcons0 + 31 + i)
        ot, fc = i // 4, i % 4
        ctx.dma("sync", wld[b][:, :], w2v[ot * 128:(ot + 1) * 128, fc * 1024:(fc + 1) * 1024], ld)
        quant_piece(b, 1, ld.v, 32 + i)
        qv = qcons.v
        sh.pt.group(
            [lambda e, s=wbf[b], fs=fs, p=sh.pt.ap, idn=sh.ident: e.transpose(
                p[:, fs * 128:(fs + 1) * 128], s[:, fs * 128:(fs + 1) * 128], idn[:, :])
             for fs in range(8)],
            [lambda e, p=sh.pt.ap, d=w2qt, ot=ot, fc=fc: ecopy(e,
                out=d[:, fc * 8:(fc + 1) * 8, ot * 128:(ot + 1) * 128],
                in_=p[:, :].rearrange("p (a b) -> p a b", a=8))],
            evac_eng="scalar", extra_waits=[(qcons, qv)])
        trw.append(sh.pt.wr.v)

    bld = ctx.sem(name + "bld")
    ctx.dma("sync", browf[:, :], b1_ap.rearrange("(a p) -> a p", p=128), bld)
    bv = bld.v
    sh.pb.group(
        [lambda e, s=browf, p=sh.pb.ap, idn=sh.identf32: e.transpose(
            p[:, 0:32], s[:, :], idn[0:32, 0:32])],
        [("vector", lambda e, p=sh.pb.ap, d=b1t: e.tensor_copy(out=d[:, :], in_=p[:, 0:32]))],
        extra_waits=[(bld, bv)])
    b2row = es.enter_context(nc.sbuf_tensor(name + "b2row", [1, 1024], F32))
    ctx.dma("sync", b2row[0:1, :], b2_ap.rearrange("(a d) -> a d", a=1), bld)
    bv = bld.v
    for h in range(2):
        sh.pb.group(
            [lambda e, r=b2row, p=sh.pb.ap, of=sh.onesf32, h=h: e.matmul(
                p[:, 0:512], of[0:1, :], r[0:1, h * 512:(h + 1) * 512], start=True, stop=True)],
            [("vector", lambda e, p=sh.pb.ap, d=b2b, h=h: e.tensor_copy(
                out=d[:, h * 512:(h + 1) * 512], in_=p[:, 0:512]))],
            extra_waits=[(bld, bv)])
    pre_done = sh.pb.ev.v

    osb_free = [0]
    stx0 = stx.v
    for c in range(NCH):
        b = c % 2
        if c >= 1:
            ctx.wait("sync", stx, stx0 + c * 16)
        ctx.dma("sync", xs[b][:, :, :],
                x_in[c * CH:(c + 1) * CH, :].rearrange("(a p) d -> p a d", p=128), ldx)
        lv = ldx.v
        ptw = sh.pt.wr.v
        for tt in range(NTT):
            emit_ln_tile(ctx, xs[b][:, tt, :], xn[:, tt, :], sq, stat, D, sLN,
                         waits=[(ldx, lv), (sh.pt.wr, ptw), (sRes, sRes.v)], sv=svA, sh=sh)
        lnv = sLN.v
        for tt in range(NTT):
            sh.pt.group(
                [lambda e, s=xn, tt=tt, dt=dt, p=sh.pt.ap, idn=sh.ident: e.transpose(
                    p[:, dt * 128:(dt + 1) * 128], s[:, tt, dt * 128:(dt + 1) * 128], idn[:, :])
                 for dt in range(8)],
                [lambda e, p=sh.pt.ap, d=xnt, tt=tt: ecopy(e,
                    out=d[:, :, tt * 128:(tt + 1) * 128],
                    in_=p[:, :].rearrange("p (a b) -> p a b", a=8))],
                evac_eng="scalar", extra_waits=[(sLN, lnv)])
        xnt_t = sh.pt.ev.v
        pk_prev_ev = [sh.pk[i].ev.v for i in range(4)]
        for half in range(2):
            hbanks = [sh.pk[4], sh.pk[5]]
            outw = sOut.v
            for ft in range(16):
                fg = half * 16 + ft
                pkb = hbanks[ft % 2]
                if half == 0:
                    ew = [(sh.pk[i].ev, pk_prev_ev[i]) for i in range(4)]
                else:
                    ew = [(sOut, outw)]
                pkb.group(
                    [lambda e, w=w1qt, x_=xnt, dt=dt, fg=fg, p=pkb.ap: e.matmul(
                        p[:, 0:CH], w[:, dt, fg * 128:(fg + 1) * 128], x_[:, dt, :],
                        start=(dt == 0), stop=(dt == 7))
                     for dt in range(8)],
                    [lambda e, p=pkb.ap, d=ht, ft=ft, b1=b1t, fg=fg: e.activation(
                        out=d[:, ft, :], in_=p[:, 0:CH], func=AF.Relu, bias=b1[:, fg:fg + 1])],
                    evac_eng="scalar",
                    extra_waits=[(sh.pt.ev, xnt_t)],
                    evac_extra_waits=ew + [(sh.pb.ev, pre_done)])
            htv = [sh.pk[4].ev.v, sh.pk[5].ev.v]
            for tt in range(NTT):
                for oh in range(2):
                    pkb = sh.pk[tt * 2 + oh]
                    ctx.wait("tensor", pkb.ev, pkb.ev.v)
                    ctx.wait("tensor", sh.pk[4].ev, htv[0])
                    ctx.wait("tensor", sh.pk[5].ev, htv[1])
                    for ft in range(16):
                        def mmf(e, w=w2qt, h_=ht, ft=ft, tt=tt, oh=oh, p=pkb.ap, half=half):
                            return e.matmul(p[:, 0:512], h_[:, ft, tt * 128:(tt + 1) * 128],
                                            w[:, half * 16 + ft, oh * 512:(oh + 1) * 512],
                                            start=(half == 0 and ft == 0), stop=(half == 1 and ft == 15))
                        if half == 0 and tt == NTT - 1 and oh == 1 and ft == 15:
                            ctx.run("tensor", mmf, inc=sOut)
                        elif half == 1 and ft == 15:
                            ctx.run("tensor", mmf, inc=pkb.wr)
                        else:
                            ctx.on("tensor", mmf)
        for tt in range(NTT):
            for oh in range(2):
                pkb = sh.pk[tt * 2 + oh]
                ctx.wait("vector", pkb.wr, pkb.wr.v)
                ctx.wait("vector", sh.pb.ev, pre_done)
                ctx.wait("vector", sLN, osb_free[0])
                ctx.run("vector", lambda e, p=pkb.ap, d=osb, tt=tt, oh=oh, bb=b2b: e.tensor_tensor(
                    out=d[:, tt, oh * 512:(oh + 1) * 512], in0=p[:, 0:512],
                    in1=bb[:, oh * 512:(oh + 1) * 512], op=ALU.add), inc=pkb.ev)
        for tt in range(NTT):
            w = [(sh.pk[tt * 2].ev, sh.pk[tt * 2].ev.v),
                 (sh.pk[tt * 2 + 1].ev, sh.pk[tt * 2 + 1].ev.v), (sRes, sRes.v)]
            emit_postln_resid(ctx, osb[:, tt, :], xs[b][:, tt, :], sq, stat, D, sRes,
                              waits=w, sv=svB, sln=sLN, sh=sh)
        osb_free[0] = sLN.v
        rv = sRes.v
        ctx.wait("gpsimd", sRes, rv)
        ctx.dma("gpsimd", x_out[c * CH:(c + 1) * CH, :].rearrange("(a p) d -> p a d", p=128),
                xs[b][:, :, :], stx)
    ctx.wait("gpsimd", stx, stx.v)
    return stx


def emit_loadT(ctx, sh, w_ap, F, D, dst, wld, wbf, ld, cons):
    nft, ndt = F // 128, D // 128
    meta = ctx.meta = getattr(ctx, "meta", {})
    for ft in range(nft):
        b = ft % 2
        u = meta.setdefault("loadT_uses", 0)
        if u >= 2:
            ps, pv = meta["loadT_cons"][u - 2]
            ctx.wait("sync", ps, pv)
        ctx.dma("sync", wld[b][:, 0:D], w_ap[ft * 128:(ft + 1) * 128, :], ld)
        ctx.wait("scalar", ld, ld.v)
        if u >= 2:
            ctx.wait("scalar", sh.pt.wr, meta["loadT_trw"][u - 2])
        ctx.run("scalar", lambda e, s=wld[b], d=wbf[b], D=D: e.activation(
            out=d[:, 0:D], in_=s[:, 0:D], func=AF.Copy), inc=cons)
        meta.setdefault("loadT_cons", []).append((cons, cons.v))
        qv = cons.v
        for g0 in range(0, ndt, 8):
            g1 = min(ndt, g0 + 8)
            sh.pt.group(
                [lambda e, s=wbf[b], dt=dt, g0=g0, p=sh.pt.ap, idn=sh.ident: e.transpose(
                    p[:, (dt - g0) * 128:(dt - g0 + 1) * 128], s[:, dt * 128:(dt + 1) * 128], idn[:, :])
                 for dt in range(g0, g1)],
                [lambda e, p=sh.pt.ap, d=dst, g0=g0, g1=g1, ft=ft: ecopy(
                    e, out=d[:, g0:g1, ft * 128:(ft + 1) * 128],
                    in_=p[:, 0:(g1 - g0) * 128].rearrange("p (a b) -> p a b", a=g1 - g0))],
                evac_eng="scalar", extra_waits=[(cons, qv)])
        meta.setdefault("loadT_trw", []).append(sh.pt.wr.v)
        meta["loadT_uses"] = u + 1


def emit_load_vecT(ctx, sh, vec_ap, N, dst, ld):
    nc = ctx.nc
    na = N // 128
    assert na <= 128
    tmp = ctx.meta.get("vecT_tmp")
    ctx.wait("sync", sh.pb.wr, sh.pb.wr.v)
    ctx.dma("sync", tmp[0:na, 0:128], vec_ap.rearrange("(a p) -> a p", p=128), ld)
    lv = ld.v
    sh.pb.group(
        [lambda e, s=tmp, p=sh.pb.ap, idn=sh.identf32, na=na: e.transpose(
            p[:, 0:na], s[0:na, 0:128], idn[0:na, 0:na])],
        [("vector", lambda e, p=sh.pb.ap, d=dst, na=na: e.tensor_copy(
            out=d[:, 0:na], in_=p[:, 0:na]))],
        extra_waits=[(ld, lv)])
    return sh.pb.ev.v


def emit_bcast_row(ctx, sh, vec_ap, N, dst, ld, tmprow):
    ctx.wait("sync", sh.pb.wr, sh.pb.wr.v)
    ctx.dma("sync", tmprow[0:1, 0:N], vec_ap.rearrange("(a d) -> a d", a=1), ld)
    lv = ld.v
    for c0 in range(0, N, 512):
        c1 = min(N, c0 + 512)
        sh.pb.group(
            [lambda e, r=tmprow, p=sh.pb.ap, of=sh.onesf32, c0=c0, c1=c1: e.matmul(
                p[:, 0:c1 - c0], of[0:1, :], r[0:1, c0:c1], start=True, stop=True)],
            [("vector", lambda e, p=sh.pb.ap, d=dst, c0=c0, c1=c1: e.tensor_copy(
                out=d[:, c0:c1], in_=p[:, 0:c1 - c0]))],
            extra_waits=[(ld, lv)])
    return sh.pb.ev.v



def emit_l0_attn(ctx, sh, es, x_in, x_out, W, name="a0"):
    nc = ctx.nc
    T, D = 4096, 1024
    CH, NTT = 256, 2
    NCH = T // CH
    H, HD = 8, 128
    SCALE = 1.0 / float(np.sqrt(np.float32(128.0)))

    wqkT = es.enter_context(nc.sbuf_tensor(name + "wqkT", [128, 8, D], BF16))
    wvT = es.enter_context(nc.sbuf_tensor(name + "wvT", [128, 8, D], BF16))
    wqT = es.enter_context(nc.sbuf_tensor(name + "wqT", [128, 8, D], BF16))
    wkT = es.enter_context(nc.sbuf_tensor(name + "wkT", [128, 8, D], BF16))
    wvT2 = es.enter_context(nc.sbuf_tensor(name + "wvT2", [128, 8, D], BF16))
    woT = es.enter_context(nc.sbuf_tensor(name + "woT", [128, 8, D], BF16))
    wld = [es.enter_context(nc.sbuf_tensor(name + f"wld{i}", [128, 1024], F32)) for i in range(2)]
    wbf = [es.enter_context(nc.sbuf_tensor(name + f"wbf{i}", [128, 1024], BF16)) for i in range(2)]
    vec8 = es.enter_context(nc.sbuf_tensor(name + "vec8", [128, 128], F32))
    bqk = es.enter_context(nc.sbuf_tensor(name + "bqk", [128, 8], F32))
    bvv = es.enter_context(nc.sbuf_tensor(name + "bvv", [128, 8], F32))
    bq = es.enter_context(nc.sbuf_tensor(name + "bq", [128, 8], F32))
    bk = es.enter_context(nc.sbuf_tensor(name + "bk", [128, 8], F32))
    bv2B = es.enter_context(nc.sbuf_tensor(name + "bv2B", [128, 1024], F32))
    boB = es.enter_context(nc.sbuf_tensor(name + "boB", [128, 1024], F32))
    tmprow = es.enter_context(nc.sbuf_tensor(name + "tmprow", [1, 1024], F32))
    xs = [es.enter_context(nc.sbuf_tensor(name + f"xs{i}", [128, NTT, 1024], F32)) for i in range(2)]
    xn = es.enter_context(nc.sbuf_tensor(name + "xn", [128, NTT, 1024], BF16))
    xnt = es.enter_context(nc.sbuf_tensor(name + "xnt", [128, 8, CH], BF16))
    qkT = es.enter_context(nc.sbuf_tensor(name + "qkT", [128, 8, CH], BF16))
    viT = es.enter_context(nc.sbuf_tensor(name + "viT", [128, 8, CH], BF16))
    qTb = es.enter_context(nc.sbuf_tensor(name + "qTb", [128, 8, CH], BF16))
    kTb = es.enter_context(nc.sbuf_tensor(name + "kTb", [128, 8, CH], BF16))
    v_sb = es.enter_context(nc.sbuf_tensor(name + "v_sb", [128, NTT, 1024], BF16))
    oT = es.enter_context(nc.sbuf_tensor(name + "oT", [128, 8, CH], BF16))
    osb = es.enter_context(nc.sbuf_tensor(name + "osb", [128, NTT, 1024], F32))
    sc_sb = es.enter_context(nc.sbuf_tensor(name + "sc", [128, 128], F32))
    e_sb = es.enter_context(nc.sbuf_tensor(name + "e", [128, 128], BF16))
    at_sb = es.enter_context(nc.sbuf_tensor(name + "at", [128, 128], BF16))
    mst = es.enter_context(nc.sbuf_tensor(name + "mst", [128, 4], F32))
    sq = es.enter_context(nc.sbuf_tensor(name + "sq", [128, 1024], F32))
    stat = es.enter_context(nc.sbuf_tensor(name + "stat", [128, 8], F32))

    ld = ctx.sem(name + "ld")
    cons = ctx.sem(name + "cons")
    ldx = ctx.sem(name + "ldx")
    stx = ctx.sem(name + "stx")
    sLN = ctx.sem(name + "sLN")
    sRes = ctx.sem(name + "sRes")
    svA = ctx.sem(name + "svA")
    svB = ctx.sem(name + "svB")
    sAt = ctx.sem(name + "sAt")

    ctx.meta = getattr(ctx, "meta", {})
    ctx.meta["vecT_tmp"] = vec8

    emit_loadT(ctx, sh, W["proj_qk_w"], D, D, wqkT, wld, wbf, ld, cons)
    emit_loadT(ctx, sh, W["proj_v_w"], D, D, wvT, wld, wbf, ld, cons)
    emit_loadT(ctx, sh, W["in_proj_w"][0:1024, :], D, D, wqT, wld, wbf, ld, cons)
    emit_loadT(ctx, sh, W["in_proj_w"][1024:2048, :], D, D, wkT, wld, wbf, ld, cons)
    emit_loadT(ctx, sh, W["in_proj_w"][2048:3072, :], D, D, wvT2, wld, wbf, ld, cons)
    emit_loadT(ctx, sh, W["out_proj_w"], D, D, woT, wld, wbf, ld, cons)
    emit_load_vecT(ctx, sh, W["proj_qk_b"], D, bqk, ld)
    emit_load_vecT(ctx, sh, W["proj_v_b"], D, bvv, ld)
    emit_load_vecT(ctx, sh, W["in_proj_b"][0:1024], D, bq, ld)
    emit_load_vecT(ctx, sh, W["in_proj_b"][1024:2048], D, bk, ld)
    bv2_t = emit_bcast_row(ctx, sh, W["in_proj_b"][2048:3072], D, bv2B, ld, tmprow)
    bo_t = emit_bcast_row(ctx, sh, W["out_proj_b"], D, boB, ld, tmprow)
    ctx.wait("vector", sh.pb.ev, sh.pb.ev.v)
    ctx.run("vector", lambda e, b=bq, s=SCALE: e.tensor_scalar(
        out=b[:, :], in0=b[:, :], scalar1=s, scalar2=None, op0=ALU.mult), inc=cons)
    pre_done_pb = sh.pb.ev.v
    pre_done_cons = cons.v
    wq_trw = sh.pt.ev.v

    def fproj(src, wT, nsrc_t, dst, bias_col, scale_imm, extra_ev=()):
        for ft in range(8):
            pkb = sh.pk[4 + ft % 2]
            pkb.group(
                [lambda e, w=wT, s=src, dt=dt, ft=ft, p=pkb.ap: e.matmul(
                    p[:, 0:CH], w[:, dt, ft * 128:(ft + 1) * 128], s[:, dt, :],
                    start=(dt == 0), stop=(dt == 7))
                 for dt in range(8)],
                [lambda e, p=pkb.ap, d=dst, ft=ft, bc=bias_col, si=scale_imm: e.activation(
                    out=d[:, ft, :], in_=p[:, 0:CH], func=AF.Identity,
                    bias=bc[:, ft:ft + 1], scale=si)],
                evac_eng="scalar",
                extra_waits=[(sh.pt.ev, sh.pt.ev.v)],
                evac_extra_waits=list(extra_ev) + [(cons, pre_done_cons), (sh.pb.ev, pre_done_pb)])

    stx0 = stx.v
    for c in range(NCH):
        b = c % 2
        if c >= 2:
            ctx.wait("sync", stx, stx0 + (c - 1) * 16)
        ctx.dma("sync", xs[b][:, :, :],
                x_in[c * CH:(c + 1) * CH, :].rearrange("(a p) d -> p a d", p=128), ldx)
        lv = ldx.v
        ptw = sh.pt.wr.v
        for tt in range(NTT):
            emit_ln_tile(ctx, xs[b][:, tt, :], xn[:, tt, :], sq, stat, D, sLN,
                         waits=[(ldx, lv), (sh.pt.wr, ptw), (sRes, sRes.v)], sv=svA, sh=sh)
        lnv = sLN.v
        for tt in range(NTT):
            sh.pt.group(
                [lambda e, s=xn, tt=tt, dt=dt, p=sh.pt.ap, idn=sh.ident: e.transpose(
                    p[:, dt * 128:(dt + 1) * 128], s[:, tt, dt * 128:(dt + 1) * 128], idn[:, :])
                 for dt in range(8)],
                [lambda e, p=sh.pt.ap, d=xnt, tt=tt: ecopy(
                    e, out=d[:, :, tt * 128:(tt + 1) * 128],
                    in_=p[:, :].rearrange("p (a b) -> p a b", a=8))],
                evac_eng="scalar", extra_waits=[(sLN, lnv)])
        fproj(xnt, wqkT, NTT, qkT, bqk, 1.0)
        fproj(xnt, wvT, NTT, viT, bvv, 1.0)
        fproj(qkT, wqT, NTT, qTb, bq, SCALE)
        fproj(qkT, wkT, NTT, kTb, bk, 1.0)
        for tt in range(NTT):
            for fh in range(2):
                pkb = sh.pk[fh]
                pkb.group(
                    [lambda e, s=viT, w=wvT2, ft=ft, tt=tt, fh=fh, p=pkb.ap: e.matmul(
                        p[:, 0:512], s[:, ft, tt * 128:(tt + 1) * 128], w[:, ft, fh * 512:(fh + 1) * 512],
                        start=(ft == 0), stop=(ft == 7))
                     for ft in range(8)],
                    [("vector", lambda e, p=pkb.ap, d=v_sb, tt=tt, fh=fh, bb=bv2B: e.tensor_tensor(
                        out=d[:, tt, fh * 512:(fh + 1) * 512], in0=p[:, 0:512],
                        in1=bb[:, fh * 512:(fh + 1) * 512], op=ALU.add))],
                    evac_extra_waits=[(sh.pb.ev, pre_done_pb)])
        for tt in range(NTT):
            for h in range(H):
                pkb = sh.pk[2 + (h % 2)]
                pkb.group(
                    [lambda e, q=qTb, k=kTb, h=h, tt=tt, p=pkb.ap: e.matmul(
                        p[:, 0:128], q[:, h, tt * 128:(tt + 1) * 128], k[:, h, tt * 128:(tt + 1) * 128],
                        start=True, stop=True)],
                    [("vector", lambda e, p=pkb.ap, m=sh.maskseg: e.tensor_tensor(
                        out=sc_sb[:, :], in0=p[:, 0:128], in1=m[:, :], op=ALU.add))],
                    evac_extra_waits=[(sAt, sAt.v)])
                ctx.wait("vector", pkb.ev, pkb.ev.v)
                ctx.run("vector", lambda e: e.tensor_reduce(
                    out=mst[:, 0:1], in_=sc_sb[:, :], axis=AX.X, op=ALU.max, negate=True), inc=sAt)
                ctx.wait("scalar", sAt, sAt.v)
                ctx.run("scalar", lambda e: e.activation(
                    out=e_sb[:, :], in_=sc_sb[:, :], func=AF.Exp, bias=mst[:, 0:1]), inc=sAt)
                ctx.wait("vector", sAt, sAt.v)
                ctx.run("vector", lambda e: e.tensor_reduce(
                    out=mst[:, 1:2], in_=e_sb[:, :], axis=AX.X, op=ALU.add), inc=sAt)
                ctx.wait("vector", sAt, sAt.v)
                ctx.run("vector", lambda e: e.reciprocal(out=mst[:, 2:3], in_=mst[:, 1:2]), inc=sAt)
                ctx.wait("scalar", sAt, sAt.v)
                ctx.wait("scalar", sh.pk[2].wr, sh.pk[2].wr.v)
                ctx.wait("scalar", sh.pk[3].wr, sh.pk[3].wr.v)
                ctx.run("scalar", lambda e: e.activation(
                    out=at_sb[:, :], in_=e_sb[:, :], func=AF.Identity, scale=mst[:, 2:3]), inc=sAt)
                atv = sAt.v
                sh.pt.group(
                    [lambda e, p=sh.pt.ap, idn=sh.ident: e.transpose(p[:, 0:128], at_sb[:, :], idn[:, :])],
                    [lambda e, p=sh.pt.ap: ecopy(e, out=at_sb[:, :], in_=p[:, 0:128])],
                    evac_eng="scalar", extra_waits=[(sAt, atv)])
                attv = sh.pt.ev.v
                pkc = sh.pk[2 + ((h + 1) % 2)]
                pkc.group(
                    [lambda e, v=v_sb, h=h, tt=tt, p=pkc.ap: e.matmul(
                        p[:, 0:128], v[:, tt, h * 128:(h + 1) * 128], at_sb[:, :],
                        start=True, stop=True)],
                    [lambda e, p=pkc.ap, d=oT, h=h, tt=tt: ecopy(
                        e, out=d[:, h, tt * 128:(tt + 1) * 128], in_=p[:, 0:128])],
                    evac_eng="scalar",
                    extra_waits=[(sh.pt.ev, attv)])
        for tt in range(NTT):
            for oh in range(2):
                pkb = sh.pk[oh]
                pkb.group(
                    [lambda e, s=oT, w=woT, ft=ft, tt=tt, oh=oh, p=pkb.ap: e.matmul(
                        p[:, 0:512], s[:, ft, tt * 128:(tt + 1) * 128], w[:, ft, oh * 512:(oh + 1) * 512],
                        start=(ft == 0), stop=(ft == 7))
                     for ft in range(8)],
                    [("vector", lambda e, p=pkb.ap, d=osb, tt=tt, oh=oh, bb=boB: e.tensor_tensor(
                        out=d[:, tt, oh * 512:(oh + 1) * 512], in0=p[:, 0:512],
                        in1=bb[:, oh * 512:(oh + 1) * 512], op=ALU.add))],
                    evac_extra_waits=[(sh.pb.ev, pre_done_pb), (sLN, sLN.v)])
        for tt in range(NTT):
            w = [(sh.pk[0].ev, sh.pk[0].ev.v), (sh.pk[1].ev, sh.pk[1].ev.v), (sRes, sRes.v)]
            emit_postln_resid(ctx, osb[:, tt, :], xs[b][:, tt, :], sq, stat, D, sRes,
                              waits=w, sv=svB, sln=sLN, sh=sh)
        rv = sRes.v
        ctx.wait("gpsimd", sRes, rv)
        ctx.dma("gpsimd", x_out[c * CH:(c + 1) * CH, :].rearrange("(a p) d -> p a d", p=128),
                xs[b][:, :, :], stx)
    ctx.wait("gpsimd", stx, stx.v)
    return stx


def emit_l3_ffn(ctx, sh, es, x_in, x_out, W, name="f3"):
    nc = ctx.nc
    T, D, E, R = 4096, 1024, 8, 16
    CH, NTT = 256, 2
    NCH = T // CH

    wT = es.enter_context(nc.sbuf_tensor(name + "wT", [128, 8, D], BF16))
    rT = es.enter_context(nc.sbuf_tensor(name + "rT", [128, 8, E], BF16))
    aT = es.enter_context(nc.sbuf_tensor(name + "aT", [128, 8, E, R], BF16))
    bT = es.enter_context(nc.sbuf_tensor(name + "bT", [16, E, D], BF16))
    bB = es.enter_context(nc.sbuf_tensor(name + "bB", [128, D], F32))
    wld = [es.enter_context(nc.sbuf_tensor(name + f"wld{i}", [128, 1024], F32)) for i in range(2)]
    wbf = [es.enter_context(nc.sbuf_tensor(name + f"wbf{i}", [128, 1024], BF16)) for i in range(2)]
    tmprow = es.enter_context(nc.sbuf_tensor(name + "tmprow", [1, 1024], F32))
    vrow = es.enter_context(nc.sbuf_tensor(name + "vrow", [16, 1024], F32))
    vbf = es.enter_context(nc.sbuf_tensor(name + "vbf", [16, 1024], BF16))
    xs = [es.enter_context(nc.sbuf_tensor(name + f"xs{i}", [128, NTT, 1024], F32)) for i in range(2)]
    xn = es.enter_context(nc.sbuf_tensor(name + "xn", [128, NTT, 1024], BF16))
    xnt = es.enter_context(nc.sbuf_tensor(name + "xnt", [128, 8, CH], BF16))
    lg = es.enter_context(nc.sbuf_tensor(name + "lg", [128, NTT, E], F32))
    ge = es.enter_context(nc.sbuf_tensor(name + "ge", [128, E], F32))
    gaten = es.enter_context(nc.sbuf_tensor(name + "gaten", [128, NTT, E], F32))
    gst = es.enter_context(nc.sbuf_tensor(name + "gst", [128, 8], F32))
    gh = es.enter_context(nc.sbuf_tensor(name + "gh", [128, R], BF16))
    ght = es.enter_context(nc.sbuf_tensor(name + "ght", [16, E, 128], BF16))
    osb = es.enter_context(nc.sbuf_tensor(name + "osb", [128, NTT, 1024], F32))
    sq = es.enter_context(nc.sbuf_tensor(name + "sq", [128, 1024], F32))
    stat = es.enter_context(nc.sbuf_tensor(name + "stat", [128, 8], F32))

    ld = ctx.sem(name + "ld")
    cons = ctx.sem(name + "cons")
    ldx = ctx.sem(name + "ldx")
    stx = ctx.sem(name + "stx")
    sLN = ctx.sem(name + "sLN")
    sRes = ctx.sem(name + "sRes")
    svA = ctx.sem(name + "svA")
    svB = ctx.sem(name + "svB")
    sG = ctx.sem(name + "sG")

    ctx.meta = getattr(ctx, "meta", {})

    emit_loadT(ctx, sh, W["W"], D, D, wT, wld, wbf, ld, cons)
    bB_t = emit_bcast_row(ctx, sh, W["b"], D, bB, ld, tmprow)

    def loadT_small(src_ap, nrows, dst_fn):
        ctx.wait("sync", sh.pt.wr, sh.pt.wr.v)
        ctx.wait("sync", cons, cons.v)
        ctx.dma("sync", vrow[0:nrows, :], src_ap, ld)
        ctx.wait("scalar", ld, ld.v)
        ctx.run("scalar", lambda e, n=nrows: e.activation(
            out=vbf[0:n, :], in_=vrow[0:n, :], func=AF.Copy), inc=cons)
        cv = cons.v
        for dt in range(8):
            sh.pt.group(
                [lambda e, dt=dt, n=nrows, p=sh.pt.ap, idn=sh.ident: e.transpose(
                    p[:, 0:n], vbf[0:n, dt * 128:(dt + 1) * 128], idn[0:n, 0:n])],
                [lambda e, p=sh.pt.ap, dt=dt, n=nrows, dfn=dst_fn: ecopy(
                    e, out=dfn(dt), in_=p[:, 0:n])],
                evac_eng="scalar", extra_waits=[(cons, cv)])

    loadT_small(W["R"], 8, lambda dt: rT[:, dt, 0:8])
    for e_ in range(E):
        loadT_small(W["A"][e_, :, :], 16, lambda dt, e_=e_: aT[:, dt, e_, :])
    for e_ in range(E):
        ctx.wait("sync", sh.pt.wr, sh.pt.wr.v)
        ctx.wait("sync", cons, cons.v)
        for ot_ in range(8):
            ctx.dma("sync", wld[0][:, ot_ * 16:(ot_ + 1) * 16],
                    W["B"][e_, ot_ * 128:(ot_ + 1) * 128, :], ld)
        ctx.wait("scalar", ld, ld.v)
        ctx.run("scalar", lambda e: e.activation(
            out=wbf[0][:, 0:128], in_=wld[0][:, 0:128], func=AF.Copy), inc=cons)
        cv = cons.v
        for ot in range(8):
            sh.pt.group(
                [lambda e, ot=ot, p=sh.pt.ap, idn=sh.ident: e.transpose(
                    p[0:16, 0:128], wbf[0][:, ot * 16:(ot + 1) * 16], idn[:, :])],
                [lambda e, p=sh.pt.ap, ot=ot, e_=e_: ecopy(
                    e, out=bT[0:16, e_, ot * 128:(ot + 1) * 128], in_=p[0:16, 0:128])],
                evac_eng="scalar", extra_waits=[(cons, cv)])
    pre_pb = sh.pb.ev.v
    pre_cons = cons.v

    stx0 = stx.v
    for c in range(NCH):
        b = c % 2
        if c >= 2:
            ctx.wait("sync", stx, stx0 + (c - 1) * 16)
        ctx.dma("sync", xs[b][:, :, :],
                x_in[c * CH:(c + 1) * CH, :].rearrange("(a p) d -> p a d", p=128), ldx)
        lv = ldx.v
        ptw = sh.pt.wr.v
        for tt in range(NTT):
            emit_ln_tile(ctx, xs[b][:, tt, :], xn[:, tt, :], sq, stat, D, sLN,
                         waits=[(ldx, lv), (sh.pt.wr, ptw), (sRes, sRes.v)], sv=svA, sh=sh)
        lnv = sLN.v
        for tt in range(NTT):
            sh.pt.group(
                [lambda e, s=xn, tt=tt, dt=dt, p=sh.pt.ap, idn=sh.ident: e.transpose(
                    p[:, dt * 128:(dt + 1) * 128], s[:, tt, dt * 128:(dt + 1) * 128], idn[:, :])
                 for dt in range(8)],
                [lambda e, p=sh.pt.ap, d=xnt, tt=tt: ecopy(
                    e, out=d[:, :, tt * 128:(tt + 1) * 128],
                    in_=p[:, :].rearrange("p (a b) -> p a b", a=8))],
                evac_eng="scalar", extra_waits=[(sLN, lnv)])
        xnt_t = sh.pt.ev.v
        for tt in range(NTT):
            sh.pb.group(
                [lambda e, dt=dt, tt=tt, p=sh.pb.ap: e.matmul(
                    p[:, 0:E], rT[:, dt, 0:8] if False else xnt[:, dt, tt * 128:(tt + 1) * 128],
                    rT[:, dt, 0:8], start=(dt == 0), stop=(dt == 7))
                 for dt in range(8)],
                [("vector", lambda e, p=sh.pb.ap, tt=tt: e.tensor_copy(
                    out=lg[:, tt, :], in_=p[:, 0:E]))],
                extra_waits=[(sh.pt.ev, xnt_t)],
                evac_extra_waits=[(sG, sG.v)])
            ctx.wait("vector", sh.pb.ev, sh.pb.ev.v)
            ctx.run("vector", lambda e, tt=tt: e.tensor_reduce(
                out=gst[:, 0:1], in_=lg[:, tt, :], axis=AX.X, op=ALU.max, negate=True), inc=sG)
            ctx.wait("scalar", sG, sG.v)
            ctx.run("scalar", lambda e, tt=tt: e.activation(
                out=ge[:, :], in_=lg[:, tt, :], func=AF.Exp, bias=gst[:, 0:1]), inc=sG)
            ctx.wait("vector", sG, sG.v)
            ctx.run("vector", lambda e: e.tensor_reduce(
                out=gst[:, 1:2], in_=ge[:, :], axis=AX.X, op=ALU.add), inc=sG)
            ctx.wait("vector", sG, sG.v)
            ctx.run("vector", lambda e: e.reciprocal(out=gst[:, 2:3], in_=gst[:, 1:2]), inc=sG)
            ctx.wait("vector", sG, sG.v)
            ctx.run("vector", lambda e: e.tensor_scalar(
                out=ge[:, :], in0=ge[:, :], scalar1=gst[:, 2:3], scalar2=None, op0=ALU.mult), inc=sG)
            ctx.wait("vector", sG, sG.v)
            ctx.run("vector", lambda e: e.tensor_reduce(
                out=gst[:, 3:4], in_=ge[:, :], axis=AX.X, op=ALU.max), inc=sG)
            ctx.wait("vector", sG, sG.v)
            ctx.run("vector", lambda e: e.tensor_single_scalar(
                out=gst[:, 4:5] if False else sq[:, 0:8], in_=ge[:, :], scalar=gst[:, 3:4], op=ALU.is_ge), inc=sG)
            ctx.wait("vector", sG, sG.v)
            ctx.run("vector", lambda e: e.scalar_tensor_tensor(
                out=sq[:, 8:16], in0=sq[:, 0:8], scalar=-1e30, in1=ge[:, :],
                op0=ALU.mult, op1=ALU.add), inc=sG)
            ctx.wait("vector", sG, sG.v)
            ctx.run("vector", lambda e: e.tensor_reduce(
                out=gst[:, 5:6], in_=sq[:, 8:16], axis=AX.X, op=ALU.max), inc=sG)
            ctx.wait("vector", sG, sG.v)
            ctx.run("vector", lambda e: e.tensor_single_scalar(
                out=sq[:, 0:8], in_=ge[:, :], scalar=gst[:, 5:6], op=ALU.is_ge), inc=sG)
            ctx.wait("vector", sG, sG.v)
            ctx.run("vector", lambda e: e.tensor_mul(
                out=ge[:, :], in0=ge[:, :], in1=sq[:, 0:8]), inc=sG)
            ctx.wait("vector", sG, sG.v)
            ctx.run("vector", lambda e: e.tensor_reduce(
                out=gst[:, 6:7], in_=ge[:, :], axis=AX.X, op=ALU.add), inc=sG)
            ctx.wait("vector", sG, sG.v)
            ctx.run("vector", lambda e: e.tensor_scalar(
                out=gst[:, 6:7], in0=gst[:, 6:7], scalar1=1e-6, scalar2=None, op0=ALU.add), inc=sG)
            ctx.wait("vector", sG, sG.v)
            ctx.run("vector", lambda e: e.reciprocal(out=gst[:, 7:8], in_=gst[:, 6:7]), inc=sG)
            ctx.wait("vector", sG, sG.v)
            ctx.run("vector", lambda e, tt=tt: e.tensor_scalar(
                out=gaten[:, tt, :], in0=ge[:, :], scalar1=gst[:, 7:8], scalar2=None, op0=ALU.mult), inc=sG)
            gv = sG.v
            for e_ in range(E):
                pkb = sh.pk[4 + e_ % 2]
                pkb.group(
                    [lambda e, dt=dt, tt=tt, e_=e_, p=pkb.ap: e.matmul(
                        p[:, 0:R], xnt[:, dt, tt * 128:(tt + 1) * 128], aT[:, dt, e_, :],
                        start=(dt == 0), stop=(dt == 7))
                     for dt in range(8)],
                    [lambda e, p=pkb.ap, tt=tt, e_=e_: e.activation(
                        out=gh[:, :], in_=p[:, 0:R], func=AF.Identity,
                        scale=gaten[:, tt, e_:e_ + 1])],
                    evac_eng="scalar",
                    extra_waits=[(sh.pt.ev, xnt_t)],
                    evac_extra_waits=[(sG, gv), (sh.pt.wr, sh.pt.wr.v)])
                ghv = pkb.ev.v
                sh.pt.group(
                    [lambda e, p=sh.pt.ap, idn=sh.ident: e.transpose(
                        p[0:16, 0:128], gh[:, :], idn[:, :])],
                    [lambda e, p=sh.pt.ap, e_=e_: ecopy(
                        e, out=ght[0:16, e_, :], in_=p[0:16, 0:128])],
                    evac_eng="scalar", extra_waits=[(pkb.ev, ghv)])
            ghtv = sh.pt.ev.v
            for oh in range(2):
                pkb = sh.pk[tt * 2 + oh]
                mms = [lambda e, dt=dt, tt=tt, oh=oh, p=pkb.ap: e.matmul(
                    p[:, 0:512], xnt[:, dt, tt * 128:(tt + 1) * 128], wT[:, dt, oh * 512:(oh + 1) * 512],
                    start=(dt == 0), stop=False) for dt in range(8)]
                mms += [lambda e, e_=e_, oh=oh, p=pkb.ap: e.matmul(
                    p[:, 0:512], ght[0:16, e_, :], bT[0:16, e_, oh * 512:(oh + 1) * 512],
                    start=False, stop=(e_ == E - 1)) for e_ in range(E)]
                pkb.group(
                    mms,
                    [("vector", lambda e, p=pkb.ap, tt=tt, oh=oh: e.tensor_tensor(
                        out=osb[:, tt, oh * 512:(oh + 1) * 512], in0=p[:, 0:512],
                        in1=bB[:, oh * 512:(oh + 1) * 512], op=ALU.add))],
                    extra_waits=[(sh.pt.ev, ghtv)],
                    evac_extra_waits=[(sh.pb.ev, pre_pb), (sLN, sLN.v)])
        for tt in range(NTT):
            w = [(sh.pk[tt * 2].ev, sh.pk[tt * 2].ev.v),
                 (sh.pk[tt * 2 + 1].ev, sh.pk[tt * 2 + 1].ev.v), (sRes, sRes.v)]
            emit_postln_resid(ctx, osb[:, tt, :], xs[b][:, tt, :], sq, stat, D, sRes,
                              waits=w, sv=svB, sln=sLN, sh=sh)
        rv = sRes.v
        ctx.wait("gpsimd", sRes, rv)
        ctx.dma("gpsimd", x_out[c * CH:(c + 1) * CH, :].rearrange("(a p) d -> p a d", p=128),
                xs[b][:, :, :], stx)
    ctx.wait("gpsimd", stx, stx.v)
    return stx


def emit_loadT_small(ctx, sh, src_ap, nrows, D, dst_fn, vrow, vbf, ld, cons):
    ctx.wait("sync", sh.pt.wr, sh.pt.wr.v)
    ctx.wait("sync", cons, cons.v)
    ctx.dma("sync", vrow[0:nrows, 0:D], src_ap, ld)
    ctx.wait("scalar", ld, ld.v)
    ctx.run("scalar", lambda e, n=nrows, D=D: e.activation(
        out=vbf[0:n, 0:D], in_=vrow[0:n, 0:D], func=AF.Copy), inc=cons)
    cv = cons.v
    for dt in range(D // 128):
        sh.pt.group(
            [lambda e, dt=dt, n=nrows, p=sh.pt.ap, idn=sh.ident: e.transpose(
                p[:, 0:n], vbf[0:n, dt * 128:(dt + 1) * 128], idn[0:n, 0:n])],
            [lambda e, p=sh.pt.ap, dt=dt, n=nrows, dfn=dst_fn: ecopy(
                e, out=dfn(dt), in_=p[:, 0:n])],
            evac_eng="scalar", extra_waits=[(cons, cv)])


def emit_l1_ffn(ctx, sh, es, x_in, x_out, W, name="m1", nexp=64, nchunks=4):
    nc = ctx.nc
    T, D, E, I = 4096, 1024, 64, 512
    CH2 = 1024
    NCH = nchunks
    NTT = CH2 // 128
    NQ = CH2 // 512

    egT_d = nc.dram_tensor(name + "egT", [E, 8, 128, I], BF16).ap()
    euT_d = nc.dram_tensor(name + "euT", [E, 8, 128, I], BF16).ap()
    edT_d = nc.dram_tensor(name + "edT", [E, 4, 128, D], BF16).ap()
    sgT_d = nc.dram_tensor(name + "sgT", [8, 128, D], BF16).ap()
    suT_d = nc.dram_tensor(name + "suT", [8, 128, D], BF16).ap()
    sdT_d = nc.dram_tensor(name + "sdT", [8, 128, D], BF16).ap()

    slab = es.enter_context(nc.sbuf_tensor(name + "slab", [128, 24576], BF16))

    def slabv(off, a, b):
        return slab[:, off:off + a * b].rearrange("p (a b) -> p a b", a=a)

    gwT = es.enter_context(nc.sbuf_tensor(name + "gwT", [128, 8, E], BF16))
    gbB = es.enter_context(nc.sbuf_tensor(name + "gbB", [128, E], F32))
    wld = [es.enter_context(nc.sbuf_tensor(name + f"wld{i}", [128, 1024], F32)) for i in range(2)]
    wbf = [es.enter_context(nc.sbuf_tensor(name + f"wbf{i}", [128, 1024], BF16)) for i in range(2)]
    vrow = es.enter_context(nc.sbuf_tensor(name + "vrow", [128, 1024], F32))
    vbf = es.enter_context(nc.sbuf_tensor(name + "vbf", [128, 1024], BF16))
    tmprow = es.enter_context(nc.sbuf_tensor(name + "tmprow", [1, 1024], F32))
    xs = es.enter_context(nc.sbuf_tensor(name + "xs", [128, 4, 1024], F32))
    xn = es.enter_context(nc.sbuf_tensor(name + "xn", [128, 4, 1024], BF16))
    xnt = es.enter_context(nc.sbuf_tensor(name + "xnt", [128, 8, CH2], BF16))
    hsT = es.enter_context(nc.sbuf_tensor(name + "hsT", [128, 8, CH2], BF16))
    hsg = es.enter_context(nc.sbuf_tensor(name + "hsg", [128, 512], BF16))
    acc = es.enter_context(nc.sbuf_tensor(name + "acc", [128, NTT, 1024], F32))
    lgs = es.enter_context(nc.sbuf_tensor(name + "lgs", [128, E], F32))
    msk = es.enter_context(nc.sbuf_tensor(name + "msk", [128, E], F32))
    ms2 = es.enter_context(nc.sbuf_tensor(name + "ms2", [128, E], F32))
    meq = es.enter_context(nc.sbuf_tensor(name + "meq", [128, E], F32))
    gsb = es.enter_context(nc.sbuf_tensor(name + "gsb", [128, 24], F32))
    comb = es.enter_context(nc.sbuf_tensor(name + "comb", [128, NTT, E], F32))
    sq = es.enter_context(nc.sbuf_tensor(name + "sq", [128, 1024], F32))
    stat = es.enter_context(nc.sbuf_tensor(name + "stat", [128, 8], F32))

    ld = ctx.sem(name + "ld")
    cons = ctx.sem(name + "cons")
    ldw = ctx.sem(name + "ldw")
    stp = ctx.sem(name + "stp")
    ldx = ctx.sem(name + "ldx")
    stx = ctx.sem(name + "stx")
    sLN = ctx.sem(name + "sLN")
    sRes = ctx.sem(name + "sRes")
    svA = ctx.sem(name + "svA")
    svB = ctx.sem(name + "svB")
    sG = ctx.sem(name + "sG")
    sAcc = ctx.sem(name + "sAcc")
    sXT = ctx.sem(name + "sXT")

    ctx.meta = getattr(ctx, "meta", {})

    def vstep(fn):
        ctx.wait("vector", sG, sG.v)
        ctx.run("vector", fn, inc=sG)

    emit_loadT_small(ctx, sh, W["gate_w"], 64, D, lambda dt: gwT[:, dt, :], vrow, vbf, ld, cons)
    emit_bcast_row(ctx, sh, W["gate_bias"], E, gbB, ld, tmprow)

    def prep_to_hbm(src_ap, F, Dd, dram_view, ndt):
        emit_loadT(ctx, sh, src_ap, F, Dd, slabv(0, ndt, F), wld, wbf, ld, cons)
        ctx.wait("gpsimd", sh.pt.ev, sh.pt.ev.v)
        ctx.dma("gpsimd", dram_view, slabv(0, ndt, F), stp)
        ctx.meta["l1_slab_guard"] = stp.v

    def slab_guarded_loadT(src_ap, F, Dd, ndt):
        g = ctx.meta.get("l1_slab_guard")
        if g is not None:
            ctx.wait("scalar", stp, g)
        emit_loadT(ctx, sh, src_ap, F, Dd, slabv(0, ndt, F), wld, wbf, ld, cons)

    def prep2(src_ap, F, Dd, dram_view, ndt):
        g = ctx.meta.get("l1_slab_guard")
        if g is not None:
            ctx.wait("scalar", stp, g)
        emit_loadT(ctx, sh, src_ap, F, Dd, slabv(0, ndt, F), wld, wbf, ld, cons)
        ctx.wait("gpsimd", sh.pt.ev, sh.pt.ev.v)
        ctx.dma("gpsimd", dram_view, slabv(0, ndt, F), stp)
        ctx.meta["l1_slab_guard"] = stp.v

    prep2(W["sg_w"], D, D, sgT_d.rearrange("a p d -> p a d"), 8)
    prep2(W["su_w"], D, D, suT_d.rearrange("a p d -> p a d"), 8)
    prep2(W["sd_w"], D, D, sdT_d.rearrange("a p d -> p a d"), 8)
    for e_ in range(nexp):
        prep2(W["eg_w"][e_], I, D, egT_d[e_].rearrange("a p d -> p a d"), 8)
        prep2(W["eu_w"][e_], I, D, euT_d[e_].rearrange("a p d -> p a d"), 8)
        prep2(W["ed_w"][e_], D, I, edT_d[e_].rearrange("a p d -> p a d"), 4)
    prep_done_stp = stp.v
    pre_cons = cons.v

    wsg = slabv(0, 8, 1024)
    wsu = slabv(8192, 8, 1024)
    wsd = slabv(16384, 8, 1024)

    def exview(slot, which):
        off = [0, 12288][slot] + {"eg": 0, "eu": 4096, "ed": 8192}[which]
        if which == "ed":
            return slabv(off, 4, 1024)
        return slabv(off, 8, 512)

    for c in range(NCH):
        for scc in range(2):
            ctx.wait("sync", sLN, sLN.v)
            ctx.wait("sync", sRes, sRes.v)
            ctx.wait("sync", stx, stx.v)
            ctx.dma("sync", xs[:, :, :],
                    x_in[c * CH2 + scc * 512: c * CH2 + (scc + 1) * 512, :]
                    .rearrange("(a p) d -> p a d", p=128), ldx)
            lv = ldx.v
            ptw = sh.pt.wr.v
            for tt in range(4):
                emit_ln_tile(ctx, xs[:, tt, :], xn[:, tt, :], sq, stat, D, sLN,
                             waits=[(ldx, lv), (sh.pt.wr, ptw), (sRes, sRes.v)], sv=svA, sh=sh)
            lnv = sLN.v
            for tt in range(4):
                sh.pt.group(
                    [lambda e, s=xn, tt=tt, dt=dt, p=sh.pt.ap, idn=sh.ident: e.transpose(
                        p[:, dt * 128:(dt + 1) * 128], s[:, tt, dt * 128:(dt + 1) * 128], idn[:, :])
                     for dt in range(8)],
                    [lambda e, p=sh.pt.ap, d=xnt, tt=tt, scc=scc: ecopy(
                        e, out=d[:, :, (scc * 4 + tt) * 128:(scc * 4 + tt + 1) * 128],
                        in_=p[:, :].rearrange("p (a b) -> p a b", a=8))],
                    evac_eng="scalar", extra_waits=[(sLN, lnv)],
                    evac_extra_waits=[(sXT, sXT.v)])
        xnt_t = sh.pt.ev.v
        for tt in range(NTT):
            sh.pb.group(
                [lambda e, dt=dt, tt=tt, p=sh.pb.ap: e.matmul(
                    p[:, 0:E], xnt[:, dt, tt * 128:(tt + 1) * 128], gwT[:, dt, :],
                    start=(dt == 0), stop=(dt == 7))
                 for dt in range(8)],
                [lambda e, p=sh.pb.ap: e.activation(
                    out=lgs[:, :], in_=p[:, 0:E], func=AF.Sigmoid)],
                evac_eng="scalar",
                extra_waits=[(sh.pt.ev, xnt_t)],
                evac_extra_waits=[(sG, sG.v), (cons, pre_cons)])
            ctx.wait("vector", sh.pb.ev, sh.pb.ev.v)
            ctx.run("vector", lambda e: e.tensor_add(out=lgs[:, :], in0=lgs[:, :], in1=gbB[:, :]), inc=sG)
            ctx.wait("vector", sG, sG.v)
            for g in range(8):
                last = g == 7
                ctx.run("vector", lambda e, g=g: e.tensor_reduce(
                    out=gsb[:, g:g + 1], in_=lgs[:, g * 8:(g + 1) * 8], axis=AX.X, op=ALU.max),
                    inc=sG if last else None)
            vstep(lambda e: e.tensor_copy(out=gsb[:, 8:16], in_=gsb[:, 0:8]))
            for it4 in range(4):
                vstep(lambda e: e.tensor_reduce(
                    out=gsb[:, 16:17], in_=gsb[:, 8:16], axis=AX.X, op=ALU.max))
                if it4 < 3:
                    vstep(lambda e: e.tensor_single_scalar(
                        out=gsb[:, 17:18] if False else meq[:, 0:8], in_=gsb[:, 8:16],
                        scalar=gsb[:, 16:17], op=ALU.is_ge))
                    vstep(lambda e: e.scalar_tensor_tensor(
                        out=gsb[:, 8:16], in0=meq[:, 0:8], scalar=-1e30, in1=gsb[:, 8:16],
                        op0=ALU.mult, op1=ALU.add))
            vstep(lambda e: e.tensor_single_scalar(
                out=gsb[:, 8:16], in_=gsb[:, 0:8], scalar=gsb[:, 16:17], op=ALU.is_ge))
            ctx.wait("vector", sG, sG.v)
            for g in range(8):
                last = g == 7
                ctx.run("vector", lambda e, g=g: e.tensor_scalar(
                    out=msk[:, g * 8:(g + 1) * 8], in0=lgs[:, g * 8:(g + 1) * 8],
                    scalar1=gsb[:, 8 + g:9 + g], scalar2=None, op0=ALU.mult),
                    inc=sG if last else None)
            vstep(lambda e: e.tensor_copy(out=ms2[:, :], in_=msk[:, :]))
            for it6 in range(6):
                vstep(lambda e: e.tensor_reduce(
                    out=gsb[:, 17:18], in_=msk[:, :], axis=AX.X, op=ALU.max))
                if it6 < 5:
                    vstep(lambda e: e.tensor_single_scalar(
                        out=meq[:, :], in_=msk[:, :], scalar=gsb[:, 17:18], op=ALU.is_ge))
                    vstep(lambda e: e.scalar_tensor_tensor(
                        out=msk[:, :], in0=meq[:, :], scalar=-1e30, in1=msk[:, :],
                        op0=ALU.mult, op1=ALU.add))
            vstep(lambda e: e.tensor_scalar(
                out=gsb[:, 18:19], in0=gsb[:, 17:18], scalar1=0.0, scalar2=None, op0=ALU.max))
            vstep(lambda e: e.tensor_single_scalar(
                out=meq[:, :], in_=ms2[:, :], scalar=gsb[:, 18:19], op=ALU.is_ge))
            vstep(lambda e: e.tensor_mul(out=ms2[:, :], in0=ms2[:, :], in1=meq[:, :]))
            vstep(lambda e: e.tensor_reduce(
                out=gsb[:, 19:20], in_=ms2[:, :], axis=AX.X, op=ALU.add))
            vstep(lambda e: e.tensor_scalar(
                out=gsb[:, 20:21], in0=gsb[:, 19:20], scalar1=1e-8, scalar2=None, op0=ALU.add))
            vstep(lambda e: e.reciprocal(out=gsb[:, 21:22], in_=gsb[:, 20:21]))
            vstep(lambda e, tt=tt: e.tensor_scalar(
                out=comb[:, tt, :], in0=ms2[:, :], scalar1=gsb[:, 21:22], scalar2=None, op0=ALU.mult))
        comb_t = sG.v
        ldw_guard = sXT.v
        ctx.wait("sync", sXT, ldw_guard)
        if c == 0:
            ctx.wait("sync", stp, prep_done_stp)
        ctx.dma("sync", wsg, sgT_d.rearrange("a p d -> p a d"), ldw)
        ctx.dma("sync", wsu, suT_d.rearrange("a p d -> p a d"), ldw)
        ldw_sh = ldw.v
        for ft in range(8):
            for nq in range(NQ):
                pk4, pk5 = sh.pk[4], sh.pk[5]
                pk4.group(
                    [lambda e, dt=dt, ft=ft, nq=nq, p=pk4.ap: e.matmul(
                        p[:, 0:512], wsg[:, dt, ft * 128:(ft + 1) * 128],
                        xnt[:, dt, nq * 512:(nq + 1) * 512], start=(dt == 0), stop=(dt == 7))
                     for dt in range(8)],
                    [lambda e, p=pk4.ap: e.activation(
                        out=hsg[:, :], in_=p[:, 0:512], func=AF.Silu)],
                    evac_eng="scalar",
                    extra_waits=[(ldw, ldw_sh), (sh.pt.ev, xnt_t)],
                    evac_extra_waits=[(pk5.ev, pk5.ev.v)])
                hv = pk4.ev.v
                pk5.group(
                    [lambda e, dt=dt, ft=ft, nq=nq, p=pk5.ap: e.matmul(
                        p[:, 0:512], wsu[:, dt, ft * 128:(ft + 1) * 128],
                        xnt[:, dt, nq * 512:(nq + 1) * 512], start=(dt == 0), stop=(dt == 7))
                     for dt in range(8)],
                    [("vector", lambda e, p=pk5.ap, ft=ft, nq=nq: e.tensor_tensor(
                        out=hsT[:, ft, nq * 512:(nq + 1) * 512], in0=p[:, 0:512],
                        in1=hsg[:, :], op=ALU.mult))],
                    extra_waits=[(ldw, ldw_sh)],
                    evac_extra_waits=[(pk4.ev, hv), (sXT, ldw_guard)])
        ctx.dma("sync", wsd, sdT_d.rearrange("a p d -> p a d"), ldw)
        ldw_sd = ldw.v
        hsT_t = sh.pk[5].ev.v
        for tt in range(NTT):
            for oh in range(2):
                pkb = sh.pk[(tt * 2 + oh) % 4]
                pkb.group(
                    [lambda e, ft=ft, tt=tt, oh=oh, p=pkb.ap: e.matmul(
                        p[:, 0:512], hsT[:, ft, tt * 128:(tt + 1) * 128],
                        wsd[:, ft, oh * 512:(oh + 1) * 512], start=(ft == 0), stop=(ft == 7))
                     for ft in range(8)],
                    [("vector", lambda e, p=pkb.ap, tt=tt, oh=oh: e.tensor_copy(
                        out=acc[:, tt, oh * 512:(oh + 1) * 512], in0=p[:, 0:512])
                      if False else e.tensor_copy(
                        out=acc[:, tt, oh * 512:(oh + 1) * 512], in_=p[:, 0:512]))],
                    extra_waits=[(ldw, ldw_sd), (sh.pk[5].ev, hsT_t)],
                    evac_extra_waits=[(sLN, sLN.v)])
        ctx.run("tensor", lambda e: e.nop(), inc=sXT)
        for e_ in range(nexp):
            slot = e_ % 2
            ctx.wait("sync", sXT, sXT.v if e_ == 0 else sXT.v - 1)
            ctx.dma("sync", exview(slot, "eg"), egT_d[e_].rearrange("a p d -> p a d"), ldw)
            ctx.dma("sync", exview(slot, "eu"), euT_d[e_].rearrange("a p d -> p a d"), ldw)
            ctx.dma("sync", exview(slot, "ed"), edT_d[e_].rearrange("a p d -> p a d"), ldw)
            ldw_e = ldw.v
            for it in range(4):
                for nq in range(NQ):
                    pk4, pk5 = sh.pk[4], sh.pk[5]
                    pk4.group(
                        [lambda e, dt=dt, it=it, nq=nq, slot=slot, p=pk4.ap: e.matmul(
                            p[:, 0:512], exview(slot, "eg")[:, dt, it * 128:(it + 1) * 128],
                            xnt[:, dt, nq * 512:(nq + 1) * 512], start=(dt == 0), stop=(dt == 7))
                         for dt in range(8)],
                        [lambda e, p=pk4.ap: e.activation(
                            out=hsg[:, :], in_=p[:, 0:512], func=AF.Silu)],
                        evac_eng="scalar",
                        extra_waits=[(ldw, ldw_e)],
                        evac_extra_waits=[(pk5.ev, pk5.ev.v)])
                    hv = pk4.ev.v
                    pk5.group(
                        [lambda e, dt=dt, it=it, nq=nq, slot=slot, p=pk5.ap: e.matmul(
                            p[:, 0:512], exview(slot, "eu")[:, dt, it * 128:(it + 1) * 128],
                            xnt[:, dt, nq * 512:(nq + 1) * 512], start=(dt == 0), stop=(dt == 7))
                         for dt in range(8)],
                        [("vector", lambda e, p=pk5.ap, it=it, nq=nq: e.tensor_tensor(
                            out=hsT[:, it, nq * 512:(nq + 1) * 512], in0=p[:, 0:512],
                            in1=hsg[:, :], op=ALU.mult))],
                        extra_waits=[(ldw, ldw_e)],
                        evac_extra_waits=[(pk4.ev, hv), (sXT, sXT.v)])
            hsTe_t = sh.pk[5].ev.v
            for tt in range(NTT):
                for oh in range(2):
                    pkb = sh.pk[(tt * 2 + oh) % 4]
                    pkb.group(
                        [lambda e, it=it, tt=tt, oh=oh, slot=slot, p=pkb.ap: e.matmul(
                            p[:, 0:512], hsT[:, it, tt * 128:(tt + 1) * 128],
                            exview(slot, "ed")[:, it, oh * 512:(oh + 1) * 512],
                            start=(it == 0), stop=(it == 3))
                         for it in range(4)],
                        [("vector", lambda e, p=pkb.ap, tt=tt, oh=oh, e_=e_: e.scalar_tensor_tensor(
                            out=acc[:, tt, oh * 512:(oh + 1) * 512], in0=p[:, 0:512],
                            scalar=comb[:, tt, e_:e_ + 1], in1=acc[:, tt, oh * 512:(oh + 1) * 512],
                            op0=ALU.mult, op1=ALU.add))],
                        extra_waits=[(sh.pk[5].ev, hsTe_t)],
                        evac_extra_waits=[(sG, comb_t)])
            ctx.run("tensor", lambda e: e.nop(), inc=sXT)
        ctx.run("vector", lambda e: e.engine_nop(), inc=sAcc)
        accv = sAcc.v
        for scc in range(2):
            ctx.wait("sync", sLN, sLN.v)
            ctx.wait("sync", sRes, sRes.v)
            ctx.wait("sync", stx, stx.v)
            ctx.dma("sync", xs[:, :, :],
                    x_in[c * CH2 + scc * 512: c * CH2 + (scc + 1) * 512, :]
                    .rearrange("(a p) d -> p a d", p=128), ldx)
            lv = ldx.v
            for tt in range(4):
                w = [(ldx, lv), (sAcc, accv), (sRes, sRes.v)]
                emit_postln_resid(ctx, acc[:, scc * 4 + tt, :], xs[:, tt, :], sq, stat, D, sRes,
                                  waits=w, sv=svB, sln=sLN, sh=sh)
            rv = sRes.v
            ctx.wait("gpsimd", sRes, rv)
            ctx.dma("gpsimd",
                    x_out[c * CH2 + scc * 512: c * CH2 + (scc + 1) * 512, :]
                    .rearrange("(a p) d -> p a d", p=128), xs[:, :, :], stx)
    ctx.wait("gpsimd", stx, stx.v)
    return stx


def emit_l2_attn(ctx, sh, es, x_in, x_out, W, name="a2"):
    nc = ctx.nc
    T, D, H, DH = 4096, 1024, 16, 64
    CH, NTT = 256, 2
    NCH = T // CH
    MAXLOG = float(np.log(1.0 / 0.01))

    qT_d = nc.dram_tensor(name + "qT", [1024, T], BF16).ap()
    kT_d = nc.dram_tensor(name + "kT", [1024, T], BF16).ap()
    v_d = nc.dram_tensor(name + "v", [T, 1024], BF16).ap()
    oT_d = nc.dram_tensor(name + "oT", [1024, T], BF16).ap()

    wqkv = es.enter_context(nc.sbuf_tensor(name + "wqkv", [128, 8, 3072], BF16))
    wout = es.enter_context(nc.sbuf_tensor(name + "wout", [128, 8, 1024], BF16))
    wld = [es.enter_context(nc.sbuf_tensor(name + f"wld{i}", [128, 1536], F32)) for i in range(2)]
    boB = es.enter_context(nc.sbuf_tensor(name + "boB", [128, 1024], F32))
    tmprow = es.enter_context(nc.sbuf_tensor(name + "tmprow", [1, 1024], F32))
    lsrow = es.enter_context(nc.sbuf_tensor(name + "lsr", [16, 2], F32))
    scB = es.enter_context(nc.sbuf_tensor(name + "scB", [128, 8], F32))
    negsc = es.enter_context(nc.sbuf_tensor(name + "negsc", [128, 16], F32))
    screp = es.enter_context(nc.sbuf_tensor(name + "screp", [1, 16], F32))
    xs = [es.enter_context(nc.sbuf_tensor(name + f"xs{i}", [128, NTT, 1024], F32)) for i in range(2)]
    xn = es.enter_context(nc.sbuf_tensor(name + "xn", [128, NTT, 1024], BF16))
    xnt = es.enter_context(nc.sbuf_tensor(name + "xnt", [128, 8, CH], BF16))
    rawT = es.enter_context(nc.sbuf_tensor(name + "rawT", [128, CH], BF16))
    sqf = es.enter_context(nc.sbuf_tensor(name + "sqf", [128, CH], F32))
    rn2 = es.enter_context(nc.sbuf_tensor(name + "rn2", [2, CH], F32))
    nrmT = es.enter_context(nc.sbuf_tensor(name + "nrmT", [128, CH], BF16))
    vtt = es.enter_context(nc.sbuf_tensor(name + "vtt", [128, NTT, 1024], BF16))
    sq = es.enter_context(nc.sbuf_tensor(name + "sq", [128, 1024], F32))
    stat = es.enter_context(nc.sbuf_tensor(name + "stat", [128, 8], F32))

    ld = ctx.sem(name + "ld")
    cons = ctx.sem(name + "cons")
    ldx = ctx.sem(name + "ldx")
    stx = ctx.sem(name + "stx")
    sta = ctx.sem(name + "sta")
    sLN = ctx.sem(name + "sLN")
    sRes = ctx.sem(name + "sRes")
    svA = ctx.sem(name + "svA")
    svB = ctx.sem(name + "svB")
    sN = ctx.sem(name + "sN")

    ctx.meta = getattr(ctx, "meta", {})

    for i in range(16):
        b = i % 2
        dt, half = i // 2, i % 2
        if i >= 2:
            ctx.wait("sync", cons, cons.v - 1)
        ctx.dma("sync", wld[b][:, :], W["w_qkv"][dt * 128:(dt + 1) * 128, half * 1536:(half + 1) * 1536], ld)
        ctx.wait("scalar", ld, ld.v)
        ctx.run("scalar", lambda e, s=wld[b], dt=dt, half=half: e.activation(
            out=wqkv[:, dt, half * 1536:(half + 1) * 1536], in_=s[:, :], func=AF.Copy), inc=cons)
    for dt in range(8):
        b = dt % 2
        ctx.wait("sync", cons, cons.v - 1)
        ctx.dma("sync", wld[b][:, 0:1024], W["w_out"][dt * 128:(dt + 1) * 128, :], ld)
        ctx.wait("scalar", ld, ld.v)
        ctx.run("scalar", lambda e, s=wld[b], dt=dt: e.activation(
            out=wout[:, dt, :], in_=s[:, 0:1024], func=AF.Copy), inc=cons)
    bo_t = emit_bcast_row(ctx, sh, W["b_out"], 1024, boB, ld, tmprow)
    ctx.dma("sync", lsrow[0:16, 0:1], W["logit_scale"].rearrange("a b c -> a (b c)"), ld)
    ctx.wait("vector", ld, ld.v)
    ctx.run("vector", lambda e: e.tensor_scalar(
        out=lsrow[0:16, 1:2], in0=lsrow[0:16, 0:1], scalar1=MAXLOG, scalar2=None, op0=ALU.min), inc=sN)
    ctx.wait("scalar", sN, sN.v)
    ctx.run("scalar", lambda e: e.activation(
        out=lsrow[0:16, 0:1], in_=lsrow[0:16, 1:2], func=AF.Exp), inc=sN)
    scv = sN.v
    for ft in range(8):
        sh.pb.group(
            [lambda e, ft=ft, p=sh.pb.ap: e.matmul(
                p[:, 0:1], sh.sel2[:, ft * 128:(ft + 1) * 128], lsrow[0:16, 0:1],
                start=True, stop=True)],
            [("vector", lambda e, p=sh.pb.ap, ft=ft: e.tensor_copy(
                out=scB[:, ft:ft + 1], in_=p[:, 0:1]))],
            extra_waits=[(sN, scv)])
    sh.pb.group(
        [lambda e, p=sh.pb.ap, idn=sh.identf32: e.transpose(
            p[0:1, 0:16], lsrow[0:16, 0:1], idn[0:16, 0:16])],
        [("vector", lambda e, p=sh.pb.ap: e.tensor_scalar(
            out=screp[0:1, :], in0=p[0:1, 0:16], scalar1=-1.0, scalar2=None, op0=ALU.mult))],
        extra_waits=[(sN, scv)])
    sh.pb.group(
        [lambda e, p=sh.pb.ap, of=sh.onesf32: e.matmul(
            p[:, 0:16], of[0:1, :], screp[0:1, :], start=True, stop=True)],
        [("vector", lambda e, p=sh.pb.ap: e.tensor_copy(out=negsc[:, :], in_=p[:, 0:16]))])
    pre_pb = sh.pb.ev.v
    pre_cons = cons.v

    for c in range(NCH):
        b = c % 2
        ctx.wait("sync", sLN, sLN.v)
        ctx.dma("sync", xs[b][:, :, :],
                x_in[c * CH:(c + 1) * CH, :].rearrange("(a p) d -> p a d", p=128), ldx)
        lv = ldx.v
        ptw = sh.pt.wr.v
        for tt in range(NTT):
            emit_ln_tile(ctx, xs[b][:, tt, :], xn[:, tt, :], sq, stat, D, sLN,
                         waits=[(ldx, lv), (sh.pt.wr, ptw), (sN, sN.v)], sv=svA, sh=sh)
        lnv = sLN.v
        for tt in range(NTT):
            sh.pt.group(
                [lambda e, s=xn, tt=tt, dt=dt, p=sh.pt.ap, idn=sh.ident: e.transpose(
                    p[:, dt * 128:(dt + 1) * 128], s[:, tt, dt * 128:(dt + 1) * 128], idn[:, :])
                 for dt in range(8)],
                [lambda e, p=sh.pt.ap, d=xnt, tt=tt: ecopy(
                    e, out=d[:, :, tt * 128:(tt + 1) * 128],
                    in_=p[:, :].rearrange("p (a b) -> p a b", a=8))],
                evac_eng="scalar", extra_waits=[(sLN, lnv)])
        xnt_t = sh.pt.ev.v
        for which, base in (("q", 0), ("k", 8)):
            for ft in range(8):
                pkb = sh.pk[4 + ft % 2]
                pkb.group(
                    [lambda e, dt=dt, ft=ft, base=base, p=pkb.ap: e.matmul(
                        p[:, 0:CH], wqkv[:, dt, (base + ft) * 128:(base + ft + 1) * 128],
                        xnt[:, dt, :], start=(dt == 0), stop=(dt == 7))
                     for dt in range(8)],
                    [("scalar", lambda e, p=pkb.ap: e.activation(
                        out=rawT[:, :], in_=p[:, 0:CH], func=AF.Copy)),
                     ("scalar", lambda e, p=pkb.ap: e.activation(
                        out=sqf[:, :], in_=p[:, 0:CH], func=AF.Square))],
                    extra_waits=[(sh.pt.ev, xnt_t), (cons, pre_cons)],
                    evac_extra_waits=[(sta, sta.v), (sN, sN.v), (sh.pb.ev, sh.pb.ev.v)])
                sqv = pkb.ev.v
                sh.pb.group(
                    [lambda e, p=sh.pb.ap: e.matmul(
                        p[0:2, 0:CH], sh.bo2[:, :], sqf[:, :], start=True, stop=True)],
                    [("scalar", lambda e, p=sh.pb.ap: e.activation(
                        out=rn2[0:2, :], in_=p[0:2, 0:CH], func=AF.Sqrt))],
                    extra_waits=[(pkb.ev, sqv)])
                ctx.wait("vector", sh.pb.ev, sh.pb.ev.v)
                ctx.run("vector", lambda e: e.tensor_scalar(
                    out=rn2[0:2, :], in0=rn2[0:2, :], scalar1=1e-12, scalar2=None, op0=ALU.max), inc=sN)
                ctx.wait("vector", sN, sN.v)
                ctx.run("vector", lambda e: e.reciprocal(out=rn2[0:2, :], in_=rn2[0:2, :]), inc=sN)
                rnv = sN.v
                sh.pb.group(
                    [lambda e, p=sh.pb.ap: e.matmul(
                        p[:, 0:CH], sh.bc2[:, :], rn2[0:2, :], start=True, stop=True)],
                    [("vector", (lambda e, p=sh.pb.ap, ft=ft: e.scalar_tensor_tensor(
                        out=nrmT[:, :], in0=rawT[:, :], scalar=scB[:, ft:ft + 1], in1=p[:, 0:CH],
                        op0=ALU.mult, op1=ALU.mult)) if which == "q" else
                     (lambda e, p=sh.pb.ap: e.tensor_tensor(
                        out=nrmT[:, :], in0=rawT[:, :], in1=p[:, 0:CH], op=ALU.mult)))],
                    extra_waits=[(sN, rnv)],
                    evac_extra_waits=[(sta, sta.v)])
                nv = sh.pb.ev.v
                ctx.wait("gpsimd", sh.pb.ev, nv)
                dst = qT_d if which == "q" else kT_d
                ctx.dma("gpsimd", dst[ft * 128:(ft + 1) * 128, c * CH:(c + 1) * CH], nrmT[:, :], sta)
        for ft in range(8):
            pkb = sh.pk[4 + ft % 2]
            pkb.group(
                [lambda e, dt=dt, ft=ft, p=pkb.ap: e.matmul(
                    p[:, 0:CH], wqkv[:, dt, (16 + ft) * 128:(16 + ft + 1) * 128],
                    xnt[:, dt, :], start=(dt == 0), stop=(dt == 7))
                 for dt in range(8)],
                [("scalar", lambda e, p=pkb.ap: e.activation(
                    out=rawT[:, :], in_=p[:, 0:CH], func=AF.Copy))],
                extra_waits=[(sh.pt.ev, xnt_t), (cons, pre_cons)],
                evac_extra_waits=[(sh.pt.wr, sh.pt.wr.v), (sh.pb.ev, sh.pb.ev.v)])
            rv_ = pkb.ev.v
            for tt in range(NTT):
                sh.pt.group(
                    [lambda e, tt=tt, p=sh.pt.ap, idn=sh.ident: e.transpose(
                        p[:, 0:128], rawT[:, tt * 128:(tt + 1) * 128], idn[:, :])],
                    [lambda e, p=sh.pt.ap, tt=tt, ft=ft: ecopy(
                        e, out=vtt[:, tt, ft * 128:(ft + 1) * 128], in_=p[:, 0:128])],
                    evac_eng="scalar", extra_waits=[(pkb.ev, rv_)],
                    evac_extra_waits=[(sta, sta.v)])
        vt_t = sh.pt.ev.v
        ctx.wait("gpsimd", sh.pt.ev, vt_t)
        ctx.dma("gpsimd", v_d[c * CH:(c + 1) * CH, :].rearrange("(a p) d -> p a d", p=128),
                vtt[:, :, :], sta)
    phaseA_sta = sta.v

    knh = es.enter_context(nc.sbuf_tensor(name + "knh", [64, T], BF16))
    qth = es.enter_context(nc.sbuf_tensor(name + "qth", [64, 512], BF16))
    vh = [es.enter_context(nc.sbuf_tensor(name + f"vh{i}", [128, 32, 65], BF16)) for i in range(2)]
    expT = es.enter_context(nc.sbuf_tensor(name + "expT", [128, 32, 512], BF16))
    lrow = es.enter_context(nc.sbuf_tensor(name + "lrow", [1, 512], F32))
    linvB = es.enter_context(nc.sbuf_tensor(name + "linvB", [64, 512], F32))
    oTn = es.enter_context(nc.sbuf_tensor(name + "oTn", [64, 512], BF16))
    ldh = ctx.sem(name + "ldh")
    ctx.wait("vector", sh.pb.ev, pre_pb)
    ctx.run("vector", lambda e: e.memset(vh[0][:, :, 64:65], 1.0), inc=sN)
    ctx.run("vector", lambda e: e.memset(vh[1][:, :, 64:65], 1.0), inc=sN)
    pk2 = sh.pk[2]
    for h in range(H):
        b = h % 2
        ctx.wait("sync", sh.pk[0].wr, sh.pk[0].wr.v)
        ctx.wait("sync", sh.pk[1].wr, sh.pk[1].wr.v)
        if h == 0:
            ctx.wait("sync", sta, phaseA_sta)
        ctx.dma("sync", knh[0:64, :], kT_d[h * 64:(h + 1) * 64, :], ldh)
        ctx.wait("sync", pk2.wr, pk2.wr.v)
        for quarter in range(4):
            ctx.dma("sync", vh[b][:, quarter * 8:(quarter + 1) * 8, 0:64],
                    v_d[quarter * 1024:(quarter + 1) * 1024, h * 64:(h + 1) * 64]
                    .rearrange("(a p) d -> p a d", p=128), ldh)
        ldh_h = ldh.v
        for qc in range(8):
            ctx.wait("sync", sh.pk[0].wr, sh.pk[0].wr.v)
            ctx.wait("sync", sh.pk[1].wr, sh.pk[1].wr.v)
            ctx.dma("sync", qth[0:64, :], qT_d[h * 64:(h + 1) * 64, qc * 512:(qc + 1) * 512], ldh)
            ldh_q = ldh.v
            for kt in range(32):
                pkb = sh.pk[kt % 2]
                pkb.group(
                    [lambda e, kt=kt, p=pkb.ap: e.matmul(
                        p[:, 0:512], knh[0:64, kt * 128:(kt + 1) * 128], qth[0:64, :],
                        start=True, stop=True)],
                    [("scalar", lambda e, p=pkb.ap, kt=kt, h=h: e.activation(
                        out=expT[:, kt, :], in_=p[:, 0:512], func=AF.Exp,
                        bias=negsc[:, h:h + 1]))],
                    extra_waits=[(ldh, ldh_q)],
                    evac_extra_waits=[(pk2.wr, pk2.wr.v)])
            ctx.wait("tensor", pk2.ev, pk2.ev.v)
            ctx.wait("tensor", sh.pk[0].ev, sh.pk[0].ev.v)
            ctx.wait("tensor", sh.pk[1].ev, sh.pk[1].ev.v)
            ctx.wait("tensor", ldh, ldh_h)
            for kt in range(32):
                ctx.run("tensor", lambda e, kt=kt, b=b: e.matmul(
                    pk2.ap[0:65, 0:512], vh[b][:, kt, :], expT[:, kt, :],
                    start=(kt == 0), stop=(kt == 31)),
                    inc=pk2.wr if kt == 31 else None)
            ctx.wait("vector", pk2.wr, pk2.wr.v)
            ctx.wait("vector", sta, sta.v)
            ctx.run("vector", lambda e: e.reciprocal(out=lrow[0:1, :], in_=pk2.ap[64:65, 0:512]), inc=sN)
            lr_v = sN.v
            sh.pb.group(
                [lambda e, p=sh.pb.ap, of=sh.onesf32: e.matmul(
                    p[0:64, 0:512], of[0:1, 0:64], lrow[0:1, :], start=True, stop=True)],
                [("scalar", lambda e, p=sh.pb.ap: e.activation(
                    out=linvB[0:64, :], in_=p[0:64, 0:512], func=AF.Copy))],
                extra_waits=[(sN, lr_v)])
            ctx.wait("vector", sh.pb.ev, sh.pb.ev.v)
            ctx.run("vector", lambda e: e.tensor_tensor(
                out=oTn[0:64, :], in0=pk2.ap[0:64, 0:512], in1=linvB[0:64, :],
                op=ALU.mult), inc=pk2.ev)
            ctx.wait("gpsimd", pk2.ev, pk2.ev.v)
            ctx.dma("gpsimd", oT_d[h * 64:(h + 1) * 64, qc * 512:(qc + 1) * 512],
                    oTn[0:64, :], sta)
    phaseB_sta = sta.v

    oth = es.enter_context(nc.sbuf_tensor(name + "oth", [128, 8, CH], BF16))
    osb = es.enter_context(nc.sbuf_tensor(name + "osb", [128, NTT, 1024], F32))
    for c in range(NCH):
        b = c % 2
        ctx.wait("sync", stx, stx.v)
        ctx.wait("sync", sLN, sLN.v)
        if c == 0:
            ctx.wait("sync", sta, phaseB_sta)
        ctx.dma("sync", xs[b][:, :, :],
                x_in[c * CH:(c + 1) * CH, :].rearrange("(a p) d -> p a d", p=128), ldx)
        ctx.wait("sync", sh.pk[0].wr, sh.pk[0].wr.v)
        ctx.wait("sync", sh.pk[1].wr, sh.pk[1].wr.v)
        ctx.wait("sync", sh.pk[2].wr, sh.pk[2].wr.v)
        ctx.wait("sync", sh.pk[3].wr, sh.pk[3].wr.v)
        ctx.dma("sync", oth[:, :, :],
                oT_d[:, c * CH:(c + 1) * CH].rearrange("(a p) t -> p a t", p=128), ldx)
        lv = ldx.v
        for tt in range(NTT):
            for oh in range(2):
                pkb = sh.pk[tt * 2 + oh]
                pkb.group(
                    [lambda e, ft=ft, tt=tt, oh=oh, p=pkb.ap: e.matmul(
                        p[:, 0:512], oth[:, ft, tt * 128:(tt + 1) * 128],
                        wout[:, ft, oh * 512:(oh + 1) * 512], start=(ft == 0), stop=(ft == 7))
                     for ft in range(8)],
                    [("vector", lambda e, p=pkb.ap, tt=tt, oh=oh: e.tensor_tensor(
                        out=osb[:, tt, oh * 512:(oh + 1) * 512], in0=p[:, 0:512],
                        in1=boB[:, oh * 512:(oh + 1) * 512], op=ALU.add))],
                    extra_waits=[(ldx, lv), (cons, pre_cons)],
                    evac_extra_waits=[(sh.pb.ev, pre_pb), (sLN, sLN.v)])
        for tt in range(NTT):
            w = [(sh.pk[tt * 2].ev, sh.pk[tt * 2].ev.v),
                 (sh.pk[tt * 2 + 1].ev, sh.pk[tt * 2 + 1].ev.v), (sRes, sRes.v), (ldx, lv)]
            emit_postln_resid(ctx, osb[:, tt, :], xs[b][:, tt, :], sq, stat, D, sRes,
                              waits=w, sv=svB, sln=sLN, sh=sh)
        rv = sRes.v
        ctx.wait("gpsimd", sRes, rv)
        ctx.dma("gpsimd", x_out[c * CH:(c + 1) * CH, :].rearrange("(a p) d -> p a d", p=128),
                xs[b][:, :, :], stx)
    ctx.wait("gpsimd", stx, stx.v)
    return stx


def emit_l3_attn(ctx, sh, es, x_in, x_out, W, name="a3"):
    nc = ctx.nc
    T, D, H, DH = 4096, 1024, 8, 128
    CH, NTT = 256, 2
    NCH = T // CH
    SC = 1.0 / float(np.sqrt(128.0))
    LAMI = 0.8

    qT_d = nc.dram_tensor(name + "qT", [2048, T], BF16).ap()
    kT_d = nc.dram_tensor(name + "kT", [2048, T], BF16).ap()
    v_d = nc.dram_tensor(name + "v", [T, 2048], BF16).ap()
    oT_d = nc.dram_tensor(name + "oT", [2048, T], BF16).ap()

    xs = [es.enter_context(nc.sbuf_tensor(name + f"xs{i}", [128, NTT, 1024], F32)) for i in range(2)]
    xn = es.enter_context(nc.sbuf_tensor(name + "xn", [128, NTT, 1024], BF16))
    xnt = es.enter_context(nc.sbuf_tensor(name + "xnt", [128, 8, CH], BF16))
    sq = es.enter_context(nc.sbuf_tensor(name + "sq", [128, 1024], F32))
    stat = es.enter_context(nc.sbuf_tensor(name + "stat", [128, 8], F32))
    lamt = es.enter_context(nc.sbuf_tensor(name + "lamt", [8, 640], F32))
    lamB = es.enter_context(nc.sbuf_tensor(name + "lamB", [128, 8], F32))
    rawT = es.enter_context(nc.sbuf_tensor(name + "rawT", [128, CH], BF16))

    ld = ctx.sem(name + "ld")
    cons = ctx.sem(name + "cons")
    ldx = ctx.sem(name + "ldx")
    stx = ctx.sem(name + "stx")
    sta = ctx.sem(name + "sta")
    sLN = ctx.sem(name + "sLN")
    sRes = ctx.sem(name + "sRes")
    svA = ctx.sem(name + "svA")
    svB = ctx.sem(name + "svB")
    sN = ctx.sem(name + "sN")

    ctx.meta = getattr(ctx, "meta", {})

    ctx.dma("sync", lamt[0:8, 0:128], W["lambda_q1"], ld)
    ctx.dma("sync", lamt[0:8, 128:256], W["lambda_k1"], ld)
    ctx.dma("sync", lamt[0:8, 256:384], W["lambda_q2"], ld)
    ctx.dma("sync", lamt[0:8, 384:512], W["lambda_k2"], ld)
    ctx.wait("vector", ld, ld.v)
    ctx.run("vector", lambda e: e.tensor_mul(
        out=lamt[0:8, 0:128], in0=lamt[0:8, 0:128], in1=lamt[0:8, 128:256]), inc=sN)
    ctx.run("vector", lambda e: e.tensor_mul(
        out=lamt[0:8, 256:384], in0=lamt[0:8, 256:384], in1=lamt[0:8, 384:512]), inc=sN)
    ctx.wait("vector", sN, sN.v)
    ctx.on("vector", lambda e: e.tensor_reduce(
        out=lamt[0:8, 512:513], in_=lamt[0:8, 0:128], axis=AX.X, op=ALU.add))
    ctx.run("vector", lambda e: e.tensor_reduce(
        out=lamt[0:8, 513:514], in_=lamt[0:8, 256:384], axis=AX.X, op=ALU.add), inc=sN)
    ctx.wait("scalar", sN, sN.v)
    ctx.run("scalar", lambda e: e.activation(
        out=lamt[0:8, 514:516], in_=lamt[0:8, 512:514], func=AF.Exp), inc=sN)
    ctx.wait("vector", sN, sN.v)
    ctx.run("vector", lambda e: e.scalar_tensor_tensor(
        out=lamt[0:8, 516:517], in0=lamt[0:8, 514:515], scalar=-1.0, in1=lamt[0:8, 515:516],
        op0=ALU.mult, op1=ALU.add), inc=sN)
    ctx.wait("vector", sN, sN.v)
    ctx.run("vector", lambda e: e.tensor_scalar(
        out=lamt[0:8, 516:517], in0=lamt[0:8, 516:517], scalar1=-LAMI, scalar2=None,
        op0=ALU.add), inc=sN)
    lam_t = sN.v
    for h in range(H):
        sh.pb.group(
            [lambda e, h=h, p=sh.pb.ap: e.matmul(
                p[:, 0:1], sh.sel8[:, h * 128:(h + 1) * 128], lamt[0:8, 516:517],
                start=True, stop=True)],
            [("vector", lambda e, p=sh.pb.ap, h=h: e.tensor_copy(
                out=lamB[:, h:h + 1], in_=p[:, 0:1]))],
            extra_waits=[(sN, lam_t)])
    pre_pb = sh.pb.ev.v

    with contextlib.ExitStack() as es2:
        wq = es2.enter_context(nc.sbuf_tensor(name + "wq", [128, 8, 2048], BF16))
        wk = es2.enter_context(nc.sbuf_tensor(name + "wk", [128, 8, 2048], BF16))
        wv = es2.enter_context(nc.sbuf_tensor(name + "wv", [128, 8, 2048], BF16))
        wld2 = [es2.enter_context(nc.sbuf_tensor(name + f"wld{i}", [128, 2048], F32)) for i in range(2)]
        vtt = es2.enter_context(nc.sbuf_tensor(name + "vtt", [128, NTT, 2048], BF16))
        for wi, (w_ap, dst) in enumerate([(W["W_q"], wq), (W["W_k"], wk), (W["W_v"], wv)]):
            for dt in range(8):
                b = dt % 2
                if wi + dt > 1:
                    ctx.wait("sync", cons, cons.v - 1)
                ctx.dma("sync", wld2[b][:, :], w_ap[dt * 128:(dt + 1) * 128, :], ld)
                ctx.wait("scalar", ld, ld.v)
                ctx.run("scalar", lambda e, s=wld2[b], d=dst, dt=dt: e.activation(
                    out=d[:, dt, :], in_=s[:, :], func=AF.Copy), inc=cons)
        pre_cons = cons.v
        for c in range(NCH):
            b = c % 2
            ctx.wait("sync", sLN, sLN.v)
            ctx.dma("sync", xs[b][:, :, :],
                    x_in[c * CH:(c + 1) * CH, :].rearrange("(a p) d -> p a d", p=128), ldx)
            lv = ldx.v
            ptw = sh.pt.wr.v
            for tt in range(NTT):
                emit_ln_tile(ctx, xs[b][:, tt, :], xn[:, tt, :], sq, stat, D, sLN,
                             waits=[(ldx, lv), (sh.pt.wr, ptw), (sN, sN.v)], sv=svA, sh=sh)
            lnv = sLN.v
            for tt in range(NTT):
                sh.pt.group(
                    [lambda e, s=xn, tt=tt, dt=dt, p=sh.pt.ap, idn=sh.ident: e.transpose(
                        p[:, dt * 128:(dt + 1) * 128], s[:, tt, dt * 128:(dt + 1) * 128], idn[:, :])
                     for dt in range(8)],
                    [lambda e, p=sh.pt.ap, d=xnt, tt=tt: ecopy(
                        e, out=d[:, :, tt * 128:(tt + 1) * 128],
                        in_=p[:, :].rearrange("p (a b) -> p a b", a=8))],
                    evac_eng="scalar", extra_waits=[(sLN, lnv)])
            xnt_t = sh.pt.ev.v
            for which, wsb, dst in (("q", wq, qT_d), ("k", wk, kT_d)):
                for ft in range(16):
                    pkb = sh.pk[4 + ft % 2]
                    pkb.group(
                        [lambda e, dt=dt, ft=ft, w=wsb, p=pkb.ap: e.matmul(
                            p[:, 0:CH], w[:, dt, ft * 128:(ft + 1) * 128],
                            xnt[:, dt, :], start=(dt == 0), stop=(dt == 7))
                         for dt in range(8)],
                        [("scalar", lambda e, p=pkb.ap, which=which: e.activation(
                            out=rawT[:, :], in_=p[:, 0:CH], func=AF.Copy,
                            scale=SC if which == "q" else 1.0))],
                        extra_waits=[(sh.pt.ev, xnt_t), (cons, pre_cons)],
                        evac_extra_waits=[(sta, sta.v)])
                    nv = pkb.ev.v
                    ctx.wait("gpsimd", pkb.ev, nv)
                    ctx.dma("gpsimd", dst[ft * 128:(ft + 1) * 128, c * CH:(c + 1) * CH],
                            rawT[:, :], sta)
            for ft in range(16):
                pkb = sh.pk[4 + ft % 2]
                pkb.group(
                    [lambda e, dt=dt, ft=ft, p=pkb.ap: e.matmul(
                        p[:, 0:CH], wv[:, dt, ft * 128:(ft + 1) * 128],
                        xnt[:, dt, :], start=(dt == 0), stop=(dt == 7))
                     for dt in range(8)],
                    [("scalar", lambda e, p=pkb.ap: e.activation(
                        out=rawT[:, :], in_=p[:, 0:CH], func=AF.Copy))],
                    extra_waits=[(sh.pt.ev, xnt_t), (cons, pre_cons)],
                    evac_extra_waits=[(sh.pt.wr, sh.pt.wr.v), (sta, sta.v)])
                rv_ = pkb.ev.v
                for tt in range(NTT):
                    sh.pt.group(
                        [lambda e, tt=tt, p=sh.pt.ap, idn=sh.ident: e.transpose(
                            p[:, 0:128], rawT[:, tt * 128:(tt + 1) * 128], idn[:, :])],
                        [lambda e, p=sh.pt.ap, tt=tt, ft=ft: ecopy(
                            e, out=vtt[:, tt, ft * 128:(ft + 1) * 128], in_=p[:, 0:128])],
                        evac_eng="scalar", extra_waits=[(pkb.ev, rv_)],
                        evac_extra_waits=[(sta, sta.v)])
            vt_t = sh.pt.ev.v
            ctx.wait("gpsimd", sh.pt.ev, vt_t)
            ctx.dma("gpsimd", v_d[c * CH:(c + 1) * CH, :].rearrange("(a p) d -> p a d", p=128),
                    vtt[:, :, :], sta)
    phaseA_sta = sta.v

    k1t = es.enter_context(nc.sbuf_tensor(name + "k1t", [128, 2, T], BF16))
    vh = es.enter_context(nc.sbuf_tensor(name + "vh", [128, 32, 256], BF16))
    qth = es.enter_context(nc.sbuf_tensor(name + "qth", [128, 2, 128], BF16))
    ssb = [es.enter_context(nc.sbuf_tensor(name + f"ssb{i}", [128, 4096], F32)) for i in range(2)]
    esb = [es.enter_context(nc.sbuf_tensor(name + f"esb{i}", [128, 4096], BF16)) for i in range(2)]
    atb = es.enter_context(nc.sbuf_tensor(name + "atb", [128, 32, 128], BF16))
    mst = es.enter_context(nc.sbuf_tensor(name + "mst", [128, 8], F32))
    ob = es.enter_context(nc.sbuf_tensor(name + "ob", [128, 256], BF16))
    ldh = ctx.sem(name + "ldh")
    sAt = ctx.sem(name + "sAt")
    pk2, pk3 = sh.pk[2], sh.pk[3]
    for h in range(H):
        ctx.wait("sync", sh.pk[0].wr, sh.pk[0].wr.v)
        ctx.wait("sync", sh.pk[1].wr, sh.pk[1].wr.v)
        if h == 0:
            ctx.wait("sync", sta, phaseA_sta)
        ctx.dma("sync", k1t[:, :, :],
                kT_d[h * 256:(h + 1) * 256, :].rearrange("(a p) t -> p a t", p=128), ldh)
        ctx.wait("sync", pk2.wr, pk2.wr.v)
        for quarter in range(4):
            ctx.dma("sync", vh[:, quarter * 8:(quarter + 1) * 8, :],
                    v_d[quarter * 1024:(quarter + 1) * 1024, h * 256:(h + 1) * 256]
                    .rearrange("(a p) d -> p a d", p=128), ldh)
        ldh_h = ldh.v
        for qt in range(32):
            ctx.wait("sync", sh.pk[0].wr, sh.pk[0].wr.v)
            ctx.wait("sync", sh.pk[1].wr, sh.pk[1].wr.v)
            ctx.wait("sync", sAt, sAt.v)
            ctx.dma("sync", qth[:, :, :],
                    qT_d[h * 256:(h + 1) * 256, qt * 128:(qt + 1) * 128]
                    .rearrange("(a p) t -> p a t", p=128), ldh)
            ldh_q = ldh.v
            W_k = (qt + 1) * 128
            nkc = (W_k + 511) // 512
            for half in range(2):
                pkb = sh.pk[half]
                for kc in range(nkc):
                    n = min(512, W_k - kc * 512)
                    pkb.group(
                        [lambda e, kc=kc, n=n, half=half, p=pkb.ap: e.matmul(
                            p[:, 0:n], qth[:, half, :], k1t[:, half, kc * 512:kc * 512 + n],
                            start=True, stop=True)],
                        [("vector", lambda e, p=pkb.ap, kc=kc, n=n, half=half: e.tensor_copy(
                            out=ssb[half][:, kc * 512:kc * 512 + n], in_=p[:, 0:n]))],
                        extra_waits=[(ldh, ldh_q)],
                        evac_extra_waits=[(sAt, sAt.v)])
                ctx.wait("vector", pkb.ev, pkb.ev.v)
                ctx.run("vector", lambda e, half=half, qt=qt: e.tensor_add(
                    out=ssb[half][:, qt * 128:(qt + 1) * 128],
                    in0=ssb[half][:, qt * 128:(qt + 1) * 128], in1=sh.tril[:, :]), inc=sAt)
                ctx.wait("vector", sAt, sAt.v)
                ctx.run("vector", lambda e, half=half, W_k=W_k: e.tensor_reduce(
                    out=mst[:, half:half + 1], in_=ssb[half][:, 0:W_k], axis=AX.X,
                    op=ALU.max, negate=True), inc=sAt)
                ctx.wait("scalar", sAt, sAt.v)
                ctx.run("scalar", lambda e, half=half, W_k=W_k: e.activation(
                    out=esb[half][:, 0:W_k], in_=ssb[half][:, 0:W_k], func=AF.Exp,
                    bias=mst[:, half:half + 1]), inc=sAt)
                ctx.wait("vector", sAt, sAt.v)
                ctx.run("vector", lambda e, half=half, W_k=W_k: e.tensor_reduce(
                    out=mst[:, 2 + half:3 + half], in_=esb[half][:, 0:W_k], axis=AX.X,
                    op=ALU.add), inc=sAt)
                ctx.wait("vector", sAt, sAt.v)
                ctx.run("vector", lambda e, half=half: e.reciprocal(
                    out=mst[:, 4 + half:5 + half], in_=mst[:, 2 + half:3 + half]), inc=sAt)
            ctx.wait("vector", sAt, sAt.v)
            ctx.run("vector", lambda e, h=h: e.tensor_mul(
                out=mst[:, 6:7], in0=mst[:, 5:6], in1=lamB[:, h:h + 1]), inc=sAt)
            ctx.wait("vector", sAt, sAt.v)
            ctx.run("vector", lambda e, W_k=W_k: e.tensor_scalar(
                out=ssb[0][:, 0:W_k], in0=esb[0][:, 0:W_k], scalar1=mst[:, 4:5],
                scalar2=None, op0=ALU.mult), inc=sAt)
            ctx.wait("vector", sAt, sAt.v)
            ctx.run("vector", lambda e, W_k=W_k: e.scalar_tensor_tensor(
                out=ssb[0][:, 0:W_k], in0=esb[1][:, 0:W_k], scalar=mst[:, 6:7],
                in1=ssb[0][:, 0:W_k], op0=ALU.mult, op1=ALU.add), inc=sAt)
            ctx.wait("scalar", sAt, sAt.v)
            ctx.run("scalar", lambda e, W_k=W_k: e.activation(
                out=esb[0][:, 0:W_k], in_=ssb[0][:, 0:W_k], func=AF.Copy), inc=sAt)
            attv = sAt.v
            for kt in range(qt + 1):
                sh.pt.group(
                    [lambda e, kt=kt, p=sh.pt.ap, idn=sh.ident: e.transpose(
                        p[:, 0:128], esb[0][:, kt * 128:(kt + 1) * 128], idn[:, :])],
                    [lambda e, p=sh.pt.ap, kt=kt: ecopy(
                        e, out=atb[:, kt, :], in_=p[:, 0:128])],
                    evac_eng="scalar", extra_waits=[(sAt, attv), (pk2.wr, pk2.wr.v)],
                    evac_extra_waits=[(sta, sta.v)])
            at_t = sh.pt.ev.v
            ctx.wait("tensor", pk2.ev, pk2.ev.v)
            ctx.wait("tensor", sh.pt.ev, at_t)
            ctx.wait("tensor", ldh, ldh_h)
            for kt in range(qt + 1):
                ctx.run("tensor", lambda e, kt=kt: e.matmul(
                    pk2.ap[:, 0:256], atb[:, kt, :], vh[:, kt, :],
                    start=(kt == 0), stop=(kt == qt)),
                    inc=pk2.wr if kt == qt else None)
            ctx.wait("scalar", pk2.wr, pk2.wr.v)
            ctx.wait("scalar", sAt, sAt.v)
            ctx.run("scalar", lambda e: e.activation(
                out=sq[:, 0:256], in_=pk2.ap[:, 0:256], func=AF.Square), inc=sAt)
            ctx.wait("vector", sAt, sAt.v)
            ctx.run("vector", lambda e: e.tensor_reduce(
                out=mst[:, 7:8], in_=sq[:, 0:256], axis=AX.X, op=ALU.add), inc=sAt)
            ctx.wait("scalar", sAt, sAt.v)
            ctx.on("scalar", lambda e: e.activation(
                out=mst[:, 7:8], in_=mst[:, 7:8], func=AF.Ln, scale=1.0 / 256.0,
                bias=sh.epscols[:, 0:1]))
            ctx.run("scalar", lambda e: e.activation(
                out=mst[:, 7:8], in_=mst[:, 7:8], func=AF.Exp, scale=-0.5), inc=sAt)
            rinv_t = sAt.v
            ctx.wait("vector", sAt, rinv_t)
            ctx.wait("vector", sh.pt.wr, sh.pt.wr.v)
            ctx.run("vector", lambda e: e.tensor_scalar(
                out=ob[:, :], in0=pk2.ap[:, 0:256], scalar1=mst[:, 7:8],
                scalar2=None, op0=ALU.mult), inc=pk2.ev)
            obv = pk2.ev.v
            for sub in range(2):
                sh.pt.group(
                    [lambda e, sub=sub, p=sh.pt.ap, idn=sh.ident: e.transpose(
                        p[:, 0:128], ob[:, sub * 128:(sub + 1) * 128], idn[:, :])],
                    [lambda e, p=sh.pt.ap, sub=sub: ecopy(
                        e, out=atb[:, sub, :], in_=p[:, 0:128])],
                    evac_eng="scalar", extra_waits=[(pk2.ev, obv)],
                    evac_extra_waits=[(sta, sta.v)])
            ctx.wait("gpsimd", sh.pt.ev, sh.pt.ev.v)
            ctx.dma("gpsimd", oT_d[h * 256:(h + 1) * 256, qt * 128:(qt + 1) * 128]
                    .rearrange("(a p) t -> p a t", p=128), atb[:, 0:2, :], sta)
    phaseB_sta = sta.v

    wo = es.enter_context(nc.sbuf_tensor(name + "wo", [128, 16, 1024], BF16))
    rmsT = es.enter_context(nc.sbuf_tensor(name + "rmsT", [128, 2], F32))
    wldc = [es.enter_context(nc.sbuf_tensor(name + f"wldc{i}", [128, 1024], F32)) for i in range(2)]
    oth = es.enter_context(nc.sbuf_tensor(name + "oth", [128, 16, CH], BF16))
    osb = es.enter_context(nc.sbuf_tensor(name + "osb", [128, NTT, 1024], F32))
    ctx.meta["vecT_tmp"] = lamt
    emit_load_vecT(ctx, sh, W["rms_scale"], 256, rmsT, ld)
    rms_t = sh.pb.ev.v
    for ft in range(16):
        b = ft % 2
        if ft >= 2:
            ctx.wait("sync", cons, cons.v - 1)
        ctx.dma("sync", wldc[b][:, :], W["W_o"][ft * 128:(ft + 1) * 128, :], ld)
        ctx.wait("vector", ld, ld.v)
        ctx.wait("vector", sh.pb.ev, rms_t)
        ctx.run("vector", lambda e, s=wldc[b], ft=ft: e.tensor_scalar(
            out=wo[:, ft, :], in0=s[:, :], scalar1=rmsT[:, ft % 2:ft % 2 + 1],
            scalar2=0.2, op0=ALU.mult, op1=ALU.mult), inc=cons)
    pre_cons2 = cons.v
    for c in range(NCH):
        b = c % 2
        ctx.wait("sync", stx, stx.v)
        ctx.wait("sync", sLN, sLN.v)
        if c == 0:
            ctx.wait("sync", sta, phaseB_sta)
        ctx.dma("sync", xs[b][:, :, :],
                x_in[c * CH:(c + 1) * CH, :].rearrange("(a p) d -> p a d", p=128), ldx)
        ctx.wait("sync", sh.pk[0].wr, sh.pk[0].wr.v)
        ctx.wait("sync", sh.pk[1].wr, sh.pk[1].wr.v)
        ctx.wait("sync", sh.pk[2].wr, sh.pk[2].wr.v)
        ctx.wait("sync", sh.pk[3].wr, sh.pk[3].wr.v)
        ctx.dma("sync", oth[:, :, :],
                oT_d[:, c * CH:(c + 1) * CH].rearrange("(a p) t -> p a t", p=128), ldx)
        lv = ldx.v
        for tt in range(NTT):
            for oh in range(2):
                pkb = sh.pk[tt * 2 + oh]
                pkb.group(
                    [lambda e, ft=ft, tt=tt, oh=oh, p=pkb.ap: e.matmul(
                        p[:, 0:512], oth[:, ft, tt * 128:(tt + 1) * 128],
                        wo[:, ft, oh * 512:(oh + 1) * 512], start=(ft == 0), stop=(ft == 15))
                     for ft in range(16)],
                    [("vector", lambda e, p=pkb.ap, tt=tt, oh=oh: e.tensor_copy(
                        out=osb[:, tt, oh * 512:(oh + 1) * 512], in_=p[:, 0:512]))],
                    extra_waits=[(ldx, lv), (cons, pre_cons2)],
                    evac_extra_waits=[(sLN, sLN.v)])
        for tt in range(NTT):
            w = [(sh.pk[tt * 2].ev, sh.pk[tt * 2].ev.v),
                 (sh.pk[tt * 2 + 1].ev, sh.pk[tt * 2 + 1].ev.v), (sRes, sRes.v), (ldx, lv)]
            emit_postln_resid(ctx, osb[:, tt, :], xs[b][:, tt, :], sq, stat, D, sRes,
                              waits=w, sv=svB, sln=sLN, sh=sh)
        rv = sRes.v
        ctx.wait("gpsimd", sRes, rv)
        ctx.dma("gpsimd", x_out[c * CH:(c + 1) * CH, :].rearrange("(a p) d -> p a d", p=128),
                xs[b][:, :, :], stx)
    ctx.wait("gpsimd", stx, stx.v)
    return stx


def emit_l1_attn(ctx, sh, es, x_in, x_out, W, name="s1"):
    nc = ctx.nc
    T, D, DI, NS, RK = 4096, 1024, 2048, 16, 64
    CL = 256
    NCH = T // CL
    NDT = DI // 128

    winT_d = nc.dram_tensor(name + "winT", [8, 128, 4096], BF16).ap()
    woutT_d = nc.dram_tensor(name + "woutT", [16, 128, 1024], BF16).ap()

    ld = ctx.sem(name + "ld")
    cons = ctx.sem(name + "cons")
    stp = ctx.sem(name + "stp")
    ctx.meta = getattr(ctx, "meta", {})
    wouth = es.enter_context(nc.sbuf_tensor(name + "wouth", [128, 16, 256], BF16))
    es2 = contextlib.ExitStack()
    wld = [es2.enter_context(nc.sbuf_tensor(name + f"wld{i}", [128, 2048], F32, side="right")) for i in range(2)]
    wbf = [es2.enter_context(nc.sbuf_tensor(name + f"wbf{i}", [128, 2048], BF16, side="right")) for i in range(2)]
    stage = es2.enter_context(nc.sbuf_tensor(name + "stage", [128, 8, 4096], BF16, side="right"))
    emit_loadT(ctx, sh, W["W_in"], 4096, 1024, stage, wld, wbf, ld, cons)
    ctx.wait("gpsimd", sh.pt.ev, sh.pt.ev.v)
    ctx.dma("gpsimd", winT_d.rearrange("a p d -> p a d"), stage[:, :, :], stp)
    ctx.wait("scalar", stp, stp.v)
    stage16 = stage[:, :, :].rearrange("p a (c d) -> p (a c) d", c=2)[:, :, 0:1024]
    emit_loadT(ctx, sh, W["W_out"], 1024, 2048, stage16, wld, wbf, ld, cons)
    ctx.wait("gpsimd", sh.pt.ev, sh.pt.ev.v)
    ctx.dma("gpsimd", woutT_d.rearrange("a p d -> p a d"), stage16, stp)
    ctx.wait("scalar", stp, stp.v)
    wxpT = es.enter_context(nc.sbuf_tensor(name + "wxpT", [128, 16, 96], F32))
    for dt in range(16):
        ctx.wait("sync", sh.pb.wr, sh.pb.wr.v)
        ctx.dma("sync", wld[0][0:96, 0:128], W["W_xproj"][:, dt * 128:(dt + 1) * 128], ld)
        lv = ld.v
        sh.pb.group(
            [lambda e, p=sh.pb.ap, idn=sh.identf32: e.transpose(
                p[:, 0:96], wld[0][0:96, 0:128], idn[0:96, 0:96])],
            [("vector", lambda e, p=sh.pb.ap, dt=dt: e.tensor_copy(
                out=wxpT[:, dt, :], in_=p[:, 0:96]))],
            extra_waits=[(ld, lv)])
    wdtT = es.enter_context(nc.sbuf_tensor(name + "wdtT", [64, 2048], F32))
    for dt in range(16):
        ctx.wait("sync", sh.pb.wr, sh.pb.wr.v)
        ctx.dma("sync", wld[0][:, 0:64], W["W_dt"][dt * 128:(dt + 1) * 128, :], ld)
        lv = ld.v
        sh.pb.group(
            [lambda e, p=sh.pb.ap, idn=sh.identf32: e.transpose(
                p[0:64, 0:128], wld[0][:, 0:64], idn[:, :])],
            [("vector", lambda e, p=sh.pb.ap, dt=dt: e.tensor_copy(
                out=wdtT[0:64, dt * 128:(dt + 1) * 128], in_=p[0:64, 0:128]))],
            extra_waits=[(ld, lv)])
    Abuf = es.enter_context(nc.sbuf_tensor(name + "A", [128, 16, 16], F32))
    for dt in range(16):
        ctx.wait("sync", cons, cons.v)
        ctx.wait("sync", sh.pb.wr, sh.pb.wr.v)
        ctx.dma("sync", wld[0][:, 0:16], W["A_log"][dt * 128:(dt + 1) * 128, :], ld)
        ctx.wait("scalar", ld, ld.v)
        ctx.run("scalar", lambda e, dt=dt: e.activation(
            out=Abuf[:, dt, :], in_=wld[0][:, 0:16], func=AF.Exp), inc=cons)
    ctx.wait("vector", cons, cons.v)
    ctx.run("vector", lambda e: e.tensor_scalar(
        out=Abuf[:, :, :], in0=Abuf[:, :, :], scalar1=-1.0, scalar2=None, op0=ALU.mult), inc=cons)
    vecT = es.enter_context(nc.sbuf_tensor(name + "vecT", [128, 64], F32))
    ctx.meta["vecT_tmp"] = wld[0]
    emit_load_vecT(ctx, sh, W["b_conv"], 2048, vecT[:, 0:16], ld)
    emit_load_vecT(ctx, sh, W["b_dt"], 2048, vecT[:, 16:32], ld)
    emit_load_vecT(ctx, sh, W["D_skip"], 2048, vecT[:, 32:48], ld)
    wcv = es.enter_context(nc.sbuf_tensor(name + "wcv", [128, 16, 4], F32))
    for dt in range(16):
        ctx.wait("sync", cons, cons.v)
        ctx.wait("sync", sh.pb.wr, sh.pb.wr.v)
        ctx.dma("sync", wld[0][:, 0:4], W["W_conv"][dt * 128:(dt + 1) * 128, :], ld)
        ctx.wait("vector", ld, ld.v)
        ctx.run("vector", lambda e, dt=dt: e.tensor_copy(
            out=wcv[:, dt, :], in_=wld[0][:, 0:4]), inc=cons)
    lnwT = es.enter_context(nc.sbuf_tensor(name + "lnwT", [128, 16], F32))
    emit_load_vecT(ctx, sh, W["ln_w"], 1024, lnwT[:, 0:8], ld)
    emit_load_vecT(ctx, sh, W["ln_b"], 1024, lnwT[:, 8:16], ld)
    pre_pb = sh.pb.ev.v
    pre_cons = cons.v
    pre_pt = sh.pt.ev.v
    es2.close()
    for _e in ENGS:
        ctx.wait(_e, cons, pre_cons)
        ctx.wait(_e, sh.pb.ev, pre_pb)
        ctx.wait(_e, sh.pt.ev, pre_pt)
        ctx.wait(_e, stp, stp.v)
        ctx.wait(_e, sh.pb.wr, sh.pb.wr.v)

    xs = es.enter_context(nc.sbuf_tensor(name + "xs", [128, 2, 1024], F32))
    tbuf = es.enter_context(nc.sbuf_tensor(name + "t", [128, 2, 1024], F32))
    xn = es.enter_context(nc.sbuf_tensor(name + "xn", [128, 2, 1024], BF16))
    xnt = es.enter_context(nc.sbuf_tensor(name + "xnt", [128, 8, CL], BF16))
    winp = es.enter_context(nc.sbuf_tensor(name + "winp", [128, 8, 128], BF16))
    bufA = es.enter_context(nc.sbuf_tensor(name + "bufA", [128, 16, 260], F32))
    zT = es.enter_context(nc.sbuf_tensor(name + "zT", [128, 16, CL], BF16))
    xcT = es.enter_context(nc.sbuf_tensor(name + "xcT", [128, 16, CL], F32))
    deltaT = es.enter_context(nc.sbuf_tensor(name + "deltaT", [128, 16, CL], F32))
    ubuf = es.enter_context(nc.sbuf_tensor(name + "u", [128, 16, CL], F32))
    ybuf = es.enter_context(nc.sbuf_tensor(name + "y", [128, 16, CL], F32))
    dAball = es.enter_context(nc.sbuf_tensor(name + "dAb", [128, 4096], F32))
    hball = es.enter_context(nc.sbuf_tensor(name + "hb", [128, 16, CL], F32))
    hcar = es.enter_context(nc.sbuf_tensor(name + "hcar", [128, 16, 16], F32, side="right"))
    dtlowT = es.enter_context(nc.sbuf_tensor(name + "dtlowT", [64, CL], F32, side="right"))
    BsS = es.enter_context(nc.sbuf_tensor(name + "BsS", [16, 2, CL], F32, side="right"))
    Bbc = es.enter_context(nc.sbuf_tensor(name + "Bbc", [128, 2, CL], F32, side="right"))
    zf = es.enter_context(nc.sbuf_tensor(name + "zf", [128, 3, CL], F32, side="right"))
    sq = es.enter_context(nc.sbuf_tensor(name + "sq", [128, 1024], F32, side="right"))
    stat = es.enter_context(nc.sbuf_tensor(name + "stat", [128, 8], F32, side="right"))
    dAv = dAball.rearrange("p (a b) -> p a b", a=16)
    osb = dAball.rearrange("p (a b) -> p a b", a=4)
    dBxall = bufA[:, :, 0:CL]

    ldx = ctx.sem(name + "ldx")
    ldw = ctx.sem(name + "ldw")
    stx = ctx.sem(name + "stx")
    sLN = ctx.sem(name + "sLN")
    sRes = ctx.sem(name + "sRes")
    svA = ctx.sem(name + "svA")
    svB = ctx.sem(name + "svB")
    sV = ctx.sem(name + "sV")
    sS = ctx.sem(name + "sS")
    sP = ctx.sem(name + "sP")

    ctx.wait("vector", cons, pre_cons)
    ctx.on("vector", lambda e: e.memset(bufA[:, :, 0:4], 0.0))
    ctx.run("vector", lambda e: e.memset(hcar[:, :, :], 0.0), inc=sV)

    for c in range(NCH):
        if c > 0:
            ctx.wait("vector", sS, sS.v)
            ctx.run("vector", lambda e: e.tensor_copy(
                out=bufA[:, :, 0:4], in_=bufA[:, :, 256:260]), inc=sV)
        halo_v = sV.v
        ctx.wait("sync", stx, stx.v)
        ctx.wait("sync", sLN, sLN.v)
        ctx.dma("sync", xs[:, :, :],
                x_in[c * CL:(c + 1) * CL, :].rearrange("(a p) d -> p a d", p=128), ldx)
        lv = ldx.v
        ptw = sh.pt.wr.v
        for tt in range(2):
            emit_ln_tile(ctx, xs[:, tt, :], tbuf[:, tt, :], sq, stat, D, sLN,
                         waits=[(ldx, lv), (sRes, sRes.v)], sv=svA, sh=sh)
            emit_ln_tile(ctx, tbuf[:, tt, :], xn[:, tt, :], sq, stat, D, sLN,
                         waits=[(sh.pt.wr, ptw)], sv=svA, sh=sh)
        lnv = sLN.v
        for tt in range(2):
            sh.pt.group(
                [lambda e, s=xn, tt=tt, dt=dt, p=sh.pt.ap, idn=sh.ident: e.transpose(
                    p[:, dt * 128:(dt + 1) * 128], s[:, tt, dt * 128:(dt + 1) * 128], idn[:, :])
                 for dt in range(8)],
                [lambda e, p=sh.pt.ap, d=xnt, tt=tt: ecopy(
                    e, out=d[:, :, tt * 128:(tt + 1) * 128],
                    in_=p[:, :].rearrange("p (a b) -> p a b", a=8))],
                evac_eng="scalar", extra_waits=[(sLN, lnv)])
        trv = sh.pt.ev.v
        ctx.wait("vector", sh.pt.ev, trv)
        for dt in range(8):
            ctx.run("vector", lambda e, dt=dt: e.tensor_scalar(
                out=xnt[:, dt, :], in0=xnt[:, dt, :], scalar1=lnwT[:, dt:dt + 1],
                scalar2=lnwT[:, 8 + dt:9 + dt], op0=ALU.mult, op1=ALU.add),
                inc=sV if dt == 7 else None)
        aff_v = sV.v
        for fp in range(32):
            ctx.wait("sync", sh.pk[4].wr, sh.pk[4].wr.v)
            ctx.wait("sync", sh.pk[5].wr, sh.pk[5].wr.v)
            ctx.dma("sync", winp[:, :, :],
                    winT_d[:, :, fp * 128:(fp + 1) * 128].rearrange("a p d -> p a d"), ldw)
            lwv = ldw.v
            for fs in range(1):
                fg = fp
                pkb = sh.pk[4 + fg % 2]
                if fg < 16:
                    evac = [("scalar", lambda e, p=pkb.ap, fg=fg: e.activation(
                        out=bufA[:, fg, 4:260], in_=p[:, 0:CL], func=AF.Copy))]
                else:
                    evac = [("scalar", lambda e, p=pkb.ap, fg=fg: e.activation(
                        out=zT[:, fg - 16, :], in_=p[:, 0:CL], func=AF.Copy))]
                pkb.group(
                    [lambda e, w=winp, x_=xnt, dt=dt, p=pkb.ap: e.matmul(
                        p[:, 0:CL], w[:, dt, :], x_[:, dt, :],
                        start=(dt == 0), stop=(dt == 7))
                     for dt in range(8)],
                    evac,
                    extra_waits=[(ldw, lwv), (sV, aff_v)],
                    evac_extra_waits=[(sV, sV.v), (sP, sP.v)])
        xz_v = [sh.pk[4].ev.v, sh.pk[5].ev.v]
        ctx.waits("vector", [(sh.pk[4].ev, xz_v[0]), (sh.pk[5].ev, xz_v[1]), (sV, halo_v),
                             (cons, pre_cons)])
        for k in range(4):
            for dt in range(16):
                if k == 0:
                    ctx.on("vector", lambda e, dt=dt, k=k: e.tensor_scalar(
                        out=xcT[:, dt, :], in0=bufA[:, dt, 1 + k:1 + k + CL],
                        scalar1=wcv[:, dt, k:k + 1], scalar2=None, op0=ALU.mult))
                else:
                    ctx.on("vector", lambda e, dt=dt, k=k: e.scalar_tensor_tensor(
                        out=xcT[:, dt, :], in0=bufA[:, dt, 1 + k:1 + k + CL],
                        scalar=wcv[:, dt, k:k + 1], in1=xcT[:, dt, :],
                        op0=ALU.mult, op1=ALU.add))
        ctx.run("vector", lambda e: e.engine_nop(), inc=sV)
        conv_v = sV.v
        ctx.wait("scalar", sV, conv_v)
        for dt in range(16):
            ctx.run("scalar", lambda e, dt=dt: e.activation(
                out=xcT[:, dt, :], in_=xcT[:, dt, :], func=AF.Silu,
                bias=vecT[:, dt:dt + 1]), inc=sS if dt == 15 else None)
        silu_v = sS.v
        ctx.wait("tensor", sS, silu_v)
        ctx.wait("tensor", cons, pre_cons)
        sh.pb.group(
            [lambda e, dt=dt, p=sh.pb.ap: e.matmul(
                p[0:64, 0:CL], wxpT[:, dt, 0:64], xcT[:, dt, :],
                start=(dt == 0), stop=(dt == 15)) for dt in range(16)],
            [("scalar", lambda e, p=sh.pb.ap: e.activation(
                out=dtlowT[0:64, :], in_=p[0:64, 0:CL], func=AF.Copy))],
            evac_extra_waits=[(sV, sV.v)])
        sh.pb.group(
            [lambda e, dt=dt, p=sh.pb.ap: e.matmul(
                p[0:16, 256:256 + CL], wxpT[:, dt, 64:80], xcT[:, dt, :],
                start=(dt == 0), stop=(dt == 15)) for dt in range(16)],
            [("scalar", lambda e, p=sh.pb.ap: e.activation(
                out=BsS[0:16, 0, :], in_=p[0:16, 256:256 + CL], func=AF.Copy))])
        sh.pb.group(
            [lambda e, dt=dt, p=sh.pb.ap: e.matmul(
                p[0:16, 0:CL], wxpT[:, dt, 80:96], xcT[:, dt, :],
                start=(dt == 0), stop=(dt == 15)) for dt in range(16)],
            [("scalar", lambda e, p=sh.pb.ap: e.activation(
                out=BsS[0:16, 1, :], in_=p[0:16, 0:CL], func=AF.Copy))])
        dbl_v = sh.pb.ev.v
        for dt in range(16):
            pkb = sh.pk[4 + dt % 2]
            def evacf(e, p=pkb.ap, dt=dt):
                return e.activation(out=zf[:, 0, :], in_=p[:, 0:CL], func=AF.Identity,
                                    bias=vecT[:, 16 + dt:17 + dt])
            pkb.group(
                [lambda e, dt=dt, p=pkb.ap: e.matmul(
                    p[:, 0:CL], wdtT[0:64, dt * 128:(dt + 1) * 128], dtlowT[0:64, :],
                    start=True, stop=True)],
                [("scalar", evacf)],
                extra_waits=[(sh.pb.ev, dbl_v)],
                evac_extra_waits=[(sV, sV.v)])
            zv = pkb.ev.v
            ctx.wait("scalar", pkb.ev, zv)
            ctx.on("scalar", lambda e: e.activation(out=zf[:, 1, :], in_=zf[:, 0, :], func=AF.Abs))
            ctx.on("scalar", lambda e: e.activation(out=zf[:, 1, :], in_=zf[:, 1, :], func=AF.Exp, scale=-1.0))
            ctx.on("scalar", lambda e: e.activation(out=zf[:, 1, :], in_=zf[:, 1, :], func=AF.Ln,
                                                    bias=sh.onescolf32[:, 0:1]))
            ctx.run("scalar", lambda e: e.activation(out=zf[:, 2, :], in_=zf[:, 0, :], func=AF.Relu), inc=sS)
            spv = sS.v
            ctx.wait("vector", sS, spv)
            ctx.run("vector", lambda e, dt=dt: e.tensor_add(
                out=deltaT[:, dt, :], in0=zf[:, 1, :], in1=zf[:, 2, :]), inc=sV)
        for dt in range(16):
            ctx.run("vector", lambda e, dt=dt: e.tensor_mul(
                out=ubuf[:, dt, :], in0=deltaT[:, dt, :], in1=xcT[:, dt, :]),
                inc=sV if dt == 15 else None)
        u_v = sV.v
        ctx.wait("vector", sV, u_v)
        ctx.run("vector", lambda e: e.memset(ybuf[:, :, :], 0.0), inc=sV)
        y0_v = sV.v
        for n in range(NS):
            sh.pb.group(
                [lambda e, n=n, p=sh.pb.ap: e.matmul(
                    p[:, 0:CL], sh.sel16[0:16, n * 128:(n + 1) * 128], BsS[0:16, 0, :],
                    start=True, stop=True)],
                [("scalar", lambda e, p=sh.pb.ap: e.activation(
                    out=Bbc[:, 0, :], in_=p[:, 0:CL], func=AF.Copy))],
                extra_waits=[(sh.pb.ev, dbl_v)],
                evac_extra_waits=[(sP, sP.v)])
            sh.pb.group(
                [lambda e, n=n, p=sh.pb.ap: e.matmul(
                    p[:, 256:256 + CL], sh.sel16[0:16, n * 128:(n + 1) * 128], BsS[0:16, 1, :],
                    start=True, stop=True)],
                [("scalar", lambda e, p=sh.pb.ap: e.activation(
                    out=Bbc[:, 1, :], in_=p[:, 256:256 + CL], func=AF.Copy))],
                evac_extra_waits=[(sP, sP.v)])
            bc_v = sh.pb.ev.v
            ctx.wait("scalar", sV, u_v)
            ctx.wait("scalar", sV, sV.v)
            ctx.wait("scalar", cons, pre_cons)
            for dt in range(16):
                ctx.run("scalar", lambda e, dt=dt, n=n: e.activation(
                    out=dAv[:, dt, :], in_=deltaT[:, dt, :], func=AF.Exp,
                    scale=Abuf[:, dt, n:n + 1]), inc=sS if dt == 15 else None)
            dA_v = sS.v
            ctx.wait("gpsimd", sh.pb.ev, bc_v)
            ctx.wait("gpsimd", sV, u_v)
            ctx.wait("gpsimd", sV, sV.v)
            ctx.wait("gpsimd", sS, sS.v)
            for dt in range(16):
                ctx.run("gpsimd", lambda e, dt=dt: e.tensor_tensor(
                    out=dBxall[:, dt, :], in0=ubuf[:, dt, :], in1=Bbc[:, 0, :],
                    op=ALU.mult), inc=sP if dt == 15 else None)
            dBx_v = sP.v
            ctx.wait("vector", sS, dA_v)
            ctx.wait("vector", sP, dBx_v)
            ctx.wait("vector", sV, y0_v)
            for dt in range(16):
                ctx.run("vector", lambda e, dt=dt, n=n: e.tensor_tensor_scan(
                    out=hball[:, dt, :], data0=dAv[:, dt, :], data1=dBxall[:, dt, :],
                    initial=hcar[:, dt, n:n + 1], op0=ALU.mult, op1=ALU.add),
                    inc=sV if dt == 15 else None)
            scan_v = sV.v
            for dt in range(16):
                ctx.run("vector", lambda e, dt=dt, n=n: e.tensor_copy(
                    out=hcar[:, dt, n:n + 1], in_=hball[:, dt, 255:256]),
                    inc=sV if dt == 15 else None)
            ctx.wait("gpsimd", sV, scan_v)
            for dt in range(16):
                ctx.run("gpsimd", lambda e, dt=dt: e.tensor_tensor(
                    out=dAv[:, dt, :], in0=hball[:, dt, :], in1=Bbc[:, 1, :],
                    op=ALU.mult), inc=sP if dt == 15 else None)
            ym_v = sP.v
            ctx.wait("vector", sP, ym_v)
            for dt in range(16):
                ctx.run("vector", lambda e, dt=dt: e.tensor_add(
                    out=ybuf[:, dt, :], in0=ybuf[:, dt, :], in1=dAv[:, dt, :]),
                    inc=sV if dt == 15 else None)
        yv = sV.v
        for dt in range(16):
            ctx.run("vector", lambda e, dt=dt: e.scalar_tensor_tensor(
                out=ybuf[:, dt, :], in0=xcT[:, dt, :], scalar=vecT[:, 32 + dt:33 + dt],
                in1=ybuf[:, dt, :], op0=ALU.mult, op1=ALU.add),
                inc=sV if dt == 15 else None)
        dsk_v = sV.v
        ctx.wait("scalar", sV, dsk_v)
        for dt in range(16):
            ctx.run("scalar", lambda e, dt=dt: e.activation(
                out=zT[:, dt, :], in_=zT[:, dt, :], func=AF.Silu), inc=sS if dt == 15 else None)
        sz_v = sS.v
        ctx.wait("vector", sS, sz_v)
        for dt in range(16):
            ctx.run("vector", lambda e, dt=dt: e.tensor_tensor(
                out=zT[:, dt, :], in0=ybuf[:, dt, :], in1=zT[:, dt, :],
                op=ALU.mult), inc=sV if dt == 15 else None)
        ybf_v = sV.v
        for oq in range(4):
            ctx.wait("sync", sh.pk[0].wr, sh.pk[0].wr.v)
            ctx.wait("sync", sh.pk[1].wr, sh.pk[1].wr.v)
            ctx.wait("sync", sh.pk[2].wr, sh.pk[2].wr.v)
            ctx.wait("sync", sh.pk[3].wr, sh.pk[3].wr.v)
            ctx.dma("sync", wouth[:, :, :],
                    woutT_d[:, :, oq * 256:(oq + 1) * 256].rearrange("a p d -> p a d"), ldw)
            lwo = ldw.v
            for tt in range(2):
                pkb = sh.pk[tt * 2 + oq % 2]
                pkb.group(
                    [lambda e, et=et, tt=tt, p=pkb.ap: e.matmul(
                        p[:, 0:256], zT[:, et, tt * 128:(tt + 1) * 128],
                        wouth[:, et, :], start=(et == 0), stop=(et == 15))
                     for et in range(16)],
                    [("vector", lambda e, p=pkb.ap, tt=tt, oq=oq: e.tensor_tensor(
                        out=osb[:, tt, oq * 256:(oq + 1) * 256], in0=p[:, 0:256],
                        in1=tbuf[:, tt, oq * 256:(oq + 1) * 256], op=ALU.add))],
                    extra_waits=[(sV, ybf_v), (cons, pre_cons), (ldw, lwo)],
                    evac_extra_waits=[(sS, sS.v), (sLN, sLN.v), (sP, sP.v)])
        for tt in range(2):
            w = [(sh.pk[tt * 2].ev, sh.pk[tt * 2].ev.v),
                 (sh.pk[tt * 2 + 1].ev, sh.pk[tt * 2 + 1].ev.v), (sRes, sRes.v)]
            emit_postln_resid(ctx, osb[:, tt, :], xs[:, tt, :], sq, stat, D, sRes,
                              waits=w, sv=svB, sln=sLN, sh=sh)
        rv = sRes.v
        ctx.wait("gpsimd", sRes, rv)
        ctx.dma("gpsimd", x_out[c * CL:(c + 1) * CL, :].rearrange("(a p) d -> p a d", p=128),
                xs[:, :, :], stx)
    ctx.wait("gpsimd", stx, stx.v)
    return stx


from concourse.bass_utils import run_bass_kernel_spmd

_SHAPES = {
    "L0_attn": [("proj_qk_w", (1024, 1024)), ("proj_qk_b", (1024,)),
                ("proj_v_w", (1024, 1024)), ("proj_v_b", (1024,)),
                ("in_proj_w", (3072, 1024)), ("in_proj_b", (3072,)),
                ("out_proj_w", (1024, 1024)), ("out_proj_b", (1024,))],
    "L0_ffn": [("W1", (4096, 1024)), ("b1", (4096,)), ("W2", (1024, 4096)), ("b2", (1024,))],
    "L1_attn": [("ln_w", (1024,)), ("ln_b", (1024,)), ("W_in", (4096, 1024)),
                ("W_conv", (2048, 4)), ("b_conv", (2048,)), ("W_xproj", (96, 2048)),
                ("W_dt", (2048, 64)), ("b_dt", (2048,)), ("A_log", (2048, 16)),
                ("D_skip", (2048,)), ("W_out", (1024, 2048))],
    "L1_ffn": [("gate_w", (64, 1024)), ("gate_bias", (64,)),
               ("eg_w", (64, 512, 1024)), ("eu_w", (64, 512, 1024)),
               ("ed_w", (64, 1024, 512)), ("sg_w", (1024, 1024)),
               ("su_w", (1024, 1024)), ("sd_w", (1024, 1024))],
    "L2_attn": [("w_qkv", (1024, 3072)), ("w_out", (1024, 1024)),
                ("b_out", (1024,)), ("logit_scale", (16, 1, 1))],
    "L2_ffn": [("W1", (4096, 1024)), ("b1", (4096,)), ("W2", (1024, 4096)), ("b2", (1024,))],
    "L3_attn": [("W_q", (1024, 2048)), ("W_k", (1024, 2048)), ("W_v", (1024, 2048)),
                ("W_o", (2048, 1024)), ("lambda_q1", (8, 128)), ("lambda_k1", (8, 128)),
                ("lambda_q2", (8, 128)), ("lambda_k2", (8, 128)), ("rms_scale", (256,))],
    "L3_ffn": [("W", (1024, 1024)), ("b", (1024,)), ("R", (8, 1024)),
               ("A", (8, 16, 1024)), ("B", (8, 1024, 16))],
}
_LAYERS = ["L0_attn", "L0_ffn", "L1_attn", "L1_ffn",
           "L2_attn", "L2_ffn", "L3_attn", "L3_ffn"]
_EMITTERS = {}


def _fill_emitters():
    _EMITTERS.update({
        "L0_attn": lambda ctx, sh, es, xi, xo, W: emit_l0_attn(ctx, sh, es, xi, xo, W, name="a0"),
        "L0_ffn": lambda ctx, sh, es, xi, xo, W: emit_ffn_quant(
            ctx, sh, es, xi, xo, W["W1"], W["b1"], W["W2"], W["b2"], name="f0"),
        "L1_attn": lambda ctx, sh, es, xi, xo, W: emit_l1_attn(ctx, sh, es, xi, xo, W, name="s1"),
        "L1_ffn": lambda ctx, sh, es, xi, xo, W: emit_l1_ffn(ctx, sh, es, xi, xo, W, name="m1"),
        "L2_attn": lambda ctx, sh, es, xi, xo, W: emit_l2_attn(ctx, sh, es, xi, xo, W, name="a2"),
        "L2_ffn": lambda ctx, sh, es, xi, xo, W: emit_ffn_quant(
            ctx, sh, es, xi, xo, W["W1"], W["b1"], W["W2"], W["b2"], name="f2"),
        "L3_attn": lambda ctx, sh, es, xi, xo, W: emit_l3_attn(ctx, sh, es, xi, xo, W, name="a3"),
        "L3_ffn": lambda ctx, sh, es, xi, xo, W: emit_l3_ffn(ctx, sh, es, xi, xo, W, name="f3"),
    })


def build_fused():
    _fill_emitters()
    nc = bass.Bass("TRN2", target_bir_lowering=False)
    caps = declare_consts(nc)
    x0 = nc.dram_tensor("L0_attn_x", [4096, 1024], F32, kind="ExternalInput").ap()
    xout = nc.dram_tensor("x_out", [4096, 1024], F32, kind="ExternalOutput").ap()
    xa = nc.dram_tensor("xping", [4096, 1024], F32).ap()
    xb = nc.dram_tensor("xpong", [4096, 1024], F32).ap()
    waps = {}
    for layer in _LAYERS:
        for k, s in _SHAPES[layer]:
            waps[(layer, k)] = nc.dram_tensor(layer + "_" + k, list(s), F32,
                                              kind="ExternalInput").ap()
    ctx = Ctx(nc)
    with contextlib.ExitStack() as es_top:
        sh = Shared(ctx, es_top, caps)
        cur = x0
        scratch = [xa, xb]
        for li, layer in enumerate(_LAYERS):
            nxt = xout if li == len(_LAYERS) - 1 else scratch[li % 2]
            W = {k: waps[(layer, k)] for k, _ in _SHAPES[layer]}
            with contextlib.ExitStack() as es:
                stx = _EMITTERS[layer](ctx, sh, es, cur, nxt, W)
                for _e in ENGS:
                    ctx.wait(_e, stx, stx.v)
                    for bk in sh.pk + [sh.pt, sh.pb]:
                        ctx.wait(_e, bk.ev, bk.ev.v)
            cur = nxt
        ctx.emit()
    return nc


_FUSED_NC = None


def kernel(**inputs):
    global _FUSED_NC
    if _FUSED_NC is None:
        _FUSED_NC = build_fused()
    nc = _FUSED_NC
    cons = make_consts()
    x = np.ascontiguousarray(np.asarray(inputs["L0_attn_x"], dtype=np.float32))
    base = dict(cons)
    for layer in _LAYERS:
        for k, _ in _SHAPES[layer]:
            nm = layer + "_" + k
            base[nm] = np.ascontiguousarray(np.asarray(inputs[nm], dtype=np.float32))
    in_maps = []
    for c in range(8):
        m = dict(base)
        m["L0_attn_x"] = np.ascontiguousarray(x[c])
        in_maps.append(m)
    res = run_bass_kernel_spmd(nc, in_maps, core_ids=list(range(8)))
    return np.stack([res.results[c]["x_out"] for c in range(8)]).astype(np.float32)
